# version 31
# baseline (speedup 1.0000x reference)
import os
import numpy as np
import ml_dtypes
from contextlib import ExitStack

import concourse.bass as bass
import concourse.mybir as mybir
from concourse.bass_utils import run_bass_kernel_spmd
from concourse.aot_env import allow_ici_all_to_all

os.environ.setdefault("TRN_ALLOW_ICI_ALL_TO_ALL", "1")

STRICT = False
RELAXED = ("tensor", "sync")

F32 = mybir.dt.float32
BF16 = mybir.dt.bfloat16
I8 = mybir.dt.int8
I32 = mybir.dt.int32
AF = mybir.ActivationFunctionType
ALU = mybir.AluOpType
AX = mybir.AxisListType
BF = ml_dtypes.bfloat16

S_IN, S_Z, S_U, S_DTLN = 0.05, 0.02, 0.01, 0.02
S_B, S_C, S_DT, S_Y = 0.02, 0.02, 0.01, 0.01
EPSB = 1e-6
EPSLN = 1e-5

ENGINES = ("tensor", "vector", "scalar", "gpsimd", "sync")


class Res:
    __slots__ = ("w", "readers", "name")

    def __init__(self, name=""):
        self.w = None
        self.readers = []
        self.name = name


class Op:
    __slots__ = ("eng", "fn", "deps", "queue")

    def __init__(self, eng, fn, deps, queue):
        self.eng = eng
        self.fn = fn
        self.deps = deps
        self.queue = queue


class Prog:
    def __init__(self, nc):
        self.nc = nc
        self.ops = []

    def res(self, name=""):
        return Res(name)

    def _expand(self, deps):
        out = set()
        stack = list(deps)
        while stack:
            d = stack.pop()
            if self.ops[d].fn is None:
                stack.extend(self.ops[d].deps)
            else:
                out.add(d)
        return out

    @staticmethod
    def _flat(rs):
        out = []
        stack = list(rs)
        while stack:
            r = stack.pop()
            if isinstance(r, Res):
                out.append(r)
            else:
                stack.extend(r)
        return out

    def emit(self, eng, fn, reads=(), writes=(), queue=None, tag=""):
        assert eng in ENGINES, eng
        reads = self._flat(reads)
        writes = self._flat(writes)
        deps = set()
        for r in reads:
            if r.w is not None:
                deps.add(r.w)
        for r in writes:
            if r.w is not None:
                deps.add(r.w)
            deps.update(r.readers)
        deps = self._expand(deps)
        oid = len(self.ops)
        self.ops.append(Op(eng, fn, deps, queue))
        for r in reads:
            r.readers.append(oid)
        for r in writes:
            r.w = oid
            r.readers = []
        return oid

    def build(self, ctx, block, dma_pool=10, cc_pool=3):
        nc = self.nc
        ops = self.ops
        n = len(ops)
        inc = [False] * n
        for op in ops:
            for d in op.deps:
                dop = ops[d]
                if (STRICT or dop.queue is not None or dop.eng != op.eng
                        or dop.eng not in RELAXED):
                    inc[d] = True
        sems = {}

        def get_sem(name):
            if name not in sems:
                sems[name] = ctx.enter_context(nc.semaphore(name))
            return sems[name]

        rr = {}
        total = {}
        assign = {}
        for i, op in enumerate(ops):
            if op.fn is None:
                continue
            if op.queue == "dma":
                inc[i] = True
                j = rr.get(op.eng, 0)
                rr[op.eng] = (j + 1) % dma_pool
                s = "d_%s%d" % (op.eng, j)
                prev = total.get(s, 0)
                total[s] = prev + 16
                assign[i] = (s, prev + 16, prev, 16)
            elif op.queue == "cc":
                inc[i] = True
                j = rr.get("cc", 0)
                rr["cc"] = (j + 1) % cc_pool
                s = "cc%d" % j
                prev = total.get(s, 0)
                total[s] = prev + 1
                assign[i] = (s, prev + 1, prev, 1)
            elif inc[i]:
                s = "c_%s" % op.eng
                prev = total.get(s, 0)
                total[s] = prev + 1
                assign[i] = (s, prev + 1, None, 1)

        lastw = {e: {} for e in ENGINES}
        plan = {e: [] for e in ENGINES}
        for i, op in enumerate(ops):
            waits = []

            def addw(s, v, eng=op.eng, waits=waits):
                if lastw[eng].get(s, 0) < v:
                    waits.append((s, v))
                    lastw[eng][s] = v

            for d in sorted(op.deps):
                dop = ops[d]
                if dop.fn is None:
                    continue
                if not (STRICT or dop.queue is not None or dop.eng != op.eng
                        or dop.eng not in RELAXED):
                    continue
                s, v, _, _ = assign[d]
                addw(s, v)
            if op.fn is not None and i in assign and assign[i][2] is not None:
                s, v, prev, _ = assign[i]
                if prev > 0:
                    addw(s, prev)
            myinc = assign.get(i) if (op.fn is not None and inc[i]) else None
            plan[op.eng].append((waits, op.fn, myinc))

        for i in sorted(assign):
            get_sem(assign[i][0])

        def make_body(eng):
            def body(e):
                for waits, fn, myinc in plan[eng]:
                    for s, v in waits:
                        e.wait_ge(sems[s], v)
                    if fn is None:
                        continue
                    inst = fn(e)
                    if myinc is not None:
                        s, v, prev, step = myinc
                        inst.then_inc(sems[s], step)
            return body

        for eng in ENGINES:
            if plan[eng]:
                getattr(block, eng)(make_body(eng))


class Arena:

    def __init__(self, prog, tensor, slot=512):
        self.t = tensor
        self.slot = slot
        self.W = tensor.shape[-1]
        nslots = (self.W + slot - 1) // slot
        self.res = [prog.res("%s[%d]" % (tensor.name, i)) for i in range(nslots)]

    def view(self, lo, hi):
        ap = self.t[:, lo:hi]
        rs = self.res[lo // self.slot:(hi + self.slot - 1) // self.slot]
        return ap, rs


def make_cfg(S=8192, DM=2048, DIN=4096, NC=8):
    c = dict(S=S, DM=DM, DIN=DIN, NC=NC)
    c["T"] = S // NC
    c["DC"] = DIN // NC
    c["KT"] = DM // 128
    c["DT"] = c["DC"] // 128
    c["NKT"] = DIN // 128
    c["NB"], c["N"], c["R"], c["K"] = 4, 16, 128, 4
    c["OSL"] = DM // NC
    c["CH"] = min(512, c["T"])
    c["LH"] = c["CH"]
    c["TBW"] = c["CH"]
    assert c["T"] % c["CH"] == 0 and S % c["CH"] == 0
    assert c["OSL"] % 128 == 0 and c["DT"] >= 1
    return c


def build(cfg):
    S, DM, DIN, NC = cfg["S"], cfg["DM"], cfg["DIN"], cfg["NC"]
    T, DC, KT, DT, NKT = cfg["T"], cfg["DC"], cfg["KT"], cfg["DT"], cfg["NKT"]
    NB, N, R, K = cfg["NB"], cfg["N"], cfg["R"], cfg["K"]
    OSL, CH, LH, TBW = cfg["OSL"], cfg["CH"], cfg["LH"], cfg["TBW"]
    RN2 = R + 2 * N
    NTB = S // TBW
    NCH = S // CH

    nc = bass.Bass("TRN2", target_bir_lowering=False, debug=False,
               detect_race_conditions=STRICT)

    def inp(name, shape, dt=F32):
        return nc.dram_tensor(name, shape, dt, kind="ExternalInput").ap()

    xsl = inp("xsl", [DM, T])
    wqT = inp("wqT", [DM, DM])
    woT = inp("woT", [DM, DM])
    memi = inp("memi", [NB, KT, 128, 128])
    mnl = inp("mnl", [128, KT, NB])
    bankw = inp("bankw", [KT, NB])
    pwT = inp("pwT", [DM, NB, OSL])
    nwT = inp("nwT", [DM, NB, OSL])
    pbr = inp("pbr", [128, KT, NB])
    nbr = inp("nbr", [128, KT, NB])
    fexps = inp("fexps", [1, NB])
    fbexps = inp("fbexps", [1, NB])
    fmask = inp("fmask", [1, NB])
    fscale = inp("fscale", [1, 1])
    fbsc = inp("fbsc", [1, 1])
    ipT = inp("ipT", [DM, 2 * DC])
    cwi = inp("cwi", [128, DT, K])
    cbi = inp("cbi", [128, DT])
    xpT = inp("xpT", [128, DT, RN2])
    dtln = inp("dtln", [128, 1])
    blnv = inp("blnv", [N, 1])
    clnv = inp("clnv", [N, 1])
    dpT = inp("dpT", [128, DC])
    dtbv = inp("dtbv", [128, DT])
    alog = inp("alog", [128, DT, N])
    dskip = inp("dskip", [128, DT])
    opT = inp("opT", [DC, DM])
    identi = inp("identi", [128, 128], BF16)
    seli = inp("seli", [4, 4 * 128], BF16)

    yout = nc.dram_tensor("yout", [DM, T], F32, kind="ExternalOutput").ap()

    d_xqin = nc.dram_tensor("d_xqin", [DM, T], I8)
    d_xqout = nc.dram_tensor("d_xqout", [NC * DM, T], I8)
    d_ffin = nc.dram_tensor("d_ffin", [DM, OSL], BF16)
    d_ffout = nc.dram_tensor("d_ffout", [NC * DM, OSL], BF16)
    d_ssin = [nc.dram_tensor("d_ssin%d" % i, [RN2, S // 2], F32) for i in range(2)]
    d_ssout = [nc.dram_tensor("d_ssout%d" % i, [RN2, S // 2], F32) for i in range(2)]
    d_yqin = nc.dram_tensor("d_yqin", [DIN, T], I8)
    d_yqout = nc.dram_tensor("d_yqout", [DIN, T], I8)
    d_opin = nc.dram_tensor("d_opin", [DC, DM], BF16)
    d_opout = nc.dram_tensor("d_opout", [DIN, DM], BF16)
    d_mxin = nc.dram_tensor("d_mxin", [8, 1], F32)
    d_mxout = nc.dram_tensor("d_mxout", [8, 1], F32)
    d_mxv = nc.dram_tensor("d_mxv", [128 * 8], F32)
    d_row8 = nc.dram_tensor("d_row8", [8], F32)
    d_gate = nc.dram_tensor("d_gate", [DC, S], I8)
    d_bc = nc.dram_tensor("d_bc", [2 * N, S], BF16)

    p = Prog(nc)
    E = p.emit
    RZ = p.res

    with ExitStack() as ctx:
        en = ctx.enter_context

        def sbt(name, shape, dt=F32):
            return en(nc.sbuf_tensor(name, shape, dt))

        x_sb = sbt("x_sb", [128, KT, T])
        fa_t = sbt("fa", [128, 13312])
        ba_t = sbt("ba", [128, 24576], BF16)
        ia_t = sbt("ia", [128, DT * S], I8)
        ti_b = [sbt("ti%d" % i, [128, CH], I32) for i in range(2)]
        yq_st = [sbt("yqst%d" % i, [128, CH], I8) for i in range(2)]
        xq_st = sbt("xq_st", [128, LH], I8)
        halo = sbt("halo", [128, DT, K])
        st_sb = sbt("st_sb", [128, DT * N])
        onerow_b = sbt("onerow_b", [1, 128], BF16)
        onerow_f = sbt("onerow_f", [1, 128])
        onecol_f = sbt("onecol_f", [128, 1])
        onecol_b = sbt("onecol_b", [128, 1], BF16)
        ident_b = sbt("ident_b", [128, 128], BF16)
        mxcol = sbt("mxcol", [128, 8])
        mxrow = sbt("mxrow", [8, 128])
        grow = sbt("grow", [1, 16])
        sc_sb = sbt("sc_sb", [128, 16])
        mn_b = sbt("mn_b", [128, KT, NB], BF16)
        bw_sb = sbt("bw_sb", [KT, NB + 4])
        weT_sb = sbt("weT_sb", [NB, KT])
        act_sb = sbt("act_sb", [NB, 1])
        cw_eff = sbt("cw_eff", [128, DT, K])
        cb_sb = sbt("cb_sb", [128, DT])
        xpq_b = sbt("xpq_b", [128, DT, RN2], BF16)
        dpq_b = sbt("dpq_b", [128, DC], BF16)
        dtb_sb = sbt("dtb_sb", [128, DT])
        a_sb = sbt("a_sb", [128, DT, N])
        dsk_sb = sbt("dsk_sb", [128, DT])
        wdt_sb = sbt("wdt_sb", [128, 1])
        wbc_sb = sbt("wbc_sb", [N, 2])
        cbn_sb = sbt("cbn_sb", [128, NB])
        bxc_sb = sbt("bxc_sb", [128, NB])
        bsc_sb = sbt("bsc_sb", [128, 1])
        bias_sb = sbt("bias_sb", [128, KT])
        fbq_sb = sbt("fbq_sb", [128, KT], I32)
        fb2_sb = sbt("fb2_sb", [128, KT + 8])

        ps5 = [en(nc.psum_tensor("ps5%d" % i, [128, CH])) for i in range(4)]
        psA = [en(nc.psum_tensor("psA%d" % i, [128, 2 * CH])) for i in range(2)]

        block = en(nc.Block())

        FA = Arena(p, fa_t)
        BA = Arena(p, ba_t)
        IA = Arena(p, ia_t)

        v_z2, r_z2 = FA.view(0, 1024)
        v_tq, r_tq = FA.view(1024, 2048)
        assert len(r_z2) == 2 and len(r_tq) == 2
        v_ev, r_ev = zip(*[FA.view(2048 + i * CH, 2048 + (i + 1) * CH)
                           for i in range(4)])
        v_sil, r_sil = zip(*[FA.view(4096 + i * CH, 4096 + (i + 1) * CH)
                             for i in range(2)])
        v_dtd, r_dtd = FA.view(5120, 5120 + DT * CH)
        v_wd, r_wd = FA.view(7168, 7168 + DT * CH)
        _eti = [fa_t[:, 10240 + i * 512:10240 + (i + 1) * 512].bitcast(I32)
                for i in range(2)]
        _, _reti0 = FA.view(10240, 10752)
        _, _reti1 = FA.view(10752, 11264)
        _reti = [_reti0, _reti1]
        _dab = fa_t[:, 9216:10240].bitcast(BF16)
        v_dA = [_dab[:, i * 512:i * 512 + CH] for i in range(4)]
        _, _rdA_a = FA.view(9216, 9728)
        _, _rdA_b = FA.view(9728, 10240)
        r_dA = [_rdA_a, _rdA_a, _rdA_b, _rdA_b]
        v_rows2, r_rows2 = FA.view(10240, 12288)
        ti_u = fa_t[:, 12288:12288 + CH].bitcast(I32)
        _, r_tiu = FA.view(12288, 12288 + CH)
        rows_sb = fa_t[0:1, 10240:12288]
        v_pw, r_pw = FA.view(5120, 5120 + NB * OSL)
        v_nw, r_nw = FA.view(5120 + NB * OSL, 5120 + 2 * NB * OSL)
        v_soft, r_soft = FA.view(7168, 7168 + NB * OSL)
        v_facc, r_facc = FA.view(8192, 8192 + OSL)
        v_wld, r_wld = FA.view(8448, 8448 + 1792)
        CW = min(2048, S)
        cacc_pair = [FA.view(5120, 5120 + CW), FA.view(7168, 7168 + CW)]

        dtdv = v_dtd.rearrange("p (d c) -> p d c", d=DT)
        wdv = v_wd.rearrange("p (d c) -> p d c", d=DT)

        v_xbh, r_xbh = BA.view(0, KT * LH)
        xbhv = v_xbh.rearrange("p (k t) -> p k t", k=KT)
        v_combh, r_combh = BA.view(KT * LH, 2 * KT * LH)
        combhv = v_combh.rearrange("p (k t) -> p k t", k=KT)
        v_wgl, r_wgl = zip(*[BA.view(16384 + i * 2048, 16384 + (i + 1) * 2048)
                             for i in range(2)])
        wglv = [v[:, 0:KT * 128].rearrange("p (k o) -> p k o", k=KT) for v in v_wgl]
        v_memn, r_memn = zip(*[BA.view(20480 + i * NB * 128,
                                       20480 + (i + 1) * NB * 128)
                               for i in range(2)])
        memnv = [v.rearrange("p (n e) -> p n e", n=NB) for v in v_memn]
        v_sig, r_sig = BA.view(21504, 21504 + LH)
        v_rr, r_rr = BA.view(22016, 22016 + LH)
        v_qst, r_qst = BA.view(16384, 16384 + 2048)
        car_pair = [BA.view(22528, 22528 + CH), BA.view(20480, 20480 + CH)]
        v_lnx, r_lnx = BA.view(23040, 23040 + CH)
        v_lnsq, r_lnsq = BA.view(23552, 23552 + CH)
        rbS_pair = [BA.view(24064, 24064 + CH), BA.view(20992, 20992 + CH)]
        sel_sb = sbt("sel_sb", [4, 4 * 128], BF16)
        v_fst, r_fst = BA.view(18432, 18432 + OSL)
        v_dtq, r_dtq = BA.view(0, S)
        v_wqip, r_wqip = BA.view(0, KT * 1024)
        wqipv = v_wqip.rearrange("p (k o) -> p k o", k=KT)
        TB2 = TBW // 2
        rhs_pair = [BA.view(16384, 16384 + KT * TB2),
                    BA.view(16384 + KT * TB2, 16384 + 2 * KT * TB2)]
        rhsv2 = [v.rearrange("p (k t) -> p k t", k=KT) for v, _ in rhs_pair]
        r_rhs2 = [r for _, r in rhs_pair]
        bcB_pair = [BA.view(8192 + i * 8 * CH, 8192 + i * 8 * CH + 4 * CH)
                    for i in range(2)]
        bcC_pair = [BA.view(8192 + i * 8 * CH + 4 * CH, 8192 + (i + 1) * 8 * CH)
                    for i in range(2)]
        bcBf = [v for v, _ in bcB_pair]
        bcCf = [v for v, _ in bcC_pair]
        r_bcB = [r for _, r in bcB_pair]
        r_bcC = [r for _, r in bcC_pair]
        v_wBset, r_wBset = BA.view(16384, 16384 + 4 * CH)
        v_wB = [v_wBset[:, i * CH:(i + 1) * CH] for i in range(4)]
        r_wB = [r_wBset[min(i, len(r_wBset) - 1):min(i, len(r_wBset) - 1) + 1]
               for i in range(4)]
        v_hset, r_hset = BA.view(20480, 20480 + 4 * CH)
        v_h = [v_hset[:, i * CH:(i + 1) * CH] for i in range(4)]
        r_h = [r_hset[min(i, len(r_hset) - 1):min(i, len(r_hset) - 1) + 1]
               for i in range(4)]
        v_chset, r_chset = BA.view(22528, 22528 + 4 * CH)
        v_ch = [v_chset[:, i * CH:(i + 1) * CH] for i in range(4)]
        r_ch = [r_chset[min(i, len(r_chset) - 1):min(i, len(r_chset) - 1) + 1]
               for i in range(4)]
        v_gl, r_gl = zip(*[BA.view(16384 + (6 + i) * CH, 16384 + (7 + i) * CH)
                           for i in range(2)])
        v_yr, r_yr = BA.view(0, NKT * CH)
        yrv = v_yr.rearrange("p (k t) -> p k t", k=NKT)
        v_opl, r_opl = zip(*[BA.view(16384 + i * NKT * 128,
                                     16384 + (i + 1) * NKT * 128)
                             for i in range(2)])
        oplv = [v.rearrange("p (k o) -> p k o", k=NKT) for v in v_opl]
        v_ubx, r_ubx = BA.view(8192, 8192 + DT * TBW)
        ubxv = v_ubx.rearrange("p (d t) -> p d t", d=DT)

        v_hs, r_hs = IA.view(0, DT * S)
        hsv = v_hs.rearrange("p (d t) -> p d t", d=DT)
        uv = hsv
        r_u = r_hs

        r_x = [RZ("x%d" % j) for j in range(KT)]
        r_ti = [RZ("ti%d" % i) for i in range(2)]
        r_yqst = [RZ("yqst%d" % i) for i in range(2)]
        r_xqst = RZ("xqst")
        r_halo = RZ("halo")
        r_st = [RZ("st%d" % i) for i in range(8)]
        r_const = RZ("const")
        r_mxcol, r_mxrow, r_grow, r_sc = RZ("mxc"), RZ("mxr"), RZ("grow"), RZ("sc")
        r_mn, r_bw, r_weT, r_act = RZ("mn"), RZ("bw"), RZ("weT"), RZ("act")
        r_cw, r_cb, r_xpq, r_dpq = RZ("cw"), RZ("cb"), RZ("xpq"), RZ("dpq")
        r_dtb, r_a, r_dsk, r_wdtc, r_wbcc = RZ("dtb"), RZ("a"), RZ("dsk"), RZ("wdt"), RZ("wbc")
        r_cbn, r_bxc, r_bsc, r_bias, r_fbq, r_fb2 = (RZ("cbn"), RZ("bxc"),
                                                     RZ("bsc"), RZ("bias"),
                                                     RZ("fbq"), RZ("fb2"))
        r_rows = r_rows2
        r_ps5 = [RZ("ps5%d" % i) for i in range(4)]
        r_psA = [RZ("psA%d" % i) for i in range(2)]
        r_dxqin, r_dxqout = RZ("dxqin"), RZ("dxqout")
        r_dffin, r_dffout = RZ("dffin"), RZ("dffout")
        r_dssin = [RZ("dssin%d" % i) for i in range(2)]
        r_dssout = [RZ("dssout%d" % i) for i in range(2)]
        r_dyqin, r_dyqout = RZ("dyqin"), RZ("dyqout")
        r_dopin, r_dopout = RZ("dopin"), RZ("dopout")
        r_dmxin, r_dmxout, r_dmxv, r_drow8 = RZ("dmxi"), RZ("dmxo"), RZ("dmxv"), RZ("dr8")
        r_dgate, r_dbc = RZ("dgate"), RZ("dbc")
        r_sel = RZ("sel")
        r_out = RZ("out")

        LRS = list(r_ev) + list(r_sil) + [r_z2, r_tq]

        E("vector", lambda e: e.memset(onerow_b[:, :], 1.0), writes=[r_const])
        E("vector", lambda e: e.memset(onerow_f[:, :], 1.0), writes=[r_const])
        E("vector", lambda e: e.memset(onecol_f[:, :], 1.0), writes=[r_const])
        E("vector", lambda e: e.memset(onecol_b[:, :], 1.0), writes=[r_const])
        E("sync", lambda e: e.dma_start(out=ident_b[:, :], in_=identi),
          writes=[r_const], queue="dma")
        E("sync", lambda e: e.dma_start(out=sel_sb[:, :], in_=seli),
          writes=[r_sel], queue="dma")
        E("sync", lambda e: e.dma_start(out=bw_sb[:, 0:NB], in_=bankw),
          writes=[r_bw], queue="dma")
        E("gpsimd", lambda e: e.dma_start(out=mn_b[:, :, :], in_=mnl),
          writes=[r_mn], queue="dma")
        E("sync", lambda e: e.dma_start(out=cw_eff[:, :, :], in_=cwi),
          writes=[r_cw], queue="dma")
        E("sync", lambda e: e.dma_start(out=cb_sb[:, :], in_=cbi),
          writes=[r_cb], queue="dma")
        E("sync", lambda e: e.dma_start(out=wdt_sb[:, :], in_=dtln),
          writes=[r_wdtc], queue="dma")
        E("sync", lambda e: e.dma_start(out=wbc_sb[:, 0:1], in_=blnv),
          writes=[r_wbcc], queue="dma")
        E("sync", lambda e: e.dma_start(out=wbc_sb[:, 1:2], in_=clnv),
          writes=[r_wbcc], queue="dma")
        E("sync", lambda e: e.dma_start(out=dtb_sb[:, :], in_=dtbv),
          writes=[r_dtb], queue="dma")
        E("sync", lambda e: e.dma_start(out=a_sb[:, :, :], in_=alog),
          writes=[r_a], queue="dma")
        E("sync", lambda e: e.dma_start(out=dsk_sb[:, :], in_=dskip),
          writes=[r_dsk], queue="dma")
        E("scalar", lambda e: e.activation(a_sb[:, :, :], a_sb[:, :, :], AF.Exp),
          reads=[r_a], writes=[r_a])
        E("vector", lambda e: e.tensor_scalar_mul(a_sb[:, :, :], a_sb[:, :, :], -1.0),
          reads=[r_a], writes=[r_a])
        E("vector", lambda e: e.tensor_scalar_mul(wdt_sb[:, :], wdt_sb[:, :],
                                                  1.0 / S_DTLN),
          reads=[r_wdtc], writes=[r_wdtc])
        E("vector", lambda e: e.tensor_scalar_mul(wbc_sb[:, 0:1], wbc_sb[:, 0:1],
                                                  1.0 / S_B),
          reads=[r_wbcc], writes=[r_wbcc])
        E("vector", lambda e: e.tensor_scalar_mul(wbc_sb[:, 1:2], wbc_sb[:, 1:2],
                                                  1.0 / S_C),
          reads=[r_wbcc], writes=[r_wbcc])
        for j in range(KT):
            E("sync", lambda e, j=j: e.dma_start(out=x_sb[:, j, :],
                                                 in_=xsl[j * 128:(j + 1) * 128, :]),
              writes=[r_x[j]], queue="dma")

        WLDW = 1792
        E("vector", lambda e: e.memset(mxcol[:, :], 0.0), writes=[r_mxcol])

        def absmax_into(col, src_res, src_ap, first):
            if first:
                E("vector", lambda e: e.reduce_max(mxcol[:, col:col + 1], src_ap,
                                                   axis=AX.X,
                                                   apply_absolute_value=True),
                  reads=src_res, writes=[r_mxcol])
            else:
                E("vector", lambda e: e.reduce_max(mxcol[:, 7:8], src_ap, axis=AX.X,
                                                   apply_absolute_value=True),
                  reads=src_res, writes=[r_mxcol])
                E("vector", lambda e: e.tensor_max(mxcol[:, col:col + 1],
                                                   mxcol[:, col:col + 1],
                                                   mxcol[:, 7:8]),
                  reads=[r_mxcol], writes=[r_mxcol])

        first = True
        for i in range(KT):
            for c0 in range(0, 2 * DC, WLDW):
                c1 = min(c0 + WLDW, 2 * DC)
                E("sync", lambda e, i=i, c0=c0, c1=c1: e.dma_start(
                    out=v_wld[:, 0:c1 - c0], in_=ipT[i * 128:(i + 1) * 128, c0:c1]),
                  writes=r_wld, queue="dma")
                absmax_into(0, r_wld, v_wld[:, 0:c1 - c0], first)
                first = False
        absmax_into(1, [r_cw], cw_eff.ap().rearrange("p a b -> p (a b)"), True)
        E("sync", lambda e: e.dma_start(out=v_wld[:, 0:DT * RN2], in_=xpT),
          writes=r_wld, queue="dma")
        absmax_into(2, r_wld, v_wld[:, 0:DT * RN2], True)
        E("vector", lambda e: e.tensor_copy(v_rows2[:, 0:DT * RN2],
                                            v_wld[:, 0:DT * RN2]),
          reads=r_wld, writes=r_rows2)
        E("sync", lambda e: e.dma_start(out=v_wld[:, 0:DC], in_=dpT),
          writes=r_wld, queue="dma")
        absmax_into(3, r_wld, v_wld[:, 0:DC], True)
        E("vector", lambda e: e.tensor_copy(v_z2[:, 0:DC], v_wld[:, 0:DC]),
          reads=r_wld, writes=r_z2)
        first = True
        for i in range(DT):
            for c0 in range(0, DM, WLDW):
                c1 = min(c0 + WLDW, DM)
                E("sync", lambda e, i=i, c0=c0, c1=c1: e.dma_start(
                    out=v_wld[:, 0:c1 - c0], in_=opT[i * 128:(i + 1) * 128, c0:c1]),
                  writes=r_wld, queue="dma")
                absmax_into(4, r_wld, v_wld[:, 0:c1 - c0], first)
                first = False
        E("sync", lambda e: e.dma_start(out=d_mxv[:], in_=mxcol[:, 0:8]),
          reads=[r_mxcol], writes=[r_dmxv], queue="dma")
        def mxv_rd(e):
            with nc.allow_non_contiguous_dma(reason="tiny 4KB transpose read"):
                return e.dma_start(out=mxrow[:, :],
                                   in_=d_mxv.ap().rearrange("(p c) -> c p", c=8))
        E("sync", mxv_rd, reads=[r_dmxv], writes=[r_mxrow], queue="dma")
        E("vector", lambda e: e.reduce_max(fb2_sb[0:8, KT:KT + 1], mxrow[:, :],
                                           axis=AX.X),
          reads=[r_mxrow], writes=[r_fb2])
        E("sync", lambda e: e.dma_start(out=d_mxin[:, :], in_=fb2_sb[0:8, KT:KT + 1]),
          reads=[r_fb2], writes=[r_dmxin], queue="dma")
        E("gpsimd", lambda e: e.collective_compute(
            "AllReduce", ALU.max, replica_groups=[list(range(NC))],
            ins=[d_mxin.ap().opt()], outs=[d_mxout.ap().opt()]),
          reads=[r_dmxin], writes=[r_dmxout], queue="cc")
        E("sync", lambda e: e.dma_start(out=d_row8[:], in_=d_mxout[:, :]),
          reads=[r_dmxout], writes=[r_drow8], queue="dma")
        E("sync", lambda e: e.dma_start(
            out=grow[:, 0:8], in_=d_row8.ap().rearrange("(a b) -> a b", a=1)),
          reads=[r_drow8], writes=[r_grow], queue="dma")
        E("vector", lambda e: e.reciprocal(grow[:, 8:13], grow[:, 0:5]),
          reads=[r_grow], writes=[r_grow])
        E("vector", lambda e: e.tensor_scalar_mul(grow[:, 8:13], grow[:, 8:13],
                                                  127.0),
          reads=[r_grow], writes=[r_grow])
        E("vector", lambda e: e.tensor_scalar_mul(grow[:, 13:14], grow[:, 0:1],
                                                  S_IN / (127.0 * S_Z)),
          reads=[r_grow], writes=[r_grow])
        E("vector", lambda e: e.tensor_scalar_mul(grow[:, 14:15], grow[:, 2:3],
                                                  S_U / 127.0),
          reads=[r_grow], writes=[r_grow])
        E("vector", lambda e: e.tensor_scalar_mul(grow[:, 15:16], grow[:, 3:4],
                                                  S_DTLN / (127.0 * S_DT)),
          reads=[r_grow], writes=[r_grow])
        E("vector", lambda e: e.tensor_scalar_mul(grow[:, 7:8], grow[:, 4:5],
                                                  S_Y / 127.0),
          reads=[r_grow], writes=[r_grow])
        E("vector", lambda e: e.tensor_scalar_mul(grow[:, 6:7], grow[:, 1:2],
                                                  S_Z / 127.0),
          reads=[r_grow], writes=[r_grow])
        E("tensor", lambda e: e.matmul(ps5[0][:, 0:16], onerow_f[:1, :],
                                       grow[:1, :], start=True, stop=True),
          reads=[r_const, r_grow], writes=[r_ps5[0]])
        E("vector", lambda e: e.tensor_copy(sc_sb[:, :], ps5[0][:, 0:16]),
          reads=[r_ps5[0]], writes=[r_sc])

        def qround(src_res, src_ap, dst_res, dst_ap, qcol, Wd, tag=""):
            for c0 in range(0, Wd, CH):
                c1 = min(c0 + CH, Wd)
                E("vector", lambda e, c0=c0, c1=c1: e.scalar_tensor_tensor(
                    v_ev[3][:, 0:c1 - c0], src_ap[:, c0:c1],
                    sc_sb[:, qcol:qcol + 1], src_ap[:, c0:c1],
                    ALU.mult, ALU.bypass),
                  reads=list(src_res) + [r_sc], writes=[r_ev[3]], tag="q1" + tag)
                E("vector", lambda e, c0=c0, c1=c1: e.tensor_copy(
                    ti_b[1][:, 0:c1 - c0], v_ev[3][:, 0:c1 - c0]),
                  reads=[r_ev[3]], writes=[r_ti[1]], tag="q2" + tag)
                E("vector", lambda e, c0=c0, c1=c1: e.tensor_scalar(
                    dst_ap[:, c0:c1], ti_b[1][:, 0:c1 - c0], -128.0, 127.0,
                    ALU.max, ALU.min),
                  reads=[r_ti[1]], writes=list(dst_res), tag="q3" + tag)

        qround(r_rows2, v_rows2, [r_xpq],
               xpq_b.ap().rearrange("p a b -> p (a b)"), 10, DT * RN2, "xp")
        qround(r_z2, v_z2, [r_dpq], dpq_b, 11, DC, "dp")
        E("vector", lambda e: e.scalar_tensor_tensor(
            v_ev[3][:, 0:DT * K], cw_eff.ap().rearrange("p a b -> p (a b)"),
            sc_sb[:, 9:10], cw_eff.ap().rearrange("p a b -> p (a b)"),
            ALU.mult, ALU.bypass),
          reads=[r_cw, r_sc], writes=[r_ev[3]])
        E("vector", lambda e: e.tensor_copy(ti_b[1][:, 0:DT * K],
                                            v_ev[3][:, 0:DT * K]),
          reads=[r_ev[3]], writes=[r_ti[1]])
        E("vector", lambda e: e.tensor_scalar(v_ev[3][:, 0:DT * K],
                                              ti_b[1][:, 0:DT * K],
                                              -128.0, 127.0, ALU.max, ALU.min),
          reads=[r_ti[1]], writes=[r_ev[3]])
        E("vector", lambda e: e.scalar_tensor_tensor(
            cw_eff.ap().rearrange("p a b -> p (a b)"), v_ev[3][:, 0:DT * K],
            sc_sb[:, 6:7], v_ev[3][:, 0:DT * K], ALU.mult, ALU.bypass),
          reads=[r_ev[3], r_sc], writes=[r_cw])
        for i in range(DT):
            for c0 in range(0, DM, WLDW):
                c1 = min(c0 + WLDW, DM)
                E("sync", lambda e, i=i, c0=c0, c1=c1: e.dma_start(
                    out=v_wld[:, 0:c1 - c0], in_=opT[i * 128:(i + 1) * 128, c0:c1]),
                  writes=r_wld, queue="dma")
                qround(r_wld, v_wld, r_qst, v_qst, 12, c1 - c0, "op")
                E("sync", lambda e, i=i, c0=c0, c1=c1: e.dma_start(
                    out=d_opin[i * 128:(i + 1) * 128, c0:c1],
                    in_=v_qst[:, 0:c1 - c0]),
                  reads=r_qst, writes=[r_dopin], queue="dma")
        E("gpsimd", lambda e: e.collective_compute(
            "AllGather", ALU.bypass, replica_groups=[list(range(NC))],
            ins=[d_opin.ap().opt()], outs=[d_opout.ap().opt()]),
          reads=[r_dopin], writes=[r_dopout], queue="cc")

        E("sync", lambda e: e.dma_start(out=grow[:, 8:12], in_=fmask),
          reads=[r_grow], writes=[r_grow], queue="dma")
        E("sync", lambda e: e.dma_start(out=grow[:, 12:16], in_=fexps),
          reads=[r_grow], writes=[r_grow], queue="dma")
        E("sync", lambda e: e.dma_start(out=grow[:, 4:5], in_=fscale),
          reads=[r_grow], writes=[r_grow], queue="dma")
        E("scalar", lambda e: e.activation(grow[:, 8:12], grow[:, 8:12], AF.Sigmoid),
          reads=[r_grow], writes=[r_grow])
        E("vector", lambda e: e.tensor_mul(grow[:, 8:12], grow[:, 8:12],
                                           grow[:, 12:16]),
          reads=[r_grow], writes=[r_grow])
        E("vector", lambda e: e.scalar_tensor_tensor(
            grow[:, 8:12], grow[:, 8:12], grow[:1, 4:5], grow[:, 8:12],
            ALU.mult, ALU.bypass),
          reads=[r_grow], writes=[r_grow])
        E("tensor", lambda e: e.matmul(ps5[1][:, 0:4], onerow_f[:1, :],
                                       grow[:1, 8:12], start=True, stop=True),
          reads=[r_const, r_grow], writes=[r_ps5[1]])
        E("vector", lambda e: e.tensor_copy(cbn_sb[:, :], ps5[1][:, 0:4]),
          reads=[r_ps5[1]], writes=[r_cbn])
        sview = v_soft[:, 0:NB * OSL].rearrange("p (n o) -> p n o", n=NB)
        for i in range(KT):
            E("sync", lambda e, i=i: e.dma_start(
                out=v_pw, in_=pwT[i * 128:(i + 1) * 128, :, :]),
              writes=r_pw, queue="dma")
            E("scalar", lambda e, i=i: e.dma_start(
                out=v_nw, in_=nwT[i * 128:(i + 1) * 128, :, :]),
              writes=r_nw, queue="dma")
            E("scalar", lambda e: e.activation(v_pw, v_pw, AF.Sigmoid),
              reads=r_pw, writes=r_pw)
            E("scalar", lambda e: e.activation(v_nw, v_nw, AF.Sigmoid),
              reads=r_nw, writes=r_nw)
            E("vector", lambda e: e.tensor_sub(v_soft, v_pw, v_nw),
              reads=list(r_pw) + list(r_nw), writes=r_soft)
            E("vector", lambda e: e.scalar_tensor_tensor(
                v_facc, sview[:, 0, :], cbn_sb[:, 0:1], sview[:, 0, :],
                ALU.mult, ALU.bypass),
              reads=list(r_soft) + [r_cbn], writes=r_facc)
            for nn in range(1, NB):
                E("vector", lambda e, nn=nn: e.scalar_tensor_tensor(
                    v_facc, sview[:, nn, :], cbn_sb[:, nn:nn + 1], v_facc,
                    ALU.mult, ALU.add),
                  reads=list(r_soft) + [r_cbn] + list(r_facc), writes=r_facc)
            E("vector", lambda e: e.tensor_copy(v_fst, v_facc),
              reads=r_facc, writes=r_fst)
            E("sync", lambda e, i=i: e.dma_start(
                out=d_ffin[i * 128:(i + 1) * 128, :], in_=v_fst),
              reads=r_fst, writes=[r_dffin], queue="dma")
        E("gpsimd", lambda e: e.collective_compute(
            "AllGather", ALU.bypass, replica_groups=[list(range(NC))],
            ins=[d_ffin.ap().opt()], outs=[d_ffout.ap().opt()]),
          reads=[r_dffin], writes=[r_dffout], queue="cc")
        E("sync", lambda e: e.dma_start(out=v_z2[:, 0:KT * NB], in_=pbr),
          writes=r_z2, queue="dma")
        E("sync", lambda e: e.dma_start(out=v_tq[:, 0:KT * NB], in_=nbr),
          writes=r_tq, queue="dma")
        E("vector", lambda e: e.tensor_sub(v_z2[:, 0:KT * NB], v_z2[:, 0:KT * NB],
                                           v_tq[:, 0:KT * NB]),
          reads=list(r_z2) + list(r_tq), writes=r_z2)
        E("sync", lambda e: e.dma_start(out=grow[:, 0:4], in_=fbexps),
          reads=[r_grow], writes=[r_grow], queue="dma")
        E("tensor", lambda e: e.matmul(ps5[2][:, 0:4], onerow_f[:1, :],
                                       grow[:1, 0:4], start=True, stop=True),
          reads=[r_const, r_grow], writes=[r_ps5[2]])
        E("vector", lambda e: e.tensor_copy(bxc_sb[:, :], ps5[2][:, 0:4]),
          reads=[r_ps5[2]], writes=[r_bxc])
        zview = v_z2[:, 0:KT * NB].rearrange("p (k n) -> p k n", n=NB)
        E("vector", lambda e: e.scalar_tensor_tensor(
            fb2_sb[:, 0:KT], zview[:, :, 0], bxc_sb[:, 0:1], zview[:, :, 0],
            ALU.mult, ALU.bypass),
          reads=list(r_z2) + [r_bxc], writes=[r_fb2])
        for nn in range(1, NB):
            E("vector", lambda e, nn=nn: e.scalar_tensor_tensor(
                fb2_sb[:, 0:KT], zview[:, :, nn], bxc_sb[:, nn:nn + 1],
                fb2_sb[:, 0:KT], ALU.mult, ALU.add),
              reads=list(r_z2) + [r_bxc, r_fb2], writes=[r_fb2])
        E("vector", lambda e: e.tensor_scalar(fb2_sb[:, 0:KT], fb2_sb[:, 0:KT],
                                              -1.0, 1.0, ALU.max, ALU.min),
          reads=[r_fb2], writes=[r_fb2])
        E("scalar", lambda e: e.activation(bias_sb[:, :], fb2_sb[:, 0:KT],
                                           AF.Abs, scale=15.0),
          reads=[r_fb2], writes=[r_bias])
        E("vector", lambda e: e.tensor_copy(fbq_sb[:, :], bias_sb[:, :]),
          reads=[r_bias], writes=[r_fbq])
        E("scalar", lambda e: e.sign(bias_sb[:, :], fb2_sb[:, 0:KT]),
          reads=[r_fb2], writes=[r_bias])
        E("vector", lambda e: e.tensor_mul(bias_sb[:, :], fbq_sb[:, :],
                                           bias_sb[:, :]),
          reads=[r_fbq, r_bias], writes=[r_bias])
        E("sync", lambda e: e.dma_start(out=grow[:, 5:6], in_=fbsc),
          reads=[r_grow], writes=[r_grow], queue="dma")
        E("tensor", lambda e: e.matmul(ps5[3][:, 0:1], onerow_f[:1, :],
                                       grow[:1, 5:6], start=True, stop=True),
          reads=[r_const, r_grow], writes=[r_ps5[3]])
        E("vector", lambda e: e.tensor_copy(bsc_sb[:, :], ps5[3][:, 0:1]),
          reads=[r_ps5[3]], writes=[r_bsc])
        E("vector", lambda e: e.tensor_scalar_mul(bias_sb[:, :], bias_sb[:, :],
                                                  1.0 / 15.0),
          reads=[r_bias], writes=[r_bias])
        E("vector", lambda e: e.scalar_tensor_tensor(
            bias_sb[:, :], bias_sb[:, :], bsc_sb[:, 0:1], bias_sb[:, :],
            ALU.mult, ALU.bypass),
          reads=[r_bias, r_bsc], writes=[r_bias])

        def layer_norm_half(hh, mode, tag):
            sl = slice(hh * LH, (hh + 1) * LH)
            for j in range(KT):
                E("vector", lambda e, j=j, sl=sl: e.tensor_copy(
                    v_lnx, x_sb[:, j, sl]),
                  reads=[r_x[j]], writes=r_lnx, tag="lncp" + tag)
                E("scalar", lambda e, j=j, sl=sl: e.activation(
                    v_lnsq, x_sb[:, j, sl], AF.Square),
                  reads=[r_x[j]], writes=r_lnsq, tag="lnsq" + tag)
                E("tensor", lambda e, j=j: e.matmul(
                    psA[0][:1, 0:LH], onecol_b[:, 0:1], v_lnx,
                    start=(j == 0), stop=(j == KT - 1)),
                  reads=list(r_lnx) + [r_const], writes=[r_psA[0]], tag="lnm1" + tag)
                E("tensor", lambda e, j=j: e.matmul(
                    psA[1][:1, 0:LH], onecol_b[:, 0:1], v_lnsq,
                    start=(j == 0), stop=(j == KT - 1)),
                  reads=list(r_lnsq) + [r_const], writes=[r_psA[1]], tag="lnm2" + tag)
            E("vector", lambda e: e.tensor_scalar_mul(
                rows_sb[:, 0:LH], psA[0][:1, 0:LH], 1.0 / DM),
              reads=[r_psA[0]], writes=[r_rows], tag="lnme" + tag)
            E("vector", lambda e: e.tensor_scalar_mul(
                rows_sb[:, LH:2 * LH], psA[1][:1, 0:LH], 1.0 / DM),
              reads=[r_psA[1]], writes=[r_rows], tag="lnms" + tag)
            E("vector", lambda e: e.scalar_tensor_tensor(
                rows_sb[:, 2 * LH:3 * LH], rows_sb[:, 0:LH], -1.0,
                rows_sb[:, 0:LH], ALU.mult, ALU.mult),
              reads=[r_rows], writes=[r_rows], tag="lnm2b" + tag)
            E("vector", lambda e: e.tensor_add(
                rows_sb[:, LH:2 * LH], rows_sb[:, LH:2 * LH],
                rows_sb[:, 2 * LH:3 * LH]),
              reads=[r_rows], writes=[r_rows], tag="lnvar" + tag)
            E("vector", lambda e: e.tensor_scalar_add(
                rows_sb[:, LH:2 * LH], rows_sb[:, LH:2 * LH], EPSLN),
              reads=[r_rows], writes=[r_rows], tag="lneps" + tag)
            E("scalar", lambda e: e.activation(rows_sb[:, 2 * LH:3 * LH],
                                               rows_sb[:, LH:2 * LH], AF.Sqrt),
              reads=[r_rows], writes=[r_rows], tag="lnsqr" + tag)
            E("vector", lambda e: e.reciprocal(rows_sb[:, LH:2 * LH],
                                               rows_sb[:, 2 * LH:3 * LH]),
              reads=[r_rows], writes=[r_rows], tag="lnrc" + tag)
            if mode == "xq":
                E("vector", lambda e: e.tensor_scalar_mul(
                    rows_sb[:, LH:2 * LH], rows_sb[:, LH:2 * LH], 1.0 / S_IN),
                  reads=[r_rows], writes=[r_rows], tag="lnsc" + tag)
            E("vector", lambda e: e.tensor_mul(
                rows_sb[:, 3 * LH:4 * LH], rows_sb[:, 0:LH],
                rows_sb[:, LH:2 * LH]),
              reads=[r_rows], writes=[r_rows], tag="lnmrs" + tag)
            E("tensor", lambda e: e.matmul(
                psA[0][:, 0:LH], onerow_f[:1, :], rows_sb[:1, LH:2 * LH],
                start=True, stop=True),
              reads=[r_rows, r_const], writes=[r_psA[0]], tag="lnb1" + tag)
            E("tensor", lambda e: e.matmul(
                psA[1][:, 0:LH], onerow_f[:1, :], rows_sb[:1, 3 * LH:4 * LH],
                start=True, stop=True),
              reads=[r_rows, r_const], writes=[r_psA[1]], tag="lnb2" + tag)
            for j in range(KT):
                E("vector", lambda e, j=j, sl=sl: e.scalar_tensor_tensor(
                    v_ev[0], x_sb[:, j, sl], 1.0, psA[0][:, 0:LH],
                    ALU.bypass, ALU.mult),
                  reads=[r_x[j], r_psA[0]], writes=[r_ev[0]], tag="lna1" + tag)
                if mode == "bf16":
                    E("vector", lambda e, j=j: e.tensor_sub(
                        xbhv[:, j, :], v_ev[0], psA[1][:, 0:LH]),
                      reads=[r_ev[0], r_psA[1]], writes=r_xbh, tag="lna2" + tag)
                else:
                    E("vector", lambda e, j=j: e.tensor_sub(
                        v_ev[1], v_ev[0], psA[1][:, 0:LH]),
                      reads=[r_ev[0], r_psA[1]], writes=[r_ev[1]], tag="lqa2" + tag)
                    E("vector", lambda e: e.tensor_copy(ti_b[0][:, 0:LH], v_ev[1]),
                      reads=[r_ev[1]], writes=[r_ti[0]], tag="lqc" + tag)
                    E("vector", lambda e: e.tensor_scalar(
                        xq_st[:, :], ti_b[0][:, 0:LH], -128.0, 127.0,
                        ALU.max, ALU.min),
                      reads=[r_ti[0]], writes=[r_xqst], tag="lqk" + tag)
                    E("sync", lambda e, j=j, hh=hh: e.dma_start(
                        out=d_xqin[j * 128:(j + 1) * 128, hh * LH:(hh + 1) * LH],
                        in_=xq_st[:, :]),
                      reads=[r_xqst], writes=[r_dxqin], queue="dma", tag="lqs" + tag)

        E("vector", lambda e: e.reduce_max(bw_sb[:, NB:NB + 1], bw_sb[:, 0:NB],
                                           axis=AX.X),
          reads=[r_bw], writes=[r_bw])
        E("vector", lambda e: e.scalar_tensor_tensor(
            bw_sb[:, 0:NB], bw_sb[:, 0:NB], bw_sb[:, NB:NB + 1], bw_sb[:, 0:NB],
            ALU.subtract, ALU.bypass),
          reads=[r_bw], writes=[r_bw])
        E("scalar", lambda e: e.activation(bw_sb[:, 0:NB], bw_sb[:, 0:NB], AF.Exp),
          reads=[r_bw], writes=[r_bw])
        E("vector", lambda e: e.reduce_sum(bw_sb[:, NB:NB + 1], bw_sb[:, 0:NB],
                                           axis=AX.X),
          reads=[r_bw], writes=[r_bw])
        E("vector", lambda e: e.reciprocal(bw_sb[:, NB + 1:NB + 2],
                                           bw_sb[:, NB:NB + 1]),
          reads=[r_bw], writes=[r_bw])
        E("vector", lambda e: e.scalar_tensor_tensor(
            bw_sb[:, 0:NB], bw_sb[:, 0:NB], bw_sb[:, NB + 1:NB + 2],
            bw_sb[:, 0:NB], ALU.mult, ALU.bypass),
          reads=[r_bw], writes=[r_bw])
        def actmm(e):
            last = None
            for k in range(KT):
                last = e.matmul(ps5[0][:1, 0:NB], onecol_b[:, 0:1], mn_b[:, k, :],
                                start=(k == 0), stop=(k == KT - 1))
            return last
        E("tensor", actmm, reads=[r_mn, r_const], writes=[r_ps5[0]])
        E("vector", lambda e: e.tensor_single_scalar(
            grow[:1, 0:NB], ps5[0][:1, 0:NB], EPSB, ALU.is_ge),
          reads=[r_ps5[0]], writes=[r_grow])
        E("vector", lambda e: e.tensor_copy(v_qst[0:KT, 0:NB], bw_sb[:, 0:NB]),
          reads=[r_bw], writes=r_qst)
        E("tensor", lambda e: e.matmul(ps5[1][0:NB, 0:KT], v_qst[0:KT, 0:NB],
                                       ident_b[0:KT, 0:KT], start=True, stop=True),
          reads=list(r_qst) + [r_const], writes=[r_ps5[1]])
        E("tensor", lambda e: e.matmul(ps5[2][0:NB, 0:1], grow[:1, 0:NB],
                                       onecol_f[0:1, 0:1], start=True, stop=True),
          reads=[r_grow, r_const], writes=[r_ps5[2]])
        E("vector", lambda e: e.tensor_copy(act_sb[:, :], ps5[2][0:NB, 0:1]),
          reads=[r_ps5[2]], writes=[r_act])
        E("vector", lambda e: e.tensor_copy(weT_sb[:, :], ps5[1][0:NB, 0:KT]),
          reads=[r_ps5[1]], writes=[r_weT])
        E("vector", lambda e: e.tensor_scalar_mul(weT_sb[:, :], weT_sb[:, :],
                                                  act_sb[:, 0:1]),
          reads=[r_weT, r_act], writes=[r_weT])

        for hh in range(T // LH):
            sl = slice(hh * LH, (hh + 1) * LH)
            layer_norm_half(hh, "bf16", "L1h%d" % hh)
            for h in range(KT):
                b = h % 2
                E("gpsimd", lambda e, h=h, b=b: e.dma_start(
                    out=wglv[b],
                    in_=wqT[:, h * 128:(h + 1) * 128]
                        .rearrange("(k p) o -> p k o", p=128)),
                  writes=r_wgl[b], queue="dma", tag="wqld")
                E("gpsimd", lambda e, h=h, b=b: e.dma_start(
                    out=memnv[b], in_=memi[:, h, :, :].rearrange("n p e -> p n e")),
                  writes=r_memn[b], queue="dma", tag="memld")

                def qmm(e, b=b):
                    last = None
                    for kk in range(KT):
                        last = e.matmul(psA[0][:, 0:LH], wglv[b][:, kk, :],
                                        xbhv[:, kk, :],
                                        start=(kk == 0), stop=(kk == KT - 1))
                    return last
                E("tensor", qmm, reads=list(r_wgl[b]) + list(r_xbh),
                  writes=[r_psA[0]], tag="qmm")
                E("vector", lambda e: e.tensor_scalar_min(v_ev[0], psA[0][:, 0:LH],
                                                          0.0),
                  reads=[r_psA[0]], writes=[r_ev[0]], tag="sg1")
                E("scalar", lambda e: e.activation(v_ev[1], v_ev[0], AF.Exp),
                  reads=[r_ev[0]], writes=[r_ev[1]], tag="sg2")
                E("vector", lambda e: e.scalar_tensor_tensor(
                    v_sig, psA[0][:, 0:LH], 0.0, v_ev[1], ALU.max, ALU.add),
                  reads=[r_psA[0], r_ev[1]], writes=r_sig, tag="sg3")

                def nmm(e, h=h):
                    return e.matmul(ps5[2][0:NB, 0:LH], mn_b[:, h, :], v_sig,
                                    start=True, stop=True)
                E("tensor", nmm, reads=[r_mn] + list(r_sig), writes=[r_ps5[2]],
                  tag="nmm")
                E("vector", lambda e: e.tensor_scalar_max(
                    v_ev[0][0:NB, :], ps5[2][0:NB, 0:LH], EPSB),
                  reads=[r_ps5[2]], writes=[r_ev[0]], tag="rc1")
                E("vector", lambda e: e.reciprocal(v_ev[1][0:NB, :],
                                                   v_ev[0][0:NB, :]),
                  reads=[r_ev[0]], writes=[r_ev[1]], tag="rc2")
                E("vector", lambda e, h=h: e.scalar_tensor_tensor(
                    v_rr[0:NB, :], v_ev[1][0:NB, :], weT_sb[:, h:h + 1],
                    v_ev[1][0:NB, :], ALU.mult, ALU.bypass),
                  reads=[r_ev[1], r_weT], writes=r_rr, tag="rc3")
                for nn in range(NB):
                    psa_ = ps5[0]
                    r_psa_ = r_ps5[0]

                    def amm(e, nn=nn, b=b, psa_=psa_):
                        return e.matmul(psa_[:, 0:LH], memnv[b][:, nn, :], v_sig,
                                        start=True, stop=True)
                    E("tensor", amm, reads=list(r_memn[b]) + list(r_sig),
                      writes=[r_psa_], tag="amm")

                    nb2 = 0
                    v_car, r_car = car_pair[nb2]
                    v_rbS, r_rbS = rbS_pair[nb2]
                    psb = ps5[1]
                    r_psb = r_ps5[1]

                    def rbc(e, nn=nn, psb=psb):
                        return e.matmul(psb[:, 0:LH],
                                        sel_sb[:, nn * 128:(nn + 1) * 128],
                                        v_rr[0:NB, :], start=True, stop=True)
                    E("tensor", rbc, reads=[r_sel] + list(r_rr),
                      writes=[r_psb], tag="rbc")
                    E("scalar", lambda e, v_rbS=v_rbS, psb=psb: e.copy(
                        v_rbS, psb[:, 0:LH]),
                      reads=[r_psb], writes=r_rbS, tag="rbev")
                    E("vector", lambda e, v_car=v_car, v_rbS=v_rbS, psa_=psa_: (
                        e.tensor_mul(v_car, psa_[:, 0:LH], v_rbS)),
                      reads=[r_psa_] + list(r_rbS), writes=r_car, tag="car")

                    def cacc_mm(e, nn=nn, v_car=v_car):
                        return e.matmul(psA[1][:, 0:LH], ident_b[:, :], v_car,
                                        start=(nn == 0), stop=(nn == NB - 1))
                    E("tensor", cacc_mm, reads=[r_const] + list(r_car),
                      writes=[r_psA[1]], tag="cacm")
                E("scalar", lambda e, h=h: e.copy(combhv[:, h, :], psA[1][:, 0:LH]),
                  reads=[r_psA[1]], writes=r_combh, tag="cev")
            for o in range(KT):
                b = o % 2
                E("gpsimd", lambda e, o=o, b=b: e.dma_start(
                    out=wglv[b],
                    in_=woT[:, o * 128:(o + 1) * 128]
                        .rearrange("(k p) o -> p k o", p=128)),
                  writes=r_wgl[b], queue="dma", tag="wold")

                def womm(e, b=b):
                    last = None
                    for kk in range(KT):
                        last = e.matmul(psA[0][:, 0:LH], wglv[b][:, kk, :],
                                        combhv[:, kk, :],
                                        start=(kk == 0), stop=(kk == KT - 1))
                    return last
                E("tensor", womm, reads=list(r_wgl[b]) + list(r_combh),
                  writes=[r_psA[0]], tag="womm")
                E("scalar", lambda e: e.copy(v_ev[0], psA[0][:, 0:LH]),
                  reads=[r_psA[0]], writes=[r_ev[0]], tag="woev")
                E("vector", lambda e, o=o, sl=sl: e.tensor_add(
                    x_sb[:, o, sl], x_sb[:, o, sl], v_ev[0]),
                  reads=[r_ev[0], r_x[o]], writes=[r_x[o]], tag="woadd")

        def ffn(tag):
            for hh in range(T // LH):
                sl = slice(hh * LH, (hh + 1) * LH)
                layer_norm_half(hh, "bf16", "Lf%s%d" % (tag, hh))
                for o in range(KT):
                    b = o % 2
                    rk = (o * 128) // OSL
                    c0 = (o * 128) % OSL
                    E("sync", lambda e, rk=rk, c0=c0, b=b: e.dma_start(
                        out=wglv[b],
                        in_=d_ffout.ap()[rk * DM:(rk + 1) * DM, c0:c0 + 128]
                            .rearrange("(k p) o -> p k o", p=128)),
                      reads=[r_dffout], writes=r_wgl[b], queue="dma",
                      tag="ffld" + tag)

                    def fmm(e, b=b):
                        last = None
                        for kk in range(KT):
                            last = e.matmul(psA[0][:, 0:LH], wglv[b][:, kk, :],
                                            xbhv[:, kk, :],
                                            start=(kk == 0), stop=(kk == KT - 1))
                        return last
                    E("tensor", fmm, reads=list(r_wgl[b]) + list(r_xbh),
                      writes=[r_psA[0]], tag="fmm" + tag)
                    E("scalar", lambda e, o=o: e.activation(
                        v_ev[0], psA[0][:, 0:LH], AF.Identity,
                        bias=bias_sb[:, o:o + 1]),
                      reads=[r_psA[0], r_bias], writes=[r_ev[0]], tag="fev" + tag)
                    E("vector", lambda e, o=o, sl=sl: e.tensor_add(
                        x_sb[:, o, sl], x_sb[:, o, sl], v_ev[0]),
                      reads=[r_ev[0], r_x[o]], writes=[r_x[o]], tag="fadd" + tag)

        ffn("1")

        v_wld2, r_wld2 = FA.view(5120, 5120 + 1792)
        _wpair = [(v_wld, r_wld), (v_wld2, r_wld2)]
        _wi = 0
        for i in range(KT):
            for c0 in range(0, 2 * DC, WLDW):
                c1 = min(c0 + WLDW, 2 * DC)
                vw, rw = _wpair[_wi % 2]
                _wi += 1
                E("sync", lambda e, i=i, c0=c0, c1=c1, vw=vw: e.dma_start(
                    out=vw[:, 0:c1 - c0], in_=ipT[i * 128:(i + 1) * 128, c0:c1]),
                  writes=rw, queue="dma")
                qround(rw, vw, r_wqip, wqipv[:, i, c0:c1],
                       8, c1 - c0, "ip")

        for hh in range(T // LH):
            layer_norm_half(hh, "xq", "L3h%d" % hh)
        E("gpsimd", lambda e: e.collective_compute(
            "AllGather", ALU.bypass, replica_groups=[list(range(NC))],
            ins=[d_xqin.ap().opt()], outs=[d_xqout.ap().opt()]),
          reads=[r_dxqin], writes=[r_dxqout], queue="cc")

        for half_o, o_range in ((0, range(2 * DT)),):
          for tb2 in range(2 * NTB):
            rk = (tb2 * TB2) // T
            t0 = (tb2 * TB2) % T
            rb = tb2 % 2
            E("gpsimd", lambda e, rk=rk, t0=t0, rb=rb: e.dma_start(
                out=rhsv2[rb],
                in_=d_xqout.ap()[rk * DM:(rk + 1) * DM, t0:t0 + TB2]
                    .rearrange("(k p) t -> p k t", p=128)),
              reads=[r_dxqout], writes=r_rhs2[rb], queue="dma", tag="xqld")
            for o in o_range:
                pst = ps5[o % 4]
                r_pst = r_ps5[o % 4]

                def imm(e, o=o, pst=pst, rb=rb):
                    last = None
                    for kk in range(KT):
                        last = e.matmul(pst[:, 0:TB2],
                                        wqipv[:, kk, o * 128:(o + 1) * 128],
                                        rhsv2[rb][:, kk, :],
                                        start=(kk == 0), stop=(kk == KT - 1))
                    return last
                E("tensor", imm, reads=list(r_wqip) + list(r_rhs2[rb]),
                  writes=[r_pst], tag="imm")
                eb = o % 2
                E("scalar", lambda e, pst=pst, eb=eb: e.activation(
                    v_ev[eb][:, 0:TB2], pst[:, 0:TB2], AF.Identity,
                    scale=sc_sb[:, 13:14]),
                  reads=[r_pst, r_sc], writes=[r_ev[eb]], tag="iev")
                E("vector", lambda e, eb=eb: e.tensor_copy(ti_b[eb][:, 0:TB2],
                                                           v_ev[eb][:, 0:TB2]),
                  reads=[r_ev[eb]], writes=[r_ti[eb]], tag="icast")
                if o < DT:
                    E("vector", lambda e, o=o, tb2=tb2, eb=eb: e.tensor_scalar(
                        hsv[:, o, tb2 * TB2:(tb2 + 1) * TB2], ti_b[eb][:, 0:TB2],
                        -128.0, 127.0, ALU.max, ALU.min),
                      reads=[r_ti[eb]], writes=r_hs, tag="ich")
                else:
                    E("vector", lambda e, eb=eb: e.tensor_scalar(
                        yq_st[eb][:, 0:TB2], ti_b[eb][:, 0:TB2],
                        -128.0, 127.0, ALU.max, ALU.min),
                      reads=[r_ti[eb]], writes=[r_yqst[eb]], tag="icg")
                    E("sync", lambda e, o=o, tb2=tb2, eb=eb: e.dma_start(
                        out=d_gate[(o - DT) * 128:(o - DT + 1) * 128,
                                   tb2 * TB2:(tb2 + 1) * TB2],
                        in_=yq_st[eb][:, 0:TB2]),
                      reads=[r_yqst[eb]], writes=[r_dgate], queue="dma", tag="gsp")

        def u_res(d, c0, c1):
            lo = (d * S + c0) // 512
            hi = (d * S + c1 + 511) // 512
            return r_u[lo:hi]

        def xproj_half(hf):
            for tb in range(hf * (NTB // 2), (hf + 1) * (NTB // 2)):
                xproj_tb(tb, hf)
            E("gpsimd", lambda e, hf=hf: e.collective_compute(
                "AllReduce", ALU.add, replica_groups=[list(range(NC))],
                ins=[d_ssin[hf].ap().opt()], outs=[d_ssout[hf].ap().opt()]),
              reads=[r_dssin[hf]], writes=[r_dssout[hf]], queue="cc")

        def xproj_tb(tb, hf):
            r_u_tb = sum((u_res(d, tb * TBW, (tb + 1) * TBW)
                          for d in range(DT)), [])
            E("gpsimd", lambda e, tb=tb: e.dma_start(
                out=ubxv, in_=uv[:, :, tb * TBW:(tb + 1) * TBW]),
              reads=r_u_tb, writes=r_ubx, queue="dma", tag="uxld")

            def xpm(e):
                last = None
                for kk in range(DT):
                    last = e.matmul(ps5[0][:, 0:TBW], xpq_b[:, kk, 0:128],
                                    ubxv[:, kk, :],
                                    start=(kk == 0), stop=(kk == DT - 1))
                for kk in range(DT):
                    last = e.matmul(ps5[1][0:2 * N, 0:TBW], xpq_b[:, kk, 128:RN2],
                                    ubxv[:, kk, :],
                                    start=(kk == 0), stop=(kk == DT - 1))
                return last
            E("tensor", xpm, reads=[r_xpq] + list(r_ubx),
              writes=[r_ps5[0], r_ps5[1]], tag="xpm")
            E("scalar", lambda e: e.activation(v_ev[0], ps5[0][:, 0:TBW],
                                               AF.Identity, scale=sc_sb[:, 14:15]),
              reads=[r_ps5[0], r_sc], writes=[r_ev[0]], tag="xpe0")
            E("scalar", lambda e: e.activation(v_ev[1][0:2 * N, :],
                                               ps5[1][0:2 * N, 0:TBW],
                                               AF.Identity,
                                               scale=sc_sb[0:2 * N, 14:15]),
              reads=[r_ps5[1], r_sc], writes=[r_ev[1]], tag="xpe1")
            c0 = tb * TBW - hf * (S // 2)
            E("sync", lambda e, c0=c0, hf=hf: e.dma_start(
                out=d_ssin[hf][0:128, c0:c0 + TBW], in_=v_ev[0]),
              reads=[r_ev[0]], writes=[r_dssin[hf]], queue="dma", tag="xps0")
            E("sync", lambda e, c0=c0, hf=hf: e.dma_start(
                out=d_ssin[hf][128:RN2, c0:c0 + TBW],
                in_=v_ev[1][0:2 * N, :]),
              reads=[r_ev[1]], writes=[r_dssin[hf]], queue="dma", tag="xps1")

        NTH = S // CW
        for th in range(NTH):
            for d in range(DT):
                sl0 = th * CW
                v_cacc, r_cacc = cacc_pair[(th * DT + d) % 2]
                E("vector", lambda e, v_cacc=v_cacc: e.memset(v_cacc, 0.0),
                  writes=r_cacc, tag="cv0")
                for kk in range(K):
                    sh = K - 1 - kk
                    lo = sh
                    if sl0 > 0 and sh > 0:
                        E("vector", lambda e, d=d, kk=kk, sh=sh, v_cacc=v_cacc: (
                            e.scalar_tensor_tensor(
                                v_cacc[:, 0:sh], halo[:, d, K - sh:K],
                                cw_eff[:, d, kk:kk + 1], v_cacc[:, 0:sh],
                                ALU.mult, ALU.add)),
                          reads=[r_halo, r_cw] + list(r_cacc), writes=r_cacc,
                          tag="cvh")
                    E("vector", lambda e, d=d, kk=kk, sh=sh, lo=lo, sl0=sl0,
                        v_cacc=v_cacc: (
                        e.scalar_tensor_tensor(
                            v_cacc[:, lo:CW],
                            hsv[:, d, sl0 + lo - sh:sl0 + CW - sh],
                            cw_eff[:, d, kk:kk + 1], v_cacc[:, lo:CW],
                            ALU.mult, ALU.add)),
                      reads=list(r_hs) + [r_cw] + list(r_cacc), writes=r_cacc,
                      tag="cv")
                E("vector", lambda e, d=d, sl0=sl0: e.tensor_copy(
                    halo[:, d, 1:K], hsv[:, d, sl0 + CW - (K - 1):sl0 + CW]),
                  reads=r_hs, writes=[r_halo], tag="cvsv")
                E("scalar", lambda e, d=d, v_cacc=v_cacc: e.activation(
                    v_cacc, v_cacc, AF.Silu, bias=cb_sb[:, d:d + 1]),
                  reads=list(r_cacc) + [r_cb], writes=r_cacc, tag="cvsl")
                for ci in range(CW // CH):
                    E("scalar", lambda e, ci=ci, v_cacc=v_cacc: e.activation(
                        ti_u[:, 0:CH], v_cacc[:, ci * CH:(ci + 1) * CH],
                        AF.Identity, scale=1.0 / S_U),
                      reads=r_cacc, writes=r_tiu, tag="us")
                    E("vector", lambda e, d=d, sl0=sl0, ci=ci: e.tensor_scalar(
                        uv[:, d, sl0 + ci * CH:sl0 + (ci + 1) * CH],
                        ti_u[:, 0:CH], -128.0, 127.0, ALU.max, ALU.min),
                      reads=r_tiu,
                      writes=u_res(d, sl0 + ci * CH, sl0 + (ci + 1) * CH),
                      tag="uk")
            if NTH >= 2 and th == NTH // 2 - 1:
                xproj_half(0)
        if NTH < 2:
            xproj_half(0)
        xproj_half(1)

        for tb in range(NCH):
            sl = slice(tb * CH, (tb + 1) * CH)
            hf = tb // (NCH // 2)
            lsl = slice(tb * CH - hf * (S // 2), (tb + 1) * CH - hf * (S // 2))
            p2 = tb % 2
            zh = p2 * CH
            vz, rz = v_z2[:, zh:zh + CH], [r_z2[p2]]
            vt, rt = v_tq[:, zh:zh + CH], [r_tq[p2]]
            tib, rti = ti_b[p2], [r_ti[p2]]
            vsq, rsq = v_h[p2], r_h[p2]
            vkb, rkb = v_h[2 + p2], r_h[2 + p2]
            psd, rpsd = psA[p2], [r_psA[p2]]
            rb = p2 * 2 * CH
            rw0 = rows_sb[:, rb:rb + CH]
            rw1 = rows_sb[:, rb + CH:rb + 2 * CH]
            rr = r_rows2[p2 * 2:p2 * 2 + 2]
            E("sync", lambda e, lsl=lsl, hf=hf, vz=vz: e.dma_start(
                out=vz, in_=d_ssout[hf].ap()[0:128, lsl]),
              reads=[r_dssout[hf]], writes=rz, queue="dma", tag="dtld")
            E("scalar", lambda e, vz=vz, vsq=vsq: e.activation(vsq, vz, AF.Square),
              reads=rz, writes=rsq, tag="dtsq")
            E("tensor", lambda e, vsq=vsq, psd=psd: e.matmul(
                psd[:1, 0:CH], onecol_b[:, 0:1], vsq, start=True, stop=True),
              reads=rsq + [r_const], writes=rpsd, tag="dtmm")
            E("vector", lambda e, psd=psd, rw0=rw0: e.tensor_scalar(
                rw0, psd[:1, 0:CH], 1.0 / R, EPSB, ALU.mult, ALU.add),
              reads=rpsd, writes=rr, tag="dtm")
            E("scalar", lambda e, rw0=rw0, rw1=rw1: e.activation(rw1, rw0, AF.Sqrt),
              reads=rr, writes=rr, tag="dts")
            E("vector", lambda e, rw0=rw0, rw1=rw1: e.reciprocal(rw0, rw1),
              reads=rr, writes=rr, tag="dtr")
            E("tensor", lambda e, psd=psd, rw0=rw0: e.matmul(
                psd[:, 0:CH], onerow_f[:1, :], rw0[:1, :], start=True, stop=True),
              reads=rr + [r_const], writes=rpsd, tag="dtbc")
            E("vector", lambda e, vz=vz, vt=vt, psd=psd: e.scalar_tensor_tensor(
                vt, vz, wdt_sb[:, 0:1], psd[:, 0:CH], ALU.mult, ALU.mult),
              reads=rz + [r_wdtc] + rpsd, writes=rt, tag="dtap")
            E("scalar", lambda e, vt=vt, tib=tib: e.activation(
                tib[:, 0:CH], vt, AF.Identity),
              reads=rt, writes=rti, tag="dtc")
            E("vector", lambda e, sl=sl, tib=tib: e.tensor_scalar(
                v_dtq[:, sl], tib[:, 0:CH], -128.0, 127.0, ALU.max, ALU.min),
              reads=rti, writes=r_dtq, tag="dtk")
            for bci, wcol in ((0, 0), (1, 1)):
                ro = 128 + bci * N
                E("sync", lambda e, lsl=lsl, ro=ro, hf=hf, vz=vz: e.dma_start(
                    out=vz[0:N, :], in_=d_ssout[hf].ap()[ro:ro + N, lsl]),
                  reads=[r_dssout[hf]], writes=rz, queue="dma", tag="bcld")
                E("scalar", lambda e, vz=vz, vsq=vsq: e.activation(
                    vsq[0:N, :], vz[0:N, :], AF.Square),
                  reads=rz, writes=rsq, tag="bcsq")
                E("tensor", lambda e, vsq=vsq, psd=psd: e.matmul(
                    psd[:1, 0:CH], onecol_b[0:N, 0:1], vsq[0:N, :],
                    start=True, stop=True),
                  reads=rsq + [r_const], writes=rpsd, tag="bcmm")
                E("vector", lambda e, psd=psd, rw0=rw0: e.tensor_scalar(
                    rw0, psd[:1, 0:CH], 1.0 / N, EPSB, ALU.mult, ALU.add),
                  reads=rpsd, writes=rr, tag="bcm")
                E("scalar", lambda e, rw0=rw0, rw1=rw1: e.activation(
                    rw1, rw0, AF.Sqrt),
                  reads=rr, writes=rr, tag="bcs")
                E("vector", lambda e, rw0=rw0, rw1=rw1: e.reciprocal(rw0, rw1),
                  reads=rr, writes=rr, tag="bcr")
                E("tensor", lambda e, psd=psd, rw0=rw0: e.matmul(
                    psd[0:N, 0:CH], onerow_f[:1, 0:N], rw0[:1, :],
                    start=True, stop=True),
                  reads=rr + [r_const], writes=rpsd, tag="bcbc")
                E("vector", lambda e, wcol=wcol, vz=vz, vt=vt, psd=psd: (
                    e.scalar_tensor_tensor(
                        vt[0:N, :], vz[0:N, :], wbc_sb[:, wcol:wcol + 1],
                        psd[0:N, 0:CH], ALU.mult, ALU.mult)),
                  reads=rz + [r_wbcc] + rpsd, writes=rt, tag="bcap")
                E("scalar", lambda e, vt=vt, tib=tib: e.activation(
                    tib[0:N, 0:CH], vt[0:N, :], AF.Identity),
                  reads=rt, writes=rti, tag="bcc")
                E("vector", lambda e, tib=tib, vkb=vkb: e.tensor_scalar(
                    vkb[0:N, :], tib[0:N, 0:CH], -128.0, 127.0,
                    ALU.max, ALU.min),
                  reads=rti, writes=rkb, tag="bck")
                E("sync", lambda e, ro2=bci * N, sl=sl, vkb=vkb: e.dma_start(
                    out=d_bc[ro2:ro2 + N, sl], in_=vkb[0:N, :]),
                  reads=rkb, writes=[r_dbc], queue="dma", tag="bcw")

        E("vector", lambda e: e.memset(st_sb[:, :], 0.0), writes=list(r_st))
        for cidx in range(NCH):
            sl = slice(cidx * CH, (cidx + 1) * CH)
            for d in range(DT):
                zh = (d % 2) * CH
                vz, rz = v_z2[:, zh:zh + CH], [r_z2[d % 2]]
                vt, rt = v_tq[:, zh:zh + CH], [r_tq[d % 2]]
                tib, rti = ti_b[d % 2], [r_ti[d % 2]]
                psd = psA[d % 2]
                rpsd = [r_psA[d % 2]]

                def d8m(e, d=d, sl=sl, psd=psd):
                    return e.matmul(psd[:, CH:2 * CH],
                                    dpq_b[:, d * 128:(d + 1) * 128],
                                    v_dtq[:, sl], start=True, stop=True)
                E("tensor", d8m, reads=[r_dpq] + list(r_dtq), writes=rpsd,
                  tag="d8m")
                E("scalar", lambda e, tib=tib, psd=psd: e.activation(
                    tib[:, 0:CH], psd[:, CH:2 * CH], AF.Identity,
                    scale=sc_sb[:, 15:16]),
                  reads=rpsd + [r_sc], writes=rti, tag="d8e")
                E("vector", lambda e, vt=vt, tib=tib: e.tensor_scalar(
                    vt, tib[:, 0:CH], -128.0, 127.0, ALU.max, ALU.min),
                  reads=rti, writes=rt, tag="d8k")
                E("scalar", lambda e, d=d, vz=vz, vt=vt: e.activation(
                    vz, vt, AF.Identity, scale=S_DT, bias=dtb_sb[:, d:d + 1]),
                  reads=rt + [r_dtb], writes=rz, tag="sp1")
                E("scalar", lambda e, vz=vz, vt=vt: e.activation(vt, vz, AF.Abs),
                  reads=rz, writes=rt, tag="sp2")
                E("scalar", lambda e, vt=vt: e.activation(vt, vt, AF.Exp,
                                                          scale=-1.0),
                  reads=rt, writes=rt, tag="sp3")
                E("scalar", lambda e, vt=vt: e.activation(vt, vt, AF.Ln,
                                                          bias=onecol_f[:, 0:1]),
                  reads=rt + [r_const], writes=rt, tag="sp4")
                E("vector", lambda e, d=d, vz=vz, vt=vt: e.scalar_tensor_tensor(
                    dtdv[:, d, :], vz, 0.0, vt, ALU.max, ALU.add),
                  reads=rz + rt, writes=r_dtd, tag="sp5")
                E("vector", lambda e, d=d, sl=sl: e.tensor_mul(
                    wdv[:, d, :], dtdv[:, d, :], uv[:, d, sl]),
                  reads=list(r_dtd) + list(r_u), writes=r_wd, tag="wmul")
            for nq in range(N // 4):
                db2 = nq % 2
                E("gpsimd", lambda e, nq=nq, sl=sl, db2=db2: e.dma_start(
                    out=bcBf[db2].rearrange("p (n t) -> p n t", n=4),
                    in_=d_bc.ap()[nq * 4:(nq + 1) * 4, sl].partition_broadcast(128)),
                  reads=[r_dbc], writes=r_bcB[db2], queue="dma", tag="bcBl")
                E("gpsimd", lambda e, nq=nq, sl=sl, db2=db2: e.dma_start(
                    out=bcCf[db2].rearrange("p (n t) -> p n t", n=4),
                    in_=d_bc.ap()[N + nq * 4:N + (nq + 1) * 4, sl]
                        .partition_broadcast(128)),
                  reads=[r_dbc], writes=r_bcC[db2], queue="dma", tag="bcCl")
                for d in range(DT):
                    rst = r_st[d % 8]
                    for n4 in range(4):
                        nn = nq * 4 + n4
                        E("scalar", lambda e, d=d, nn=nn, n4=n4: e.activation(
                            v_dA[n4], dtdv[:, d, :], AF.Exp,
                            scale=a_sb[:, d, nn:nn + 1]),
                          reads=list(r_dtd) + [r_a], writes=[r_dA[n4]], tag="dA")
                        E("gpsimd", lambda e, d=d, n4=n4, db2=db2: (
                            e.tensor_mul(v_wB[n4], wdv[:, d, :],
                                         bcBf[db2][:, n4 * CH:(n4 + 1) * CH])),
                          reads=list(r_wd) + list(r_bcB[db2]), writes=r_wB[n4],
                          tag="wB")
                        E("vector", lambda e, d=d, nn=nn, n4=n4: (
                            e.tensor_tensor_scan(
                                v_h[n4], v_dA[n4], v_wB[n4],
                                st_sb[:, d * N + nn:d * N + nn + 1],
                                ALU.mult, ALU.add)),
                          reads=[r_dA[n4]] + list(r_wB[n4]) + [rst],
                          writes=list(r_h[n4]), tag="scan")
                    E("vector", lambda e, d=d, nq=nq: e.tensor_copy(
                        st_sb[:, d * N + nq * 4:d * N + nq * 4 + 4],
                        v_hset.rearrange("p (n t) -> p n t", n=4)[:, :, CH - 1]),
                      reads=list(r_hset), writes=[rst], tag="stsv")
                    E("vector", lambda e, db2=db2: e.tensor_mul(
                        v_chset, v_hset, bcCf[db2]),
                      reads=list(r_hset) + list(r_bcC[db2]), writes=list(r_chset),
                      tag="chm")

                    def yac(e, d=d, nq=nq):
                        last = None
                        for n4 in range(4):
                            last = e.matmul(ps5[d][:, 0:CH], ident_b[:, :],
                                            v_ch[n4],
                                            start=(nq == 0 and n4 == 0),
                                            stop=(nq == N // 4 - 1 and n4 == 3))
                        return last
                    E("tensor", yac, reads=[r_const] + list(r_chset),
                      writes=[r_ps5[d]], tag="yac")
            for d in range(DT):
                eb = d % 2
                c0 = sl.start
                E("gpsimd", lambda e, d=d, c0=c0, eb=eb: e.dma_start(
                    out=v_gl[eb],
                    in_=d_gate.ap()[d * 128:(d + 1) * 128, c0:c0 + CH]),
                  reads=[r_dgate], writes=r_gl[eb], queue="dma", tag="glr")
                E("scalar", lambda e, eb=eb: e.activation(
                    v_sil[eb], v_gl[eb], AF.Silu, scale=S_Z),
                  reads=list(r_gl[eb]), writes=[r_sil[eb]], tag="sil")
                E("vector", lambda e, d=d, eb=eb: e.tensor_scalar_mul(
                    v_ev[eb], ps5[d][:, 0:CH], S_U * S_B * S_C / S_Y),
                  reads=[r_ps5[d]], writes=[r_ev[eb]], tag="ysc")
                E("vector", lambda e, d=d, c0=c0, eb=eb: e.scalar_tensor_tensor(
                    v_ev[2 + eb], uv[:, d, c0:c0 + CH], dsk_sb[:, d:d + 1],
                    v_ev[eb], ALU.mult, ALU.add),
                  reads=list(r_u) + [r_dsk, r_ev[eb]], writes=[r_ev[2 + eb]],
                  tag="yskp")
                E("vector", lambda e, eb=eb: e.tensor_mul(
                    v_ev[eb], v_ev[2 + eb], v_sil[eb]),
                  reads=[r_ev[2 + eb], r_sil[eb]], writes=[r_ev[eb]], tag="ygat")
                E("vector", lambda e, eb=eb: e.tensor_copy(ti_b[eb][:, 0:CH],
                                                           v_ev[eb]),
                  reads=[r_ev[eb]], writes=[r_ti[eb]], tag="ycst")
                E("vector", lambda e, eb=eb: e.tensor_scalar(
                    yq_st[eb][:, :], ti_b[eb][:, 0:CH], -128.0, 127.0,
                    ALU.max, ALU.min),
                  reads=[r_ti[eb]], writes=[r_yqst[eb]], tag="yclp")
                blk = c0 // T
                col = c0 % T
                E("sync", lambda e, d=d, blk=blk, col=col, eb=eb: e.dma_start(
                    out=d_yqin[blk * DC + d * 128:blk * DC + (d + 1) * 128,
                               col:col + CH],
                    in_=yq_st[eb][:, :]),
                  reads=[r_yqst[eb]], writes=[r_dyqin], queue="dma", tag="yqw")
        with allow_ici_all_to_all():
            E("gpsimd", lambda e: e.collective_compute(
                "AllToAll", ALU.bypass, replica_groups=[list(range(NC))],
                ins=[d_yqin.ap().opt()], outs=[d_yqout.ap().opt()]),
              reads=[r_dyqin], writes=[r_dyqout], queue="cc")

        for tc in range(T // CH):
            E("gpsimd", lambda e, tc=tc: e.dma_start(
                out=yrv,
                in_=d_yqout.ap()[:, tc * CH:(tc + 1) * CH]
                    .rearrange("(k p) t -> p k t", p=128)),
              reads=[r_dyqout], writes=r_yr, queue="dma", tag="yrld")
            for o in range(KT):
                b = o % 2
                E("sync", lambda e, o=o, b=b: e.dma_start(
                    out=oplv[b],
                    in_=d_opout.ap()[:, o * 128:(o + 1) * 128]
                        .rearrange("(k p) t -> p k t", p=128)),
                  reads=[r_dopout], writes=r_opl[b], queue="dma", tag="opld")

                def omm(e, b=b):
                    last = None
                    for kk in range(NKT):
                        last = e.matmul(psA[0][:, 0:CH], oplv[b][:, kk, :],
                                        yrv[:, kk, :],
                                        start=(kk == 0), stop=(kk == NKT - 1))
                    return last
                E("tensor", omm, reads=list(r_opl[b]) + list(r_yr),
                  writes=[r_psA[0]], tag="omm")
                E("scalar", lambda e: e.activation(v_ev[0], psA[0][:, 0:CH],
                                                   AF.Identity,
                                                   scale=sc_sb[:, 7:8]),
                  reads=[r_psA[0], r_sc], writes=[r_ev[0]], tag="oev")
                E("vector", lambda e, o=o, tc=tc: e.tensor_add(
                    x_sb[:, o, tc * CH:(tc + 1) * CH],
                    x_sb[:, o, tc * CH:(tc + 1) * CH], v_ev[0]),
                  reads=[r_ev[0], r_x[o]], writes=[r_x[o]], tag="oadd")

        ffn("2")

        for j in range(KT):
            E("sync", lambda e, j=j: e.dma_start(
                out=yout[j * 128:(j + 1) * 128, :], in_=x_sb[:, j, :]),
              reads=[r_x[j]], writes=[r_out], queue="dma")
        E("sync", lambda e: e.nop(), reads=[r_out])

        p.build(ctx, block)
    return nc


_NC_CACHE = {}


def get_nc(cfg_key):
    if cfg_key not in _NC_CACHE:
        _NC_CACHE[cfg_key] = build(make_cfg(*cfg_key))
    return _NC_CACHE[cfg_key]


def host_shard(inputs, cfg):
    S, DM, DIN, NC = cfg["S"], cfg["DM"], cfg["DIN"], cfg["NC"]
    T, DC, KT, DT = cfg["T"], cfg["DC"], cfg["KT"], cfg["DT"]
    NB, N, K = cfg["NB"], cfg["N"], cfg["K"]
    OSL = cfg["OSL"]
    f32 = np.float32

    def A(x):
        return np.ascontiguousarray(np.asarray(x), dtype=f32)

    xT = A(inputs["L0_attn_a_hidden_states"][0]).T
    WqT = np.ascontiguousarray(A(inputs["L0_attn_a_Wq"]).T)
    WoT = np.ascontiguousarray(A(inputs["L0_attn_a_Wo"]).T)
    mem = A(inputs["L0_attn_a_memories"])
    mnl = np.ascontiguousarray(
        A(inputs["L0_attn_a_memory_norms"]).transpose(2, 1, 0))
    bankw = A(inputs["L0_attn_a_bank_weights"])
    pwT = np.ascontiguousarray(A(inputs["L0_ffn_pweight"]).transpose(1, 2, 0))
    nwT = np.ascontiguousarray(A(inputs["L0_ffn_nweight"]).transpose(1, 2, 0))
    pbr = np.ascontiguousarray(
        A(inputs["L0_ffn_pbias"]).reshape(KT, 128, NB).transpose(1, 0, 2))
    nbr = np.ascontiguousarray(
        A(inputs["L0_ffn_nbias"]).reshape(KT, 128, NB).transpose(1, 0, 2))
    fexps = A(inputs["L0_ffn_exps"]).reshape(1, NB)
    fbexps = A(inputs["L0_ffn_bexps"]).reshape(1, NB)
    fmask = A(inputs["L0_ffn_mask_weight"]).reshape(1, NB)
    fscale = A(inputs["L0_ffn_scale"]).reshape(1, 1)
    fbsc = A(inputs["L0_ffn_biasscale"]).reshape(1, 1)
    ipT = A(inputs["L0_attn_b_in_proj_w"]).T
    convw = A(inputs["L0_attn_b_conv_w"])
    convb = A(inputs["L0_attn_b_conv_b"])
    xpTf = A(inputs["L0_attn_b_x_proj_w"]).T
    dtln = A(inputs["L0_attn_b_dt_ln_w"]).reshape(128, 1)
    blnv = A(inputs["L0_attn_b_B_ln_w"]).reshape(N, 1)
    clnv = A(inputs["L0_attn_b_C_ln_w"]).reshape(N, 1)
    dpTf = A(inputs["L0_attn_b_dt_proj_w"]).T
    dtb = A(inputs["L0_attn_b_dt_bias"])
    alog = A(inputs["L0_attn_b_A_log"])
    dskip = A(inputs["L0_attn_b_D_skip"])
    opTf = A(inputs["L0_attn_b_out_proj_w"]).T
    ident = np.eye(128).astype(BF)
    sel = np.zeros((NB, NB * 128), dtype=BF)
    for n in range(NB):
        sel[n, n * 128:(n + 1) * 128] = 1

    maps = []
    for r in range(NC):
        dsl = slice(r * DC, (r + 1) * DC)

        def dshape(v):
            return np.ascontiguousarray(v[dsl].reshape(DT, 128).T)

        def dshape2(v):
            return np.ascontiguousarray(
                v[dsl].reshape(DT, 128, -1).transpose(1, 0, 2))

        m = {
            "xsl": np.ascontiguousarray(xT[:, r * T:(r + 1) * T]),
            "wqT": WqT, "woT": WoT, "memi": mem, "mnl": mnl, "bankw": bankw,
            "pwT": np.ascontiguousarray(pwT[:, :, r * OSL:(r + 1) * OSL]),
            "nwT": np.ascontiguousarray(nwT[:, :, r * OSL:(r + 1) * OSL]),
            "pbr": pbr, "nbr": nbr, "fexps": fexps, "fbexps": fbexps,
            "fmask": fmask, "fscale": fscale, "fbsc": fbsc,
            "ipT": np.ascontiguousarray(
                np.concatenate([ipT[:, dsl], ipT[:, DIN + r * DC:DIN + (r + 1) * DC]],
                               axis=1)),
            "cwi": dshape2(convw), "cbi": dshape(convb),
            "xpT": dshape2(xpTf),
            "dtln": dtln, "blnv": blnv, "clnv": clnv,
            "dpT": np.ascontiguousarray(dpTf[:, dsl]),
            "dtbv": dshape(dtb),
            "alog": dshape2(alog),
            "dskip": dshape(dskip),
            "opT": np.ascontiguousarray(opTf[dsl, :]),
            "identi": ident,
            "seli": sel,
        }
        maps.append(m)
    return maps


def kernel(**inputs):
    cfg = make_cfg()
    nc = get_nc((cfg["S"], cfg["DM"], cfg["DIN"], cfg["NC"]))
    in_maps = host_shard(inputs, cfg)
    res = run_bass_kernel_spmd(nc, in_maps, core_ids=list(range(cfg["NC"])))
    parts = [np.asarray(res.results[r]["yout"]) for r in range(cfg["NC"])]
    full = np.concatenate(parts, axis=1)
    return np.ascontiguousarray(full.T)[None].astype(np.float32)


# version 33
# speedup vs baseline: 1.1136x; 1.1136x over previous
import os
import numpy as np
import ml_dtypes
from contextlib import ExitStack

import concourse.bass as bass
import concourse.mybir as mybir
from concourse.bass_utils import run_bass_kernel_spmd
from concourse.aot_env import allow_ici_all_to_all

os.environ.setdefault("TRN_ALLOW_ICI_ALL_TO_ALL", "1")

STRICT = False
RELAXED = ("tensor", "sync")

F32 = mybir.dt.float32
BF16 = mybir.dt.bfloat16
I8 = mybir.dt.int8
I32 = mybir.dt.int32
AF = mybir.ActivationFunctionType
ALU = mybir.AluOpType
AX = mybir.AxisListType
BF = ml_dtypes.bfloat16

S_IN, S_Z, S_U, S_DTLN = 0.05, 0.02, 0.01, 0.02
S_B, S_C, S_DT, S_Y = 0.02, 0.02, 0.01, 0.01
EPSB = 1e-6
EPSLN = 1e-5

ENGINES = ("tensor", "vector", "scalar", "gpsimd", "sync")


class Res:
    __slots__ = ("w", "readers", "name")

    def __init__(self, name=""):
        self.w = None
        self.readers = []
        self.name = name


class Op:
    __slots__ = ("eng", "fn", "deps", "queue")

    def __init__(self, eng, fn, deps, queue):
        self.eng = eng
        self.fn = fn
        self.deps = deps
        self.queue = queue


class Prog:
    def __init__(self, nc):
        self.nc = nc
        self.ops = []

    def res(self, name=""):
        return Res(name)

    def _expand(self, deps):
        out = set()
        stack = list(deps)
        while stack:
            d = stack.pop()
            if self.ops[d].fn is None:
                stack.extend(self.ops[d].deps)
            else:
                out.add(d)
        return out

    @staticmethod
    def _flat(rs):
        out = []
        stack = list(rs)
        while stack:
            r = stack.pop()
            if isinstance(r, Res):
                out.append(r)
            else:
                stack.extend(r)
        return out

    def emit(self, eng, fn, reads=(), writes=(), queue=None, tag=""):
        assert eng in ENGINES, eng
        reads = self._flat(reads)
        writes = self._flat(writes)
        deps = set()
        for r in reads:
            if r.w is not None:
                deps.add(r.w)
        for r in writes:
            if r.w is not None:
                deps.add(r.w)
            deps.update(r.readers)
        deps = self._expand(deps)
        oid = len(self.ops)
        self.ops.append(Op(eng, fn, deps, queue))
        for r in reads:
            r.readers.append(oid)
        for r in writes:
            r.w = oid
            r.readers = []
        return oid

    def build(self, ctx, block, dma_pool=10, cc_pool=3):
        nc = self.nc
        ops = self.ops
        n = len(ops)
        inc = [False] * n
        for op in ops:
            for d in op.deps:
                dop = ops[d]
                if (STRICT or dop.queue is not None or dop.eng != op.eng
                        or dop.eng not in RELAXED):
                    inc[d] = True
        sems = {}

        def get_sem(name):
            if name not in sems:
                sems[name] = ctx.enter_context(nc.semaphore(name))
            return sems[name]

        rr = {}
        total = {}
        assign = {}
        for i, op in enumerate(ops):
            if op.fn is None:
                continue
            if op.queue == "dma":
                inc[i] = True
                j = rr.get(op.eng, 0)
                rr[op.eng] = (j + 1) % dma_pool
                s = "d_%s%d" % (op.eng, j)
                prev = total.get(s, 0)
                total[s] = prev + 16
                assign[i] = (s, prev + 16, prev, 16)
            elif op.queue == "cc":
                inc[i] = True
                j = rr.get("cc", 0)
                rr["cc"] = (j + 1) % cc_pool
                s = "cc%d" % j
                prev = total.get(s, 0)
                total[s] = prev + 1
                assign[i] = (s, prev + 1, prev, 1)
            elif inc[i]:
                s = "c_%s" % op.eng
                prev = total.get(s, 0)
                total[s] = prev + 1
                assign[i] = (s, prev + 1, None, 1)

        lastw = {e: {} for e in ENGINES}
        plan = {e: [] for e in ENGINES}
        for i, op in enumerate(ops):
            waits = []

            def addw(s, v, eng=op.eng, waits=waits):
                if lastw[eng].get(s, 0) < v:
                    waits.append((s, v))
                    lastw[eng][s] = v

            for d in sorted(op.deps):
                dop = ops[d]
                if dop.fn is None:
                    continue
                if not (STRICT or dop.queue is not None or dop.eng != op.eng
                        or dop.eng not in RELAXED):
                    continue
                s, v, _, _ = assign[d]
                addw(s, v)
            if op.fn is not None and i in assign and assign[i][2] is not None:
                s, v, prev, _ = assign[i]
                if prev > 0:
                    addw(s, prev)
            myinc = assign.get(i) if (op.fn is not None and inc[i]) else None
            plan[op.eng].append((waits, op.fn, myinc))

        for i in sorted(assign):
            get_sem(assign[i][0])

        def make_body(eng):
            def body(e):
                for waits, fn, myinc in plan[eng]:
                    for s, v in waits:
                        e.wait_ge(sems[s], v)
                    if fn is None:
                        continue
                    inst = fn(e)
                    if myinc is not None:
                        s, v, prev, step = myinc
                        inst.then_inc(sems[s], step)
            return body

        for eng in ENGINES:
            if plan[eng]:
                getattr(block, eng)(make_body(eng))


class Arena:

    def __init__(self, prog, tensor, slot=512):
        self.t = tensor
        self.slot = slot
        self.W = tensor.shape[-1]
        nslots = (self.W + slot - 1) // slot
        self.res = [prog.res("%s[%d]" % (tensor.name, i)) for i in range(nslots)]

    def view(self, lo, hi):
        ap = self.t[:, lo:hi]
        rs = self.res[lo // self.slot:(hi + self.slot - 1) // self.slot]
        return ap, rs


def make_cfg(S=8192, DM=2048, DIN=4096, NC=8):
    c = dict(S=S, DM=DM, DIN=DIN, NC=NC)
    c["T"] = S // NC
    c["DC"] = DIN // NC
    c["KT"] = DM // 128
    c["DT"] = c["DC"] // 128
    c["NKT"] = DIN // 128
    c["NB"], c["N"], c["R"], c["K"] = 4, 16, 128, 4
    c["OSL"] = DM // NC
    c["CH"] = min(512, c["T"])
    c["LH"] = c["CH"]
    c["TBW"] = c["CH"]
    assert c["T"] % c["CH"] == 0 and S % c["CH"] == 0
    assert c["OSL"] % 128 == 0 and c["DT"] >= 1
    return c


def build(cfg):
    S, DM, DIN, NC = cfg["S"], cfg["DM"], cfg["DIN"], cfg["NC"]
    T, DC, KT, DT, NKT = cfg["T"], cfg["DC"], cfg["KT"], cfg["DT"], cfg["NKT"]
    NB, N, R, K = cfg["NB"], cfg["N"], cfg["R"], cfg["K"]
    OSL, CH, LH, TBW = cfg["OSL"], cfg["CH"], cfg["LH"], cfg["TBW"]
    RN2 = R + 2 * N
    NTB = S // TBW
    NCH = S // CH

    nc = bass.Bass("TRN2", target_bir_lowering=False, debug=False,
               detect_race_conditions=STRICT)

    def inp(name, shape, dt=F32):
        return nc.dram_tensor(name, shape, dt, kind="ExternalInput").ap()

    xsl = inp("xsl", [DM, T])
    wqT = inp("wqT", [DM, DM])
    woT = inp("woT", [DM, DM])
    memi = inp("memi", [NB, KT, 128, 128])
    mnl = inp("mnl", [128, KT, NB])
    bankw = inp("bankw", [KT, NB])
    pwT = inp("pwT", [DM, NB, OSL])
    nwT = inp("nwT", [DM, NB, OSL])
    pbr = inp("pbr", [128, KT, NB])
    nbr = inp("nbr", [128, KT, NB])
    fexps = inp("fexps", [1, NB])
    fbexps = inp("fbexps", [1, NB])
    fmask = inp("fmask", [1, NB])
    fscale = inp("fscale", [1, 1])
    fbsc = inp("fbsc", [1, 1])
    ipT = inp("ipT", [DM, 2 * DC])
    cwi = inp("cwi", [128, DT, K])
    cbi = inp("cbi", [128, DT])
    xpT = inp("xpT", [128, DT, RN2])
    dtln = inp("dtln", [128, 1])
    blnv = inp("blnv", [N, 1])
    clnv = inp("clnv", [N, 1])
    dpT = inp("dpT", [128, DC])
    dtbv = inp("dtbv", [128, DT])
    alog = inp("alog", [128, DT, N])
    dskip = inp("dskip", [128, DT])
    opT = inp("opT", [DC, DM])
    identi = inp("identi", [128, 128], BF16)
    seli = inp("seli", [4, 4 * 128], BF16)

    yout = nc.dram_tensor("yout", [DM, T], F32, kind="ExternalOutput").ap()

    d_xqin = nc.dram_tensor("d_xqin", [DM, T], I8)
    d_xqout = nc.dram_tensor("d_xqout", [NC * DM, T], I8)
    d_ffin = nc.dram_tensor("d_ffin", [DM, OSL], BF16)
    d_ffout = nc.dram_tensor("d_ffout", [NC * DM, OSL], BF16)
    d_ssin = [nc.dram_tensor("d_ssin%d" % i, [RN2, S // 2], F32) for i in range(2)]
    d_ssout = [nc.dram_tensor("d_ssout%d" % i, [RN2, S // 2], F32) for i in range(2)]
    NHF = T // cfg["CH"]
    d_yqin = [nc.dram_tensor("d_yqin%d" % i, [DIN, cfg["CH"]], I8)
              for i in range(NHF)]
    d_yqout = [nc.dram_tensor("d_yqout%d" % i, [DIN, cfg["CH"]], I8)
               for i in range(NHF)]
    d_opin = nc.dram_tensor("d_opin", [DC, DM], BF16)
    d_opout = nc.dram_tensor("d_opout", [DIN, DM], BF16)
    d_mxin = nc.dram_tensor("d_mxin", [8, 1], F32)
    d_mxout = nc.dram_tensor("d_mxout", [8, 1], F32)
    d_mxv = nc.dram_tensor("d_mxv", [128 * 8], F32)
    d_row8 = nc.dram_tensor("d_row8", [8], F32)
    d_gate = nc.dram_tensor("d_gate", [DC, S], I8)
    d_bc = nc.dram_tensor("d_bc", [2 * N, S], BF16)

    p = Prog(nc)
    E = p.emit
    RZ = p.res

    with ExitStack() as ctx:
        en = ctx.enter_context

        def sbt(name, shape, dt=F32):
            return en(nc.sbuf_tensor(name, shape, dt))

        x_sb = sbt("x_sb", [128, KT, T])
        fa_t = sbt("fa", [128, 13312])
        ba_t = sbt("ba", [128, 24576], BF16)
        ia_t = sbt("ia", [128, DT * S], I8)
        ti_b = [sbt("ti%d" % i, [128, CH], I32) for i in range(2)]
        yq_st = [sbt("yqst%d" % i, [128, CH], I8) for i in range(2)]
        yq_st4 = None
        xq_st = sbt("xq_st", [128, LH], I8)
        halo = sbt("halo", [128, DT, K])
        st_sb = sbt("st_sb", [128, DT * N])
        onerow_b = sbt("onerow_b", [1, 128], BF16)
        onerow_f = sbt("onerow_f", [1, 128])
        onecol_f = sbt("onecol_f", [128, 1])
        onecol_b = sbt("onecol_b", [128, 1], BF16)
        ident_b = sbt("ident_b", [128, 128], BF16)
        mxcol = sbt("mxcol", [128, 8])
        mxrow = sbt("mxrow", [8, 128])
        grow = sbt("grow", [1, 16])
        sc_sb = sbt("sc_sb", [128, 16])
        mn_b = sbt("mn_b", [128, KT, NB], BF16)
        bw_sb = sbt("bw_sb", [KT, NB + 4])
        weT_sb = sbt("weT_sb", [NB, KT])
        act_sb = sbt("act_sb", [NB, 1])
        cw_eff = sbt("cw_eff", [128, DT, K])
        cb_sb = sbt("cb_sb", [128, DT])
        xpq_b = sbt("xpq_b", [128, DT, RN2], BF16)
        dpq_b = sbt("dpq_b", [128, DC], BF16)
        dtb_sb = sbt("dtb_sb", [128, DT])
        a_sb = sbt("a_sb", [128, DT, N])
        dsk_sb = sbt("dsk_sb", [128, DT])
        wdt_sb = sbt("wdt_sb", [128, 1])
        wbc_sb = sbt("wbc_sb", [N, 2])
        cbn_sb = sbt("cbn_sb", [128, NB])
        bxc_sb = sbt("bxc_sb", [128, NB])
        bsc_sb = sbt("bsc_sb", [128, 1])
        bias_sb = sbt("bias_sb", [128, KT])
        fbq_sb = sbt("fbq_sb", [128, KT], I32)
        fb2_sb = sbt("fb2_sb", [128, KT + 8])

        ps5 = [en(nc.psum_tensor("ps5%d" % i, [128, CH])) for i in range(4)]
        psA = [en(nc.psum_tensor("psA%d" % i, [128, 2 * CH])) for i in range(2)]

        block = en(nc.Block())

        FA = Arena(p, fa_t)
        BA = Arena(p, ba_t)
        IA = Arena(p, ia_t)

        v_z2, r_z2 = FA.view(0, 1024)
        v_tq, r_tq = FA.view(1024, 2048)
        assert len(r_z2) == 2 and len(r_tq) == 2
        v_ev, r_ev = zip(*[FA.view(2048 + i * CH, 2048 + (i + 1) * CH)
                           for i in range(4)])
        v_sil, r_sil = zip(*[FA.view(4096 + i * CH, 4096 + (i + 1) * CH)
                             for i in range(2)])
        v_dtd, r_dtd = FA.view(5120, 5120 + DT * CH)
        v_wd, r_wd = FA.view(7168, 7168 + DT * CH)
        v_dA = [fa_t[:, 9216 + i * 512:9216 + i * 512 + (CH + 1) // 2]
                .bitcast(BF16) for i in range(4)]
        r_dA = [FA.view(9216 + i * 512, 9216 + i * 512 + (CH + 1) // 2)[1]
                for i in range(4)]
        v_rows2, r_rows2 = FA.view(10240, 12288)
        ti_u = fa_t[:, 12288:12288 + CH].bitcast(I32)
        _, r_tiu = FA.view(12288, 12288 + CH)
        rows_sb = fa_t[0:1, 10240:12288]
        v_pw, r_pw = FA.view(5120, 5120 + NB * OSL)
        v_nw, r_nw = FA.view(5120 + NB * OSL, 5120 + 2 * NB * OSL)
        v_soft, r_soft = FA.view(7168, 7168 + NB * OSL)
        v_facc, r_facc = FA.view(8192, 8192 + OSL)
        v_wld, r_wld = FA.view(8448, 8448 + 1792)
        CW = min(2048, S)
        cacc_pair = [FA.view(5120, 5120 + CW), FA.view(7168, 7168 + CW)]

        dtdv = v_dtd.rearrange("p (d c) -> p d c", d=DT)
        wdv = v_wd.rearrange("p (d c) -> p d c", d=DT)

        v_xbh, r_xbh = BA.view(0, KT * LH)
        xbhv = v_xbh.rearrange("p (k t) -> p k t", k=KT)
        v_combh, r_combh = BA.view(KT * LH, 2 * KT * LH)
        combhv = v_combh.rearrange("p (k t) -> p k t", k=KT)
        v_wgl, r_wgl = zip(*[BA.view(16384 + i * 2048, 16384 + (i + 1) * 2048)
                             for i in range(2)])
        wglv = [v[:, 0:KT * 128].rearrange("p (k o) -> p k o", k=KT) for v in v_wgl]
        v_memn, r_memn = zip(*[BA.view(20480 + i * NB * 128,
                                       20480 + (i + 1) * NB * 128)
                               for i in range(2)])
        memnv = [v.rearrange("p (n e) -> p n e", n=NB) for v in v_memn]
        v_sig, r_sig = BA.view(21504, 21504 + LH)
        v_rr, r_rr = BA.view(22016, 22016 + LH)
        v_qst, r_qst = BA.view(16384, 16384 + 2048)
        car_pair = [BA.view(22528, 22528 + CH), BA.view(20480, 20480 + CH)]
        v_lnx, r_lnx = BA.view(23040, 23040 + CH)
        v_lnsq, r_lnsq = BA.view(23552, 23552 + CH)
        rbS_pair = [BA.view(24064, 24064 + CH), BA.view(20992, 20992 + CH)]
        sel_sb = sbt("sel_sb", [4, 4 * 128], BF16)
        v_fst, r_fst = BA.view(18432, 18432 + OSL)
        v_dtq, r_dtq = BA.view(0, S)
        v_wqip, r_wqip = BA.view(0, KT * 1024)
        wqipv = v_wqip.rearrange("p (k o) -> p k o", k=KT)
        TB2 = TBW // 2
        rhs_pair = [BA.view(16384, 16384 + KT * TB2),
                    BA.view(16384 + KT * TB2, 16384 + 2 * KT * TB2)]
        rhsv2 = [v.rearrange("p (k t) -> p k t", k=KT) for v, _ in rhs_pair]
        r_rhs2 = [r for _, r in rhs_pair]
        bcB_pair = [BA.view(8192 + i * 8 * CH, 8192 + i * 8 * CH + 4 * CH)
                    for i in range(2)]
        bcC_pair = [BA.view(8192 + i * 8 * CH + 4 * CH, 8192 + (i + 1) * 8 * CH)
                    for i in range(2)]
        bcBf = [v for v, _ in bcB_pair]
        bcCf = [v for v, _ in bcC_pair]
        r_bcB = [r for _, r in bcB_pair]
        r_bcC = [r for _, r in bcC_pair]
        v_wBset, r_wBset = BA.view(16384, 16384 + 4 * CH)
        v_wB = [v_wBset[:, i * CH:(i + 1) * CH] for i in range(4)]
        r_wB = [r_wBset[min(i, len(r_wBset) - 1):min(i, len(r_wBset) - 1) + 1]
               for i in range(4)]
        v_hset, r_hset = BA.view(20480, 20480 + 4 * CH)
        v_h = [v_hset[:, i * CH:(i + 1) * CH] for i in range(4)]
        r_h = [r_hset[min(i, len(r_hset) - 1):min(i, len(r_hset) - 1) + 1]
               for i in range(4)]
        v_chset, r_chset = BA.view(22528, 22528 + 4 * CH)
        v_ch = [v_chset[:, i * CH:(i + 1) * CH] for i in range(4)]
        r_ch = [r_chset[min(i, len(r_chset) - 1):min(i, len(r_chset) - 1) + 1]
               for i in range(4)]
        v_gl, r_gl = zip(*[BA.view(16384 + (6 + i) * CH, 16384 + (7 + i) * CH)
                           for i in range(2)])
        v_yr, r_yr = BA.view(0, NKT * CH)
        yrv = v_yr.rearrange("p (k t) -> p k t", k=NKT)
        v_opl, r_opl = zip(*[BA.view(16384 + i * NKT * 128,
                                     16384 + (i + 1) * NKT * 128)
                             for i in range(2)])
        oplv = [v.rearrange("p (k o) -> p k o", k=NKT) for v in v_opl]
        v_ubx, r_ubx = BA.view(8192, 8192 + DT * TBW)
        ubxv = v_ubx.rearrange("p (d t) -> p d t", d=DT)
        _yqb = [BA.view(18432 + i * 512, 18432 + i * 512 + (CH + 1) // 2)
                for i in range(2)]
        yq_st4 = [yq_st[0], yq_st[1],
                  _yqb[0][0].bitcast(I8), _yqb[1][0].bitcast(I8)]

        v_hs, r_hs = IA.view(0, DT * S)
        hsv = v_hs.rearrange("p (d t) -> p d t", d=DT)
        uv = hsv
        r_u = r_hs

        r_x = [RZ("x%d" % j) for j in range(KT)]
        r_ti = [RZ("ti%d" % i) for i in range(2)]
        r_yqst = [RZ("yqst%d" % i) for i in range(2)]
        r_yqst4 = [r_yqst[0], r_yqst[1], _yqb[0][1], _yqb[1][1]]
        r_xqst = RZ("xqst")
        r_halo = RZ("halo")
        r_st = [RZ("st%d" % i) for i in range(8)]
        r_const = RZ("const")
        r_mxcol, r_mxrow, r_grow, r_sc = RZ("mxc"), RZ("mxr"), RZ("grow"), RZ("sc")
        r_mn, r_bw, r_weT, r_act = RZ("mn"), RZ("bw"), RZ("weT"), RZ("act")
        r_cw, r_cb, r_xpq, r_dpq = RZ("cw"), RZ("cb"), RZ("xpq"), RZ("dpq")
        r_dtb, r_a, r_dsk, r_wdtc, r_wbcc = RZ("dtb"), RZ("a"), RZ("dsk"), RZ("wdt"), RZ("wbc")
        r_cbn, r_bxc, r_bsc, r_bias, r_fbq, r_fb2 = (RZ("cbn"), RZ("bxc"),
                                                     RZ("bsc"), RZ("bias"),
                                                     RZ("fbq"), RZ("fb2"))
        r_rows = r_rows2
        r_ps5 = [RZ("ps5%d" % i) for i in range(4)]
        r_psA = [RZ("psA%d" % i) for i in range(2)]
        r_dxqin, r_dxqout = RZ("dxqin"), RZ("dxqout")
        r_dffin, r_dffout = RZ("dffin"), RZ("dffout")
        r_dssin = [RZ("dssin%d" % i) for i in range(2)]
        r_dssout = [RZ("dssout%d" % i) for i in range(2)]
        r_dyqin = [RZ("dyqin%d" % i) for i in range(NHF)]
        r_dyqout = [RZ("dyqout%d" % i) for i in range(NHF)]
        r_dopin, r_dopout = RZ("dopin"), RZ("dopout")
        r_dmxin, r_dmxout, r_dmxv, r_drow8 = RZ("dmxi"), RZ("dmxo"), RZ("dmxv"), RZ("dr8")
        r_dgate, r_dbc = RZ("dgate"), RZ("dbc")
        r_sel = RZ("sel")
        r_out = RZ("out")

        LRS = list(r_ev) + list(r_sil) + [r_z2, r_tq]

        E("vector", lambda e: e.memset(onerow_b[:, :], 1.0), writes=[r_const])
        E("vector", lambda e: e.memset(onerow_f[:, :], 1.0), writes=[r_const])
        E("vector", lambda e: e.memset(onecol_f[:, :], 1.0), writes=[r_const])
        E("vector", lambda e: e.memset(onecol_b[:, :], 1.0), writes=[r_const])
        E("sync", lambda e: e.dma_start(out=ident_b[:, :], in_=identi),
          writes=[r_const], queue="dma")
        E("sync", lambda e: e.dma_start(out=sel_sb[:, :], in_=seli),
          writes=[r_sel], queue="dma")
        E("sync", lambda e: e.dma_start(out=bw_sb[:, 0:NB], in_=bankw),
          writes=[r_bw], queue="dma")
        E("gpsimd", lambda e: e.dma_start(out=mn_b[:, :, :], in_=mnl),
          writes=[r_mn], queue="dma")
        E("sync", lambda e: e.dma_start(out=cw_eff[:, :, :], in_=cwi),
          writes=[r_cw], queue="dma")
        E("sync", lambda e: e.dma_start(out=cb_sb[:, :], in_=cbi),
          writes=[r_cb], queue="dma")
        E("sync", lambda e: e.dma_start(out=wdt_sb[:, :], in_=dtln),
          writes=[r_wdtc], queue="dma")
        E("sync", lambda e: e.dma_start(out=wbc_sb[:, 0:1], in_=blnv),
          writes=[r_wbcc], queue="dma")
        E("sync", lambda e: e.dma_start(out=wbc_sb[:, 1:2], in_=clnv),
          writes=[r_wbcc], queue="dma")
        E("sync", lambda e: e.dma_start(out=dtb_sb[:, :], in_=dtbv),
          writes=[r_dtb], queue="dma")
        E("sync", lambda e: e.dma_start(out=a_sb[:, :, :], in_=alog),
          writes=[r_a], queue="dma")
        E("sync", lambda e: e.dma_start(out=dsk_sb[:, :], in_=dskip),
          writes=[r_dsk], queue="dma")
        E("scalar", lambda e: e.activation(a_sb[:, :, :], a_sb[:, :, :], AF.Exp),
          reads=[r_a], writes=[r_a])
        E("vector", lambda e: e.tensor_scalar_mul(a_sb[:, :, :], a_sb[:, :, :], -1.0),
          reads=[r_a], writes=[r_a])
        E("vector", lambda e: e.tensor_scalar_mul(wdt_sb[:, :], wdt_sb[:, :],
                                                  1.0 / S_DTLN),
          reads=[r_wdtc], writes=[r_wdtc])
        E("vector", lambda e: e.tensor_scalar_mul(wbc_sb[:, 0:1], wbc_sb[:, 0:1],
                                                  1.0 / S_B),
          reads=[r_wbcc], writes=[r_wbcc])
        E("vector", lambda e: e.tensor_scalar_mul(wbc_sb[:, 1:2], wbc_sb[:, 1:2],
                                                  1.0 / S_C),
          reads=[r_wbcc], writes=[r_wbcc])
        for j in range(KT):
            E("sync", lambda e, j=j: e.dma_start(out=x_sb[:, j, :],
                                                 in_=xsl[j * 128:(j + 1) * 128, :]),
              writes=[r_x[j]], queue="dma")

        WLDW = 1792
        E("vector", lambda e: e.memset(mxcol[:, :], 0.0), writes=[r_mxcol])

        def absmax_into(col, src_res, src_ap, first):
            if first:
                E("vector", lambda e: e.reduce_max(mxcol[:, col:col + 1], src_ap,
                                                   axis=AX.X,
                                                   apply_absolute_value=True),
                  reads=src_res, writes=[r_mxcol])
            else:
                E("vector", lambda e: e.reduce_max(mxcol[:, 7:8], src_ap, axis=AX.X,
                                                   apply_absolute_value=True),
                  reads=src_res, writes=[r_mxcol])
                E("vector", lambda e: e.tensor_max(mxcol[:, col:col + 1],
                                                   mxcol[:, col:col + 1],
                                                   mxcol[:, 7:8]),
                  reads=[r_mxcol], writes=[r_mxcol])

        first = True
        for i in range(KT):
            for c0 in range(0, 2 * DC, WLDW):
                c1 = min(c0 + WLDW, 2 * DC)
                E("sync", lambda e, i=i, c0=c0, c1=c1: e.dma_start(
                    out=v_wld[:, 0:c1 - c0], in_=ipT[i * 128:(i + 1) * 128, c0:c1]),
                  writes=r_wld, queue="dma")
                absmax_into(0, r_wld, v_wld[:, 0:c1 - c0], first)
                first = False
        absmax_into(1, [r_cw], cw_eff.ap().rearrange("p a b -> p (a b)"), True)
        E("sync", lambda e: e.dma_start(out=v_wld[:, 0:DT * RN2], in_=xpT),
          writes=r_wld, queue="dma")
        absmax_into(2, r_wld, v_wld[:, 0:DT * RN2], True)
        E("vector", lambda e: e.tensor_copy(v_rows2[:, 0:DT * RN2],
                                            v_wld[:, 0:DT * RN2]),
          reads=r_wld, writes=r_rows2)
        E("sync", lambda e: e.dma_start(out=v_wld[:, 0:DC], in_=dpT),
          writes=r_wld, queue="dma")
        absmax_into(3, r_wld, v_wld[:, 0:DC], True)
        E("vector", lambda e: e.tensor_copy(v_z2[:, 0:DC], v_wld[:, 0:DC]),
          reads=r_wld, writes=r_z2)
        first = True
        for i in range(DT):
            for c0 in range(0, DM, WLDW):
                c1 = min(c0 + WLDW, DM)
                E("sync", lambda e, i=i, c0=c0, c1=c1: e.dma_start(
                    out=v_wld[:, 0:c1 - c0], in_=opT[i * 128:(i + 1) * 128, c0:c1]),
                  writes=r_wld, queue="dma")
                absmax_into(4, r_wld, v_wld[:, 0:c1 - c0], first)
                first = False
        E("sync", lambda e: e.dma_start(out=d_mxv[:], in_=mxcol[:, 0:8]),
          reads=[r_mxcol], writes=[r_dmxv], queue="dma")
        def mxv_rd(e):
            with nc.allow_non_contiguous_dma(reason="tiny 4KB transpose read"):
                return e.dma_start(out=mxrow[:, :],
                                   in_=d_mxv.ap().rearrange("(p c) -> c p", c=8))
        E("sync", mxv_rd, reads=[r_dmxv], writes=[r_mxrow], queue="dma")
        E("vector", lambda e: e.reduce_max(fb2_sb[0:8, KT:KT + 1], mxrow[:, :],
                                           axis=AX.X),
          reads=[r_mxrow], writes=[r_fb2])
        E("sync", lambda e: e.dma_start(out=d_mxin[:, :], in_=fb2_sb[0:8, KT:KT + 1]),
          reads=[r_fb2], writes=[r_dmxin], queue="dma")
        E("gpsimd", lambda e: e.collective_compute(
            "AllReduce", ALU.max, replica_groups=[list(range(NC))],
            ins=[d_mxin.ap().opt()], outs=[d_mxout.ap().opt()]),
          reads=[r_dmxin], writes=[r_dmxout], queue="cc")
        E("sync", lambda e: e.dma_start(out=d_row8[:], in_=d_mxout[:, :]),
          reads=[r_dmxout], writes=[r_drow8], queue="dma")
        E("sync", lambda e: e.dma_start(
            out=grow[:, 0:8], in_=d_row8.ap().rearrange("(a b) -> a b", a=1)),
          reads=[r_drow8], writes=[r_grow], queue="dma")
        E("vector", lambda e: e.reciprocal(grow[:, 8:13], grow[:, 0:5]),
          reads=[r_grow], writes=[r_grow])
        E("vector", lambda e: e.tensor_scalar_mul(grow[:, 8:13], grow[:, 8:13],
                                                  127.0),
          reads=[r_grow], writes=[r_grow])
        E("vector", lambda e: e.tensor_scalar_mul(grow[:, 13:14], grow[:, 0:1],
                                                  S_IN / (127.0 * S_Z)),
          reads=[r_grow], writes=[r_grow])
        E("vector", lambda e: e.tensor_scalar_mul(grow[:, 14:15], grow[:, 2:3],
                                                  S_U / 127.0),
          reads=[r_grow], writes=[r_grow])
        E("vector", lambda e: e.tensor_scalar_mul(grow[:, 15:16], grow[:, 3:4],
                                                  S_DTLN / (127.0 * S_DT)),
          reads=[r_grow], writes=[r_grow])
        E("vector", lambda e: e.tensor_scalar_mul(grow[:, 7:8], grow[:, 4:5],
                                                  S_Y / 127.0),
          reads=[r_grow], writes=[r_grow])
        E("vector", lambda e: e.tensor_scalar_mul(grow[:, 6:7], grow[:, 1:2],
                                                  S_Z / 127.0),
          reads=[r_grow], writes=[r_grow])
        E("tensor", lambda e: e.matmul(ps5[0][:, 0:16], onerow_f[:1, :],
                                       grow[:1, :], start=True, stop=True),
          reads=[r_const, r_grow], writes=[r_ps5[0]])
        E("vector", lambda e: e.tensor_copy(sc_sb[:, :], ps5[0][:, 0:16]),
          reads=[r_ps5[0]], writes=[r_sc])

        def qround(src_res, src_ap, dst_res, dst_ap, qcol, Wd, tag=""):
            for c0 in range(0, Wd, CH):
                c1 = min(c0 + CH, Wd)
                E("vector", lambda e, c0=c0, c1=c1: e.scalar_tensor_tensor(
                    v_ev[3][:, 0:c1 - c0], src_ap[:, c0:c1],
                    sc_sb[:, qcol:qcol + 1], src_ap[:, c0:c1],
                    ALU.mult, ALU.bypass),
                  reads=list(src_res) + [r_sc], writes=[r_ev[3]], tag="q1" + tag)
                E("vector", lambda e, c0=c0, c1=c1: e.tensor_copy(
                    ti_b[1][:, 0:c1 - c0], v_ev[3][:, 0:c1 - c0]),
                  reads=[r_ev[3]], writes=[r_ti[1]], tag="q2" + tag)
                E("vector", lambda e, c0=c0, c1=c1: e.tensor_scalar(
                    dst_ap[:, c0:c1], ti_b[1][:, 0:c1 - c0], -128.0, 127.0,
                    ALU.max, ALU.min),
                  reads=[r_ti[1]], writes=list(dst_res), tag="q3" + tag)

        qround(r_rows2, v_rows2, [r_xpq],
               xpq_b.ap().rearrange("p a b -> p (a b)"), 10, DT * RN2, "xp")
        qround(r_z2, v_z2, [r_dpq], dpq_b, 11, DC, "dp")
        E("vector", lambda e: e.scalar_tensor_tensor(
            v_ev[3][:, 0:DT * K], cw_eff.ap().rearrange("p a b -> p (a b)"),
            sc_sb[:, 9:10], cw_eff.ap().rearrange("p a b -> p (a b)"),
            ALU.mult, ALU.bypass),
          reads=[r_cw, r_sc], writes=[r_ev[3]])
        E("vector", lambda e: e.tensor_copy(ti_b[1][:, 0:DT * K],
                                            v_ev[3][:, 0:DT * K]),
          reads=[r_ev[3]], writes=[r_ti[1]])
        E("vector", lambda e: e.tensor_scalar(v_ev[3][:, 0:DT * K],
                                              ti_b[1][:, 0:DT * K],
                                              -128.0, 127.0, ALU.max, ALU.min),
          reads=[r_ti[1]], writes=[r_ev[3]])
        E("vector", lambda e: e.scalar_tensor_tensor(
            cw_eff.ap().rearrange("p a b -> p (a b)"), v_ev[3][:, 0:DT * K],
            sc_sb[:, 6:7], v_ev[3][:, 0:DT * K], ALU.mult, ALU.bypass),
          reads=[r_ev[3], r_sc], writes=[r_cw])
        for i in range(DT):
            for c0 in range(0, DM, WLDW):
                c1 = min(c0 + WLDW, DM)
                E("sync", lambda e, i=i, c0=c0, c1=c1: e.dma_start(
                    out=v_wld[:, 0:c1 - c0], in_=opT[i * 128:(i + 1) * 128, c0:c1]),
                  writes=r_wld, queue="dma")
                qround(r_wld, v_wld, r_qst, v_qst, 12, c1 - c0, "op")
                E("sync", lambda e, i=i, c0=c0, c1=c1: e.dma_start(
                    out=d_opin[i * 128:(i + 1) * 128, c0:c1],
                    in_=v_qst[:, 0:c1 - c0]),
                  reads=r_qst, writes=[r_dopin], queue="dma")
        E("gpsimd", lambda e: e.collective_compute(
            "AllGather", ALU.bypass, replica_groups=[list(range(NC))],
            ins=[d_opin.ap().opt()], outs=[d_opout.ap().opt()]),
          reads=[r_dopin], writes=[r_dopout], queue="cc")

        E("sync", lambda e: e.dma_start(out=grow[:, 8:12], in_=fmask),
          reads=[r_grow], writes=[r_grow], queue="dma")
        E("sync", lambda e: e.dma_start(out=grow[:, 12:16], in_=fexps),
          reads=[r_grow], writes=[r_grow], queue="dma")
        E("sync", lambda e: e.dma_start(out=grow[:, 4:5], in_=fscale),
          reads=[r_grow], writes=[r_grow], queue="dma")
        E("scalar", lambda e: e.activation(grow[:, 8:12], grow[:, 8:12], AF.Sigmoid),
          reads=[r_grow], writes=[r_grow])
        E("vector", lambda e: e.tensor_mul(grow[:, 8:12], grow[:, 8:12],
                                           grow[:, 12:16]),
          reads=[r_grow], writes=[r_grow])
        E("vector", lambda e: e.scalar_tensor_tensor(
            grow[:, 8:12], grow[:, 8:12], grow[:1, 4:5], grow[:, 8:12],
            ALU.mult, ALU.bypass),
          reads=[r_grow], writes=[r_grow])
        E("tensor", lambda e: e.matmul(ps5[1][:, 0:4], onerow_f[:1, :],
                                       grow[:1, 8:12], start=True, stop=True),
          reads=[r_const, r_grow], writes=[r_ps5[1]])
        E("vector", lambda e: e.tensor_copy(cbn_sb[:, :], ps5[1][:, 0:4]),
          reads=[r_ps5[1]], writes=[r_cbn])
        sview = v_soft[:, 0:NB * OSL].rearrange("p (n o) -> p n o", n=NB)
        for i in range(KT):
            E("sync", lambda e, i=i: e.dma_start(
                out=v_pw, in_=pwT[i * 128:(i + 1) * 128, :, :]),
              writes=r_pw, queue="dma")
            E("scalar", lambda e, i=i: e.dma_start(
                out=v_nw, in_=nwT[i * 128:(i + 1) * 128, :, :]),
              writes=r_nw, queue="dma")
            E("scalar", lambda e: e.activation(v_pw, v_pw, AF.Sigmoid),
              reads=r_pw, writes=r_pw)
            E("scalar", lambda e: e.activation(v_nw, v_nw, AF.Sigmoid),
              reads=r_nw, writes=r_nw)
            E("vector", lambda e: e.tensor_sub(v_soft, v_pw, v_nw),
              reads=list(r_pw) + list(r_nw), writes=r_soft)
            E("vector", lambda e: e.scalar_tensor_tensor(
                v_facc, sview[:, 0, :], cbn_sb[:, 0:1], sview[:, 0, :],
                ALU.mult, ALU.bypass),
              reads=list(r_soft) + [r_cbn], writes=r_facc)
            for nn in range(1, NB):
                E("vector", lambda e, nn=nn: e.scalar_tensor_tensor(
                    v_facc, sview[:, nn, :], cbn_sb[:, nn:nn + 1], v_facc,
                    ALU.mult, ALU.add),
                  reads=list(r_soft) + [r_cbn] + list(r_facc), writes=r_facc)
            E("vector", lambda e: e.tensor_copy(v_fst, v_facc),
              reads=r_facc, writes=r_fst)
            E("sync", lambda e, i=i: e.dma_start(
                out=d_ffin[i * 128:(i + 1) * 128, :], in_=v_fst),
              reads=r_fst, writes=[r_dffin], queue="dma")
        E("gpsimd", lambda e: e.collective_compute(
            "AllGather", ALU.bypass, replica_groups=[list(range(NC))],
            ins=[d_ffin.ap().opt()], outs=[d_ffout.ap().opt()]),
          reads=[r_dffin], writes=[r_dffout], queue="cc")
        E("sync", lambda e: e.dma_start(out=v_z2[:, 0:KT * NB], in_=pbr),
          writes=r_z2, queue="dma")
        E("sync", lambda e: e.dma_start(out=v_tq[:, 0:KT * NB], in_=nbr),
          writes=r_tq, queue="dma")
        E("vector", lambda e: e.tensor_sub(v_z2[:, 0:KT * NB], v_z2[:, 0:KT * NB],
                                           v_tq[:, 0:KT * NB]),
          reads=list(r_z2) + list(r_tq), writes=r_z2)
        E("sync", lambda e: e.dma_start(out=grow[:, 0:4], in_=fbexps),
          reads=[r_grow], writes=[r_grow], queue="dma")
        E("tensor", lambda e: e.matmul(ps5[2][:, 0:4], onerow_f[:1, :],
                                       grow[:1, 0:4], start=True, stop=True),
          reads=[r_const, r_grow], writes=[r_ps5[2]])
        E("vector", lambda e: e.tensor_copy(bxc_sb[:, :], ps5[2][:, 0:4]),
          reads=[r_ps5[2]], writes=[r_bxc])
        zview = v_z2[:, 0:KT * NB].rearrange("p (k n) -> p k n", n=NB)
        E("vector", lambda e: e.scalar_tensor_tensor(
            fb2_sb[:, 0:KT], zview[:, :, 0], bxc_sb[:, 0:1], zview[:, :, 0],
            ALU.mult, ALU.bypass),
          reads=list(r_z2) + [r_bxc], writes=[r_fb2])
        for nn in range(1, NB):
            E("vector", lambda e, nn=nn: e.scalar_tensor_tensor(
                fb2_sb[:, 0:KT], zview[:, :, nn], bxc_sb[:, nn:nn + 1],
                fb2_sb[:, 0:KT], ALU.mult, ALU.add),
              reads=list(r_z2) + [r_bxc, r_fb2], writes=[r_fb2])
        E("vector", lambda e: e.tensor_scalar(fb2_sb[:, 0:KT], fb2_sb[:, 0:KT],
                                              -1.0, 1.0, ALU.max, ALU.min),
          reads=[r_fb2], writes=[r_fb2])
        E("scalar", lambda e: e.activation(bias_sb[:, :], fb2_sb[:, 0:KT],
                                           AF.Abs, scale=15.0),
          reads=[r_fb2], writes=[r_bias])
        E("vector", lambda e: e.tensor_copy(fbq_sb[:, :], bias_sb[:, :]),
          reads=[r_bias], writes=[r_fbq])
        E("scalar", lambda e: e.sign(bias_sb[:, :], fb2_sb[:, 0:KT]),
          reads=[r_fb2], writes=[r_bias])
        E("vector", lambda e: e.tensor_mul(bias_sb[:, :], fbq_sb[:, :],
                                           bias_sb[:, :]),
          reads=[r_fbq, r_bias], writes=[r_bias])
        E("sync", lambda e: e.dma_start(out=grow[:, 5:6], in_=fbsc),
          reads=[r_grow], writes=[r_grow], queue="dma")
        E("tensor", lambda e: e.matmul(ps5[3][:, 0:1], onerow_f[:1, :],
                                       grow[:1, 5:6], start=True, stop=True),
          reads=[r_const, r_grow], writes=[r_ps5[3]])
        E("vector", lambda e: e.tensor_copy(bsc_sb[:, :], ps5[3][:, 0:1]),
          reads=[r_ps5[3]], writes=[r_bsc])
        E("vector", lambda e: e.tensor_scalar_mul(bias_sb[:, :], bias_sb[:, :],
                                                  1.0 / 15.0),
          reads=[r_bias], writes=[r_bias])
        E("vector", lambda e: e.scalar_tensor_tensor(
            bias_sb[:, :], bias_sb[:, :], bsc_sb[:, 0:1], bias_sb[:, :],
            ALU.mult, ALU.bypass),
          reads=[r_bias, r_bsc], writes=[r_bias])

        def layer_norm_half(hh, mode, tag):
            sl = slice(hh * LH, (hh + 1) * LH)
            for j in range(KT):
                E("vector", lambda e, j=j, sl=sl: e.tensor_copy(
                    v_lnx, x_sb[:, j, sl]),
                  reads=[r_x[j]], writes=r_lnx, tag="lncp" + tag)
                E("scalar", lambda e, j=j, sl=sl: e.activation(
                    v_lnsq, x_sb[:, j, sl], AF.Square),
                  reads=[r_x[j]], writes=r_lnsq, tag="lnsq" + tag)
                E("tensor", lambda e, j=j: e.matmul(
                    psA[0][:1, 0:LH], onecol_b[:, 0:1], v_lnx,
                    start=(j == 0), stop=(j == KT - 1)),
                  reads=list(r_lnx) + [r_const], writes=[r_psA[0]], tag="lnm1" + tag)
                E("tensor", lambda e, j=j: e.matmul(
                    psA[1][:1, 0:LH], onecol_b[:, 0:1], v_lnsq,
                    start=(j == 0), stop=(j == KT - 1)),
                  reads=list(r_lnsq) + [r_const], writes=[r_psA[1]], tag="lnm2" + tag)
            E("vector", lambda e: e.tensor_scalar_mul(
                rows_sb[:, 0:LH], psA[0][:1, 0:LH], 1.0 / DM),
              reads=[r_psA[0]], writes=[r_rows], tag="lnme" + tag)
            E("vector", lambda e: e.tensor_scalar_mul(
                rows_sb[:, LH:2 * LH], psA[1][:1, 0:LH], 1.0 / DM),
              reads=[r_psA[1]], writes=[r_rows], tag="lnms" + tag)
            E("vector", lambda e: e.scalar_tensor_tensor(
                rows_sb[:, 2 * LH:3 * LH], rows_sb[:, 0:LH], -1.0,
                rows_sb[:, 0:LH], ALU.mult, ALU.mult),
              reads=[r_rows], writes=[r_rows], tag="lnm2b" + tag)
            E("vector", lambda e: e.tensor_add(
                rows_sb[:, LH:2 * LH], rows_sb[:, LH:2 * LH],
                rows_sb[:, 2 * LH:3 * LH]),
              reads=[r_rows], writes=[r_rows], tag="lnvar" + tag)
            E("vector", lambda e: e.tensor_scalar_add(
                rows_sb[:, LH:2 * LH], rows_sb[:, LH:2 * LH], EPSLN),
              reads=[r_rows], writes=[r_rows], tag="lneps" + tag)
            E("scalar", lambda e: e.activation(rows_sb[:, 2 * LH:3 * LH],
                                               rows_sb[:, LH:2 * LH], AF.Sqrt),
              reads=[r_rows], writes=[r_rows], tag="lnsqr" + tag)
            E("vector", lambda e: e.reciprocal(rows_sb[:, LH:2 * LH],
                                               rows_sb[:, 2 * LH:3 * LH]),
              reads=[r_rows], writes=[r_rows], tag="lnrc" + tag)
            if mode == "xq":
                E("vector", lambda e: e.tensor_scalar_mul(
                    rows_sb[:, LH:2 * LH], rows_sb[:, LH:2 * LH], 1.0 / S_IN),
                  reads=[r_rows], writes=[r_rows], tag="lnsc" + tag)
            E("vector", lambda e: e.tensor_mul(
                rows_sb[:, 3 * LH:4 * LH], rows_sb[:, 0:LH],
                rows_sb[:, LH:2 * LH]),
              reads=[r_rows], writes=[r_rows], tag="lnmrs" + tag)
            E("tensor", lambda e: e.matmul(
                psA[0][:, 0:LH], onerow_f[:1, :], rows_sb[:1, LH:2 * LH],
                start=True, stop=True),
              reads=[r_rows, r_const], writes=[r_psA[0]], tag="lnb1" + tag)
            E("tensor", lambda e: e.matmul(
                psA[1][:, 0:LH], onerow_f[:1, :], rows_sb[:1, 3 * LH:4 * LH],
                start=True, stop=True),
              reads=[r_rows, r_const], writes=[r_psA[1]], tag="lnb2" + tag)
            for j in range(KT):
                E("vector", lambda e, j=j, sl=sl: e.scalar_tensor_tensor(
                    v_ev[0], x_sb[:, j, sl], 1.0, psA[0][:, 0:LH],
                    ALU.bypass, ALU.mult),
                  reads=[r_x[j], r_psA[0]], writes=[r_ev[0]], tag="lna1" + tag)
                if mode == "bf16":
                    E("vector", lambda e, j=j: e.tensor_sub(
                        xbhv[:, j, :], v_ev[0], psA[1][:, 0:LH]),
                      reads=[r_ev[0], r_psA[1]], writes=r_xbh, tag="lna2" + tag)
                else:
                    E("vector", lambda e, j=j: e.tensor_sub(
                        v_ev[1], v_ev[0], psA[1][:, 0:LH]),
                      reads=[r_ev[0], r_psA[1]], writes=[r_ev[1]], tag="lqa2" + tag)
                    E("vector", lambda e: e.tensor_copy(ti_b[0][:, 0:LH], v_ev[1]),
                      reads=[r_ev[1]], writes=[r_ti[0]], tag="lqc" + tag)
                    E("vector", lambda e: e.tensor_scalar(
                        xq_st[:, :], ti_b[0][:, 0:LH], -128.0, 127.0,
                        ALU.max, ALU.min),
                      reads=[r_ti[0]], writes=[r_xqst], tag="lqk" + tag)
                    E("sync", lambda e, j=j, hh=hh: e.dma_start(
                        out=d_xqin[j * 128:(j + 1) * 128, hh * LH:(hh + 1) * LH],
                        in_=xq_st[:, :]),
                      reads=[r_xqst], writes=[r_dxqin], queue="dma", tag="lqs" + tag)

        E("vector", lambda e: e.reduce_max(bw_sb[:, NB:NB + 1], bw_sb[:, 0:NB],
                                           axis=AX.X),
          reads=[r_bw], writes=[r_bw])
        E("vector", lambda e: e.scalar_tensor_tensor(
            bw_sb[:, 0:NB], bw_sb[:, 0:NB], bw_sb[:, NB:NB + 1], bw_sb[:, 0:NB],
            ALU.subtract, ALU.bypass),
          reads=[r_bw], writes=[r_bw])
        E("scalar", lambda e: e.activation(bw_sb[:, 0:NB], bw_sb[:, 0:NB], AF.Exp),
          reads=[r_bw], writes=[r_bw])
        E("vector", lambda e: e.reduce_sum(bw_sb[:, NB:NB + 1], bw_sb[:, 0:NB],
                                           axis=AX.X),
          reads=[r_bw], writes=[r_bw])
        E("vector", lambda e: e.reciprocal(bw_sb[:, NB + 1:NB + 2],
                                           bw_sb[:, NB:NB + 1]),
          reads=[r_bw], writes=[r_bw])
        E("vector", lambda e: e.scalar_tensor_tensor(
            bw_sb[:, 0:NB], bw_sb[:, 0:NB], bw_sb[:, NB + 1:NB + 2],
            bw_sb[:, 0:NB], ALU.mult, ALU.bypass),
          reads=[r_bw], writes=[r_bw])
        def actmm(e):
            last = None
            for k in range(KT):
                last = e.matmul(ps5[0][:1, 0:NB], onecol_b[:, 0:1], mn_b[:, k, :],
                                start=(k == 0), stop=(k == KT - 1))
            return last
        E("tensor", actmm, reads=[r_mn, r_const], writes=[r_ps5[0]])
        E("vector", lambda e: e.tensor_single_scalar(
            grow[:1, 0:NB], ps5[0][:1, 0:NB], EPSB, ALU.is_ge),
          reads=[r_ps5[0]], writes=[r_grow])
        E("vector", lambda e: e.tensor_copy(v_qst[0:KT, 0:NB], bw_sb[:, 0:NB]),
          reads=[r_bw], writes=r_qst)
        E("tensor", lambda e: e.matmul(ps5[1][0:NB, 0:KT], v_qst[0:KT, 0:NB],
                                       ident_b[0:KT, 0:KT], start=True, stop=True),
          reads=list(r_qst) + [r_const], writes=[r_ps5[1]])
        E("tensor", lambda e: e.matmul(ps5[2][0:NB, 0:1], grow[:1, 0:NB],
                                       onecol_f[0:1, 0:1], start=True, stop=True),
          reads=[r_grow, r_const], writes=[r_ps5[2]])
        E("vector", lambda e: e.tensor_copy(act_sb[:, :], ps5[2][0:NB, 0:1]),
          reads=[r_ps5[2]], writes=[r_act])
        E("vector", lambda e: e.tensor_copy(weT_sb[:, :], ps5[1][0:NB, 0:KT]),
          reads=[r_ps5[1]], writes=[r_weT])
        E("vector", lambda e: e.tensor_scalar_mul(weT_sb[:, :], weT_sb[:, :],
                                                  act_sb[:, 0:1]),
          reads=[r_weT, r_act], writes=[r_weT])

        for hh in range(T // LH):
            sl = slice(hh * LH, (hh + 1) * LH)
            layer_norm_half(hh, "bf16", "L1h%d" % hh)
            for h in range(KT):
                b = h % 2
                E("gpsimd", lambda e, h=h, b=b: e.dma_start(
                    out=wglv[b],
                    in_=wqT[:, h * 128:(h + 1) * 128]
                        .rearrange("(k p) o -> p k o", p=128)),
                  writes=r_wgl[b], queue="dma", tag="wqld")
                E("gpsimd", lambda e, h=h, b=b: e.dma_start(
                    out=memnv[b], in_=memi[:, h, :, :].rearrange("n p e -> p n e")),
                  writes=r_memn[b], queue="dma", tag="memld")

                def qmm(e, b=b):
                    last = None
                    for kk in range(KT):
                        last = e.matmul(psA[0][:, 0:LH], wglv[b][:, kk, :],
                                        xbhv[:, kk, :],
                                        start=(kk == 0), stop=(kk == KT - 1))
                    return last
                E("tensor", qmm, reads=list(r_wgl[b]) + list(r_xbh),
                  writes=[r_psA[0]], tag="qmm")
                E("vector", lambda e: e.tensor_scalar_min(v_ev[0], psA[0][:, 0:LH],
                                                          0.0),
                  reads=[r_psA[0]], writes=[r_ev[0]], tag="sg1")
                E("scalar", lambda e: e.activation(v_ev[1], v_ev[0], AF.Exp),
                  reads=[r_ev[0]], writes=[r_ev[1]], tag="sg2")
                E("vector", lambda e: e.scalar_tensor_tensor(
                    v_sig, psA[0][:, 0:LH], 0.0, v_ev[1], ALU.max, ALU.add),
                  reads=[r_psA[0], r_ev[1]], writes=r_sig, tag="sg3")

                def nmm(e, h=h):
                    return e.matmul(ps5[2][0:NB, 0:LH], mn_b[:, h, :], v_sig,
                                    start=True, stop=True)
                E("tensor", nmm, reads=[r_mn] + list(r_sig), writes=[r_ps5[2]],
                  tag="nmm")
                E("vector", lambda e: e.tensor_scalar_max(
                    v_ev[0][0:NB, :], ps5[2][0:NB, 0:LH], EPSB),
                  reads=[r_ps5[2]], writes=[r_ev[0]], tag="rc1")
                E("vector", lambda e: e.reciprocal(v_ev[1][0:NB, :],
                                                   v_ev[0][0:NB, :]),
                  reads=[r_ev[0]], writes=[r_ev[1]], tag="rc2")
                E("vector", lambda e, h=h: e.scalar_tensor_tensor(
                    v_rr[0:NB, :], v_ev[1][0:NB, :], weT_sb[:, h:h + 1],
                    v_ev[1][0:NB, :], ALU.mult, ALU.bypass),
                  reads=[r_ev[1], r_weT], writes=r_rr, tag="rc3")
                for nn in range(NB):
                    psa_ = ps5[0]
                    r_psa_ = r_ps5[0]

                    def amm(e, nn=nn, b=b, psa_=psa_):
                        return e.matmul(psa_[:, 0:LH], memnv[b][:, nn, :], v_sig,
                                        start=True, stop=True)
                    E("tensor", amm, reads=list(r_memn[b]) + list(r_sig),
                      writes=[r_psa_], tag="amm")

                    nb2 = 0
                    v_car, r_car = car_pair[nb2]
                    v_rbS, r_rbS = rbS_pair[nb2]
                    psb = ps5[1]
                    r_psb = r_ps5[1]

                    def rbc(e, nn=nn, psb=psb):
                        return e.matmul(psb[:, 0:LH],
                                        sel_sb[:, nn * 128:(nn + 1) * 128],
                                        v_rr[0:NB, :], start=True, stop=True)
                    E("tensor", rbc, reads=[r_sel] + list(r_rr),
                      writes=[r_psb], tag="rbc")
                    E("scalar", lambda e, v_rbS=v_rbS, psb=psb: e.copy(
                        v_rbS, psb[:, 0:LH]),
                      reads=[r_psb], writes=r_rbS, tag="rbev")
                    E("vector", lambda e, v_car=v_car, v_rbS=v_rbS, psa_=psa_: (
                        e.tensor_mul(v_car, psa_[:, 0:LH], v_rbS)),
                      reads=[r_psa_] + list(r_rbS), writes=r_car, tag="car")

                    def cacc_mm(e, nn=nn, v_car=v_car):
                        return e.matmul(psA[1][:, 0:LH], ident_b[:, :], v_car,
                                        start=(nn == 0), stop=(nn == NB - 1))
                    E("tensor", cacc_mm, reads=[r_const] + list(r_car),
                      writes=[r_psA[1]], tag="cacm")
                E("scalar", lambda e, h=h: e.copy(combhv[:, h, :], psA[1][:, 0:LH]),
                  reads=[r_psA[1]], writes=r_combh, tag="cev")
            for o in range(KT):
                b = o % 2
                E("gpsimd", lambda e, o=o, b=b: e.dma_start(
                    out=wglv[b],
                    in_=woT[:, o * 128:(o + 1) * 128]
                        .rearrange("(k p) o -> p k o", p=128)),
                  writes=r_wgl[b], queue="dma", tag="wold")

                def womm(e, b=b):
                    last = None
                    for kk in range(KT):
                        last = e.matmul(psA[0][:, 0:LH], wglv[b][:, kk, :],
                                        combhv[:, kk, :],
                                        start=(kk == 0), stop=(kk == KT - 1))
                    return last
                E("tensor", womm, reads=list(r_wgl[b]) + list(r_combh),
                  writes=[r_psA[0]], tag="womm")
                E("scalar", lambda e: e.copy(v_ev[0], psA[0][:, 0:LH]),
                  reads=[r_psA[0]], writes=[r_ev[0]], tag="woev")
                E("vector", lambda e, o=o, sl=sl: e.tensor_add(
                    x_sb[:, o, sl], x_sb[:, o, sl], v_ev[0]),
                  reads=[r_ev[0], r_x[o]], writes=[r_x[o]], tag="woadd")

        def ffn(tag):
            for hh in range(T // LH):
                sl = slice(hh * LH, (hh + 1) * LH)
                layer_norm_half(hh, "bf16", "Lf%s%d" % (tag, hh))
                for o in range(KT):
                    b = o % 2
                    rk = (o * 128) // OSL
                    c0 = (o * 128) % OSL
                    E("sync", lambda e, rk=rk, c0=c0, b=b: e.dma_start(
                        out=wglv[b],
                        in_=d_ffout.ap()[rk * DM:(rk + 1) * DM, c0:c0 + 128]
                            .rearrange("(k p) o -> p k o", p=128)),
                      reads=[r_dffout], writes=r_wgl[b], queue="dma",
                      tag="ffld" + tag)

                    def fmm(e, b=b):
                        last = None
                        for kk in range(KT):
                            last = e.matmul(psA[0][:, 0:LH], wglv[b][:, kk, :],
                                            xbhv[:, kk, :],
                                            start=(kk == 0), stop=(kk == KT - 1))
                        return last
                    E("tensor", fmm, reads=list(r_wgl[b]) + list(r_xbh),
                      writes=[r_psA[0]], tag="fmm" + tag)
                    E("scalar", lambda e, o=o: e.activation(
                        v_ev[0], psA[0][:, 0:LH], AF.Identity,
                        bias=bias_sb[:, o:o + 1]),
                      reads=[r_psA[0], r_bias], writes=[r_ev[0]], tag="fev" + tag)
                    E("vector", lambda e, o=o, sl=sl: e.tensor_add(
                        x_sb[:, o, sl], x_sb[:, o, sl], v_ev[0]),
                      reads=[r_ev[0], r_x[o]], writes=[r_x[o]], tag="fadd" + tag)

        ffn("1")

        v_wld2, r_wld2 = FA.view(5120, 5120 + 1792)
        _wpair = [(v_wld, r_wld), (v_wld2, r_wld2)]
        _wi = 0
        for i in range(KT):
            for c0 in range(0, 2 * DC, WLDW):
                c1 = min(c0 + WLDW, 2 * DC)
                vw, rw = _wpair[_wi % 2]
                _wi += 1
                E("sync", lambda e, i=i, c0=c0, c1=c1, vw=vw: e.dma_start(
                    out=vw[:, 0:c1 - c0], in_=ipT[i * 128:(i + 1) * 128, c0:c1]),
                  writes=rw, queue="dma")
                qround(rw, vw, r_wqip, wqipv[:, i, c0:c1],
                       8, c1 - c0, "ip")

        for hh in range(T // LH):
            layer_norm_half(hh, "xq", "L3h%d" % hh)
        E("gpsimd", lambda e: e.collective_compute(
            "AllGather", ALU.bypass, replica_groups=[list(range(NC))],
            ins=[d_xqin.ap().opt()], outs=[d_xqout.ap().opt()]),
          reads=[r_dxqin], writes=[r_dxqout], queue="cc")

        for half_o, o_range in ((0, range(2 * DT)),):
          for tb2 in range(2 * NTB):
            rk = (tb2 * TB2) // T
            t0 = (tb2 * TB2) % T
            rb = tb2 % 2
            E("gpsimd", lambda e, rk=rk, t0=t0, rb=rb: e.dma_start(
                out=rhsv2[rb],
                in_=d_xqout.ap()[rk * DM:(rk + 1) * DM, t0:t0 + TB2]
                    .rearrange("(k p) t -> p k t", p=128)),
              reads=[r_dxqout], writes=r_rhs2[rb], queue="dma", tag="xqld")
            for o in o_range:
                pst = ps5[o % 4]
                r_pst = r_ps5[o % 4]

                def imm(e, o=o, pst=pst, rb=rb):
                    last = None
                    for kk in range(KT):
                        last = e.matmul(pst[:, 0:TB2],
                                        wqipv[:, kk, o * 128:(o + 1) * 128],
                                        rhsv2[rb][:, kk, :],
                                        start=(kk == 0), stop=(kk == KT - 1))
                    return last
                E("tensor", imm, reads=list(r_wqip) + list(r_rhs2[rb]),
                  writes=[r_pst], tag="imm")
                eb = o % 2
                E("scalar", lambda e, pst=pst, eb=eb: e.activation(
                    v_ev[eb][:, 0:TB2], pst[:, 0:TB2], AF.Identity,
                    scale=sc_sb[:, 13:14]),
                  reads=[r_pst, r_sc], writes=[r_ev[eb]], tag="iev")
                E("vector", lambda e, eb=eb: e.tensor_copy(ti_b[eb][:, 0:TB2],
                                                           v_ev[eb][:, 0:TB2]),
                  reads=[r_ev[eb]], writes=[r_ti[eb]], tag="icast")
                if o < DT:
                    E("vector", lambda e, o=o, tb2=tb2, eb=eb: e.tensor_scalar(
                        hsv[:, o, tb2 * TB2:(tb2 + 1) * TB2], ti_b[eb][:, 0:TB2],
                        -128.0, 127.0, ALU.max, ALU.min),
                      reads=[r_ti[eb]], writes=r_hs, tag="ich")
                else:
                    E("vector", lambda e, eb=eb: e.tensor_scalar(
                        yq_st[eb][:, 0:TB2], ti_b[eb][:, 0:TB2],
                        -128.0, 127.0, ALU.max, ALU.min),
                      reads=[r_ti[eb]], writes=[r_yqst[eb]], tag="icg")
                    E("sync", lambda e, o=o, tb2=tb2, eb=eb: e.dma_start(
                        out=d_gate[(o - DT) * 128:(o - DT + 1) * 128,
                                   tb2 * TB2:(tb2 + 1) * TB2],
                        in_=yq_st[eb][:, 0:TB2]),
                      reads=[r_yqst[eb]], writes=[r_dgate], queue="dma", tag="gsp")

        def u_res(d, c0, c1):
            lo = (d * S + c0) // 512
            hi = (d * S + c1 + 511) // 512
            return r_u[lo:hi]

        def xproj_half(hf):
            for tb in range(hf * (NTB // 2), (hf + 1) * (NTB // 2)):
                xproj_tb(tb, hf)
            E("gpsimd", lambda e, hf=hf: e.collective_compute(
                "AllReduce", ALU.add, replica_groups=[list(range(NC))],
                ins=[d_ssin[hf].ap().opt()], outs=[d_ssout[hf].ap().opt()]),
              reads=[r_dssin[hf]], writes=[r_dssout[hf]], queue="cc")

        def xproj_tb(tb, hf):
            r_u_tb = sum((u_res(d, tb * TBW, (tb + 1) * TBW)
                          for d in range(DT)), [])
            E("gpsimd", lambda e, tb=tb: e.dma_start(
                out=ubxv, in_=uv[:, :, tb * TBW:(tb + 1) * TBW]),
              reads=r_u_tb, writes=r_ubx, queue="dma", tag="uxld")

            def xpm(e):
                last = None
                for kk in range(DT):
                    last = e.matmul(ps5[0][:, 0:TBW], xpq_b[:, kk, 0:128],
                                    ubxv[:, kk, :],
                                    start=(kk == 0), stop=(kk == DT - 1))
                for kk in range(DT):
                    last = e.matmul(ps5[1][0:2 * N, 0:TBW], xpq_b[:, kk, 128:RN2],
                                    ubxv[:, kk, :],
                                    start=(kk == 0), stop=(kk == DT - 1))
                return last
            E("tensor", xpm, reads=[r_xpq] + list(r_ubx),
              writes=[r_ps5[0], r_ps5[1]], tag="xpm")
            E("scalar", lambda e: e.activation(v_ev[0], ps5[0][:, 0:TBW],
                                               AF.Identity, scale=sc_sb[:, 14:15]),
              reads=[r_ps5[0], r_sc], writes=[r_ev[0]], tag="xpe0")
            E("scalar", lambda e: e.activation(v_ev[1][0:2 * N, :],
                                               ps5[1][0:2 * N, 0:TBW],
                                               AF.Identity,
                                               scale=sc_sb[0:2 * N, 14:15]),
              reads=[r_ps5[1], r_sc], writes=[r_ev[1]], tag="xpe1")
            c0 = tb * TBW - hf * (S // 2)
            E("sync", lambda e, c0=c0, hf=hf: e.dma_start(
                out=d_ssin[hf][0:128, c0:c0 + TBW], in_=v_ev[0]),
              reads=[r_ev[0]], writes=[r_dssin[hf]], queue="dma", tag="xps0")
            E("sync", lambda e, c0=c0, hf=hf: e.dma_start(
                out=d_ssin[hf][128:RN2, c0:c0 + TBW],
                in_=v_ev[1][0:2 * N, :]),
              reads=[r_ev[1]], writes=[r_dssin[hf]], queue="dma", tag="xps1")

        NTH = S // CW
        for th in range(NTH):
            for d in range(DT):
                sl0 = th * CW
                v_cacc, r_cacc = cacc_pair[(th * DT + d) % 2]
                E("vector", lambda e, v_cacc=v_cacc: e.memset(v_cacc, 0.0),
                  writes=r_cacc, tag="cv0")
                for kk in range(K):
                    sh = K - 1 - kk
                    lo = sh
                    if sl0 > 0 and sh > 0:
                        E("vector", lambda e, d=d, kk=kk, sh=sh, v_cacc=v_cacc: (
                            e.scalar_tensor_tensor(
                                v_cacc[:, 0:sh], halo[:, d, K - sh:K],
                                cw_eff[:, d, kk:kk + 1], v_cacc[:, 0:sh],
                                ALU.mult, ALU.add)),
                          reads=[r_halo, r_cw] + list(r_cacc), writes=r_cacc,
                          tag="cvh")
                    E("vector", lambda e, d=d, kk=kk, sh=sh, lo=lo, sl0=sl0,
                        v_cacc=v_cacc: (
                        e.scalar_tensor_tensor(
                            v_cacc[:, lo:CW],
                            hsv[:, d, sl0 + lo - sh:sl0 + CW - sh],
                            cw_eff[:, d, kk:kk + 1], v_cacc[:, lo:CW],
                            ALU.mult, ALU.add)),
                      reads=list(r_hs) + [r_cw] + list(r_cacc), writes=r_cacc,
                      tag="cv")
                E("vector", lambda e, d=d, sl0=sl0: e.tensor_copy(
                    halo[:, d, 1:K], hsv[:, d, sl0 + CW - (K - 1):sl0 + CW]),
                  reads=r_hs, writes=[r_halo], tag="cvsv")
                E("scalar", lambda e, d=d, v_cacc=v_cacc: e.activation(
                    v_cacc, v_cacc, AF.Silu, bias=cb_sb[:, d:d + 1]),
                  reads=list(r_cacc) + [r_cb], writes=r_cacc, tag="cvsl")
                for ci in range(CW // CH):
                    E("scalar", lambda e, ci=ci, v_cacc=v_cacc: e.activation(
                        ti_u[:, 0:CH], v_cacc[:, ci * CH:(ci + 1) * CH],
                        AF.Identity, scale=1.0 / S_U),
                      reads=r_cacc, writes=r_tiu, tag="us")
                    E("vector", lambda e, d=d, sl0=sl0, ci=ci: e.tensor_scalar(
                        uv[:, d, sl0 + ci * CH:sl0 + (ci + 1) * CH],
                        ti_u[:, 0:CH], -128.0, 127.0, ALU.max, ALU.min),
                      reads=r_tiu,
                      writes=u_res(d, sl0 + ci * CH, sl0 + (ci + 1) * CH),
                      tag="uk")
            if NTH >= 2 and th == NTH // 2 - 1:
                xproj_half(0)
        if NTH < 2:
            xproj_half(0)
        xproj_half(1)

        for tb in range(NCH):
            sl = slice(tb * CH, (tb + 1) * CH)
            hf = tb // (NCH // 2)
            lsl = slice(tb * CH - hf * (S // 2), (tb + 1) * CH - hf * (S // 2))
            p2 = tb % 2
            zh = p2 * CH
            vz, rz = v_z2[:, zh:zh + CH], [r_z2[p2]]
            vt, rt = v_tq[:, zh:zh + CH], [r_tq[p2]]
            tib, rti = ti_b[p2], [r_ti[p2]]
            vsq, rsq = v_h[p2], r_h[p2]
            vkb, rkb = v_h[2 + p2], r_h[2 + p2]
            psd, rpsd = psA[p2], [r_psA[p2]]
            rb = p2 * 2 * CH
            rw0 = rows_sb[:, rb:rb + CH]
            rw1 = rows_sb[:, rb + CH:rb + 2 * CH]
            rr = r_rows2[p2 * 2:p2 * 2 + 2]
            E("sync", lambda e, lsl=lsl, hf=hf, vz=vz: e.dma_start(
                out=vz, in_=d_ssout[hf].ap()[0:128, lsl]),
              reads=[r_dssout[hf]], writes=rz, queue="dma", tag="dtld")
            E("scalar", lambda e, vz=vz, vsq=vsq: e.activation(vsq, vz, AF.Square),
              reads=rz, writes=rsq, tag="dtsq")
            E("tensor", lambda e, vsq=vsq, psd=psd: e.matmul(
                psd[:1, 0:CH], onecol_b[:, 0:1], vsq, start=True, stop=True),
              reads=rsq + [r_const], writes=rpsd, tag="dtmm")
            E("vector", lambda e, psd=psd, rw0=rw0: e.tensor_scalar(
                rw0, psd[:1, 0:CH], 1.0 / R, EPSB, ALU.mult, ALU.add),
              reads=rpsd, writes=rr, tag="dtm")
            E("scalar", lambda e, rw0=rw0, rw1=rw1: e.activation(rw1, rw0, AF.Sqrt),
              reads=rr, writes=rr, tag="dts")
            E("vector", lambda e, rw0=rw0, rw1=rw1: e.reciprocal(rw0, rw1),
              reads=rr, writes=rr, tag="dtr")
            E("tensor", lambda e, psd=psd, rw0=rw0: e.matmul(
                psd[:, 0:CH], onerow_f[:1, :], rw0[:1, :], start=True, stop=True),
              reads=rr + [r_const], writes=rpsd, tag="dtbc")
            E("vector", lambda e, vz=vz, vt=vt, psd=psd: e.scalar_tensor_tensor(
                vt, vz, wdt_sb[:, 0:1], psd[:, 0:CH], ALU.mult, ALU.mult),
              reads=rz + [r_wdtc] + rpsd, writes=rt, tag="dtap")
            E("scalar", lambda e, vt=vt, tib=tib: e.activation(
                tib[:, 0:CH], vt, AF.Identity),
              reads=rt, writes=rti, tag="dtc")
            E("vector", lambda e, sl=sl, tib=tib: e.tensor_scalar(
                v_dtq[:, sl], tib[:, 0:CH], -128.0, 127.0, ALU.max, ALU.min),
              reads=rti, writes=r_dtq, tag="dtk")
            for bci, wcol in ((0, 0), (1, 1)):
                ro = 128 + bci * N
                E("sync", lambda e, lsl=lsl, ro=ro, hf=hf, vz=vz: e.dma_start(
                    out=vz[0:N, :], in_=d_ssout[hf].ap()[ro:ro + N, lsl]),
                  reads=[r_dssout[hf]], writes=rz, queue="dma", tag="bcld")
                E("scalar", lambda e, vz=vz, vsq=vsq: e.activation(
                    vsq[0:N, :], vz[0:N, :], AF.Square),
                  reads=rz, writes=rsq, tag="bcsq")
                E("tensor", lambda e, vsq=vsq, psd=psd: e.matmul(
                    psd[:1, 0:CH], onecol_b[0:N, 0:1], vsq[0:N, :],
                    start=True, stop=True),
                  reads=rsq + [r_const], writes=rpsd, tag="bcmm")
                E("vector", lambda e, psd=psd, rw0=rw0: e.tensor_scalar(
                    rw0, psd[:1, 0:CH], 1.0 / N, EPSB, ALU.mult, ALU.add),
                  reads=rpsd, writes=rr, tag="bcm")
                E("scalar", lambda e, rw0=rw0, rw1=rw1: e.activation(
                    rw1, rw0, AF.Sqrt),
                  reads=rr, writes=rr, tag="bcs")
                E("vector", lambda e, rw0=rw0, rw1=rw1: e.reciprocal(rw0, rw1),
                  reads=rr, writes=rr, tag="bcr")
                E("tensor", lambda e, psd=psd, rw0=rw0: e.matmul(
                    psd[0:N, 0:CH], onerow_f[:1, 0:N], rw0[:1, :],
                    start=True, stop=True),
                  reads=rr + [r_const], writes=rpsd, tag="bcbc")
                E("vector", lambda e, wcol=wcol, vz=vz, vt=vt, psd=psd: (
                    e.scalar_tensor_tensor(
                        vt[0:N, :], vz[0:N, :], wbc_sb[:, wcol:wcol + 1],
                        psd[0:N, 0:CH], ALU.mult, ALU.mult)),
                  reads=rz + [r_wbcc] + rpsd, writes=rt, tag="bcap")
                E("scalar", lambda e, vt=vt, tib=tib: e.activation(
                    tib[0:N, 0:CH], vt[0:N, :], AF.Identity),
                  reads=rt, writes=rti, tag="bcc")
                E("vector", lambda e, tib=tib, vkb=vkb: e.tensor_scalar(
                    vkb[0:N, :], tib[0:N, 0:CH], -128.0, 127.0,
                    ALU.max, ALU.min),
                  reads=rti, writes=rkb, tag="bck")
                E("sync", lambda e, ro2=bci * N, sl=sl, vkb=vkb: e.dma_start(
                    out=d_bc[ro2:ro2 + N, sl], in_=vkb[0:N, :]),
                  reads=rkb, writes=[r_dbc], queue="dma", tag="bcw")

        E("vector", lambda e: e.memset(st_sb[:, :], 0.0), writes=list(r_st))
        for cidx in range(NCH):
            sl = slice(cidx * CH, (cidx + 1) * CH)
            for d in range(DT):
                zh = (d % 2) * CH
                vz, rz = v_z2[:, zh:zh + CH], [r_z2[d % 2]]
                vt, rt = v_tq[:, zh:zh + CH], [r_tq[d % 2]]
                tib, rti = ti_b[d % 2], [r_ti[d % 2]]
                psd = psA[d % 2]
                rpsd = [r_psA[d % 2]]

                def d8m(e, d=d, sl=sl, psd=psd):
                    return e.matmul(psd[:, CH:2 * CH],
                                    dpq_b[:, d * 128:(d + 1) * 128],
                                    v_dtq[:, sl], start=True, stop=True)
                E("tensor", d8m, reads=[r_dpq] + list(r_dtq), writes=rpsd,
                  tag="d8m")
                E("scalar", lambda e, tib=tib, psd=psd: e.activation(
                    tib[:, 0:CH], psd[:, CH:2 * CH], AF.Identity,
                    scale=sc_sb[:, 15:16]),
                  reads=rpsd + [r_sc], writes=rti, tag="d8e")
                E("vector", lambda e, vt=vt, tib=tib: e.tensor_scalar(
                    vt, tib[:, 0:CH], -128.0, 127.0, ALU.max, ALU.min),
                  reads=rti, writes=rt, tag="d8k")
                E("scalar", lambda e, d=d, vz=vz, vt=vt: e.activation(
                    vz, vt, AF.Identity, scale=S_DT, bias=dtb_sb[:, d:d + 1]),
                  reads=rt + [r_dtb], writes=rz, tag="sp1")
                E("scalar", lambda e, vz=vz, vt=vt: e.activation(vt, vz, AF.Abs),
                  reads=rz, writes=rt, tag="sp2")
                E("scalar", lambda e, vt=vt: e.activation(vt, vt, AF.Exp,
                                                          scale=-1.0),
                  reads=rt, writes=rt, tag="sp3")
                E("scalar", lambda e, vt=vt: e.activation(vt, vt, AF.Ln,
                                                          bias=onecol_f[:, 0:1]),
                  reads=rt + [r_const], writes=rt, tag="sp4")
                E("vector", lambda e, d=d, vz=vz, vt=vt: e.scalar_tensor_tensor(
                    dtdv[:, d, :], vz, 0.0, vt, ALU.max, ALU.add),
                  reads=rz + rt, writes=[r_dtd[min(d, len(r_dtd) - 1)]],
                  tag="sp5")
                E("vector", lambda e, d=d, sl=sl: e.tensor_mul(
                    wdv[:, d, :], dtdv[:, d, :], uv[:, d, sl]),
                  reads=[r_dtd[min(d, len(r_dtd) - 1)]] + list(r_u),
                  writes=[r_wd[min(d, len(r_wd) - 1)]], tag="wmul")
            for nq in range(N // 4):
                db2 = nq % 2
                E("gpsimd", lambda e, nq=nq, sl=sl, db2=db2: e.dma_start(
                    out=bcBf[db2].rearrange("p (n t) -> p n t", n=4),
                    in_=d_bc.ap()[nq * 4:(nq + 1) * 4, sl].partition_broadcast(128)),
                  reads=[r_dbc], writes=r_bcB[db2], queue="dma", tag="bcBl")
                E("gpsimd", lambda e, nq=nq, sl=sl, db2=db2: e.dma_start(
                    out=bcCf[db2].rearrange("p (n t) -> p n t", n=4),
                    in_=d_bc.ap()[N + nq * 4:N + (nq + 1) * 4, sl]
                        .partition_broadcast(128)),
                  reads=[r_dbc], writes=r_bcC[db2], queue="dma", tag="bcCl")
                for d in range(DT):
                    rst = r_st[d % 8]
                    for n4 in range(4):
                        nn = nq * 4 + n4
                        E("scalar", lambda e, d=d, nn=nn, n4=n4: e.activation(
                            v_dA[n4], dtdv[:, d, :], AF.Exp,
                            scale=a_sb[:, d, nn:nn + 1]),
                          reads=[r_dtd[min(d, len(r_dtd) - 1)], r_a],
                          writes=[r_dA[n4]], tag="dA")
                        E("gpsimd", lambda e, d=d, n4=n4, db2=db2: (
                            e.tensor_mul(v_wB[n4], wdv[:, d, :],
                                         bcBf[db2][:, n4 * CH:(n4 + 1) * CH])),
                          reads=[r_wd[min(d, len(r_wd) - 1)]] + list(r_bcB[db2]),
                          writes=r_wB[n4], tag="wB")
                        E("vector", lambda e, d=d, nn=nn, n4=n4: (
                            e.tensor_tensor_scan(
                                v_h[n4], v_dA[n4], v_wB[n4],
                                st_sb[:, d * N + nn:d * N + nn + 1],
                                ALU.mult, ALU.add)),
                          reads=[r_dA[n4]] + list(r_wB[n4]) + [rst],
                          writes=list(r_h[n4]), tag="scan")
                    E("vector", lambda e, d=d, nq=nq: e.tensor_copy(
                        st_sb[:, d * N + nq * 4:d * N + nq * 4 + 4],
                        v_hset.rearrange("p (n t) -> p n t", n=4)[:, :, CH - 1]),
                      reads=list(r_hset), writes=[rst], tag="stsv")
                    E("vector", lambda e, db2=db2: e.tensor_mul(
                        v_chset, v_hset, bcCf[db2]),
                      reads=list(r_hset) + list(r_bcC[db2]), writes=list(r_chset),
                      tag="chm")

                    def yac(e, d=d, nq=nq):
                        last = None
                        for n4 in range(4):
                            last = e.matmul(ps5[d][:, 0:CH], ident_b[:, :],
                                            v_ch[n4],
                                            start=(nq == 0 and n4 == 0),
                                            stop=(nq == N // 4 - 1 and n4 == 3))
                        return last
                    E("tensor", yac, reads=[r_const] + list(r_chset),
                      writes=[r_ps5[d]], tag="yac")
            for d in range(DT):
                eb = d % 2
                c0 = sl.start
                E("gpsimd", lambda e, d=d, c0=c0, eb=eb: e.dma_start(
                    out=v_gl[eb],
                    in_=d_gate.ap()[d * 128:(d + 1) * 128, c0:c0 + CH]),
                  reads=[r_dgate], writes=r_gl[eb], queue="dma", tag="glr")
                E("scalar", lambda e, eb=eb: e.activation(
                    v_sil[eb], v_gl[eb], AF.Silu, scale=S_Z),
                  reads=list(r_gl[eb]), writes=[r_sil[eb]], tag="sil")
                E("vector", lambda e, d=d, eb=eb: e.tensor_scalar_mul(
                    v_ev[eb], ps5[d][:, 0:CH], S_U * S_B * S_C / S_Y),
                  reads=[r_ps5[d]], writes=[r_ev[eb]], tag="ysc")
                E("vector", lambda e, d=d, c0=c0, eb=eb: e.scalar_tensor_tensor(
                    v_ev[2 + eb], uv[:, d, c0:c0 + CH], dsk_sb[:, d:d + 1],
                    v_ev[eb], ALU.mult, ALU.add),
                  reads=list(r_u) + [r_dsk, r_ev[eb]], writes=[r_ev[2 + eb]],
                  tag="yskp")
                E("vector", lambda e, eb=eb: e.tensor_mul(
                    v_ev[eb], v_ev[2 + eb], v_sil[eb]),
                  reads=[r_ev[2 + eb], r_sil[eb]], writes=[r_ev[eb]], tag="ygat")
                E("vector", lambda e, eb=eb: e.tensor_copy(ti_b[eb][:, 0:CH],
                                                           v_ev[eb]),
                  reads=[r_ev[eb]], writes=[r_ti[eb]], tag="ycst")
                E("vector", lambda e, eb=eb, d=d: e.tensor_scalar(
                    yq_st4[d][:, 0:CH], ti_b[eb][:, 0:CH], -128.0, 127.0,
                    ALU.max, ALU.min),
                  reads=[r_ti[eb]], writes=[r_yqst4[d]], tag="yclp")
                blk = c0 // T
                haf = (c0 % T) // CH
                E("sync", lambda e, d=d, blk=blk, haf=haf: e.dma_start(
                    out=d_yqin[haf][blk * DC + d * 128:blk * DC + (d + 1) * 128,
                                    0:CH],
                    in_=yq_st4[d][:, 0:CH]),
                  reads=[r_yqst4[d]], writes=[r_dyqin[haf]], queue="dma",
                  tag="yqw")
            for haf in range(NHF):
                if cidx == NCH - NHF + haf:
                    with allow_ici_all_to_all():
                        E("gpsimd", lambda e, haf=haf: e.collective_compute(
                            "AllToAll", ALU.bypass,
                            replica_groups=[list(range(NC))],
                            ins=[d_yqin[haf].ap().opt()],
                            outs=[d_yqout[haf].ap().opt()]),
                          reads=[r_dyqin[haf]], writes=[r_dyqout[haf]],
                          queue="cc")
        for tc in range(T // CH):
            E("gpsimd", lambda e, tc=tc: e.dma_start(
                out=yrv,
                in_=d_yqout[tc].ap()[:, 0:CH]
                    .rearrange("(k p) t -> p k t", p=128)),
              reads=[r_dyqout[tc]], writes=r_yr, queue="dma", tag="yrld")
            for o in range(KT):
                b = o % 2
                E("sync", lambda e, o=o, b=b: e.dma_start(
                    out=oplv[b],
                    in_=d_opout.ap()[:, o * 128:(o + 1) * 128]
                        .rearrange("(k p) t -> p k t", p=128)),
                  reads=[r_dopout], writes=r_opl[b], queue="dma", tag="opld")

                def omm(e, b=b):
                    last = None
                    for kk in range(NKT):
                        last = e.matmul(psA[0][:, 0:CH], oplv[b][:, kk, :],
                                        yrv[:, kk, :],
                                        start=(kk == 0), stop=(kk == NKT - 1))
                    return last
                E("tensor", omm, reads=list(r_opl[b]) + list(r_yr),
                  writes=[r_psA[0]], tag="omm")
                E("scalar", lambda e: e.activation(v_ev[0], psA[0][:, 0:CH],
                                                   AF.Identity,
                                                   scale=sc_sb[:, 7:8]),
                  reads=[r_psA[0], r_sc], writes=[r_ev[0]], tag="oev")
                E("vector", lambda e, o=o, tc=tc: e.tensor_add(
                    x_sb[:, o, tc * CH:(tc + 1) * CH],
                    x_sb[:, o, tc * CH:(tc + 1) * CH], v_ev[0]),
                  reads=[r_ev[0], r_x[o]], writes=[r_x[o]], tag="oadd")

        ffn("2")

        for j in range(KT):
            E("sync", lambda e, j=j: e.dma_start(
                out=yout[j * 128:(j + 1) * 128, :], in_=x_sb[:, j, :]),
              reads=[r_x[j]], writes=[r_out], queue="dma")
        E("sync", lambda e: e.nop(), reads=[r_out])

        p.build(ctx, block)
    return nc


_NC_CACHE = {}


def get_nc(cfg_key):
    if cfg_key not in _NC_CACHE:
        _NC_CACHE[cfg_key] = build(make_cfg(*cfg_key))
    return _NC_CACHE[cfg_key]


def host_shard(inputs, cfg):
    S, DM, DIN, NC = cfg["S"], cfg["DM"], cfg["DIN"], cfg["NC"]
    T, DC, KT, DT = cfg["T"], cfg["DC"], cfg["KT"], cfg["DT"]
    NB, N, K = cfg["NB"], cfg["N"], cfg["K"]
    OSL = cfg["OSL"]
    f32 = np.float32

    def A(x):
        return np.ascontiguousarray(np.asarray(x), dtype=f32)

    xT = A(inputs["L0_attn_a_hidden_states"][0]).T
    WqT = np.ascontiguousarray(A(inputs["L0_attn_a_Wq"]).T)
    WoT = np.ascontiguousarray(A(inputs["L0_attn_a_Wo"]).T)
    mem = A(inputs["L0_attn_a_memories"])
    mnl = np.ascontiguousarray(
        A(inputs["L0_attn_a_memory_norms"]).transpose(2, 1, 0))
    bankw = A(inputs["L0_attn_a_bank_weights"])
    pwT = np.ascontiguousarray(A(inputs["L0_ffn_pweight"]).transpose(1, 2, 0))
    nwT = np.ascontiguousarray(A(inputs["L0_ffn_nweight"]).transpose(1, 2, 0))
    pbr = np.ascontiguousarray(
        A(inputs["L0_ffn_pbias"]).reshape(KT, 128, NB).transpose(1, 0, 2))
    nbr = np.ascontiguousarray(
        A(inputs["L0_ffn_nbias"]).reshape(KT, 128, NB).transpose(1, 0, 2))
    fexps = A(inputs["L0_ffn_exps"]).reshape(1, NB)
    fbexps = A(inputs["L0_ffn_bexps"]).reshape(1, NB)
    fmask = A(inputs["L0_ffn_mask_weight"]).reshape(1, NB)
    fscale = A(inputs["L0_ffn_scale"]).reshape(1, 1)
    fbsc = A(inputs["L0_ffn_biasscale"]).reshape(1, 1)
    ipT = A(inputs["L0_attn_b_in_proj_w"]).T
    convw = A(inputs["L0_attn_b_conv_w"])
    convb = A(inputs["L0_attn_b_conv_b"])
    xpTf = A(inputs["L0_attn_b_x_proj_w"]).T
    dtln = A(inputs["L0_attn_b_dt_ln_w"]).reshape(128, 1)
    blnv = A(inputs["L0_attn_b_B_ln_w"]).reshape(N, 1)
    clnv = A(inputs["L0_attn_b_C_ln_w"]).reshape(N, 1)
    dpTf = A(inputs["L0_attn_b_dt_proj_w"]).T
    dtb = A(inputs["L0_attn_b_dt_bias"])
    alog = A(inputs["L0_attn_b_A_log"])
    dskip = A(inputs["L0_attn_b_D_skip"])
    opTf = A(inputs["L0_attn_b_out_proj_w"]).T
    ident = np.eye(128).astype(BF)
    sel = np.zeros((NB, NB * 128), dtype=BF)
    for n in range(NB):
        sel[n, n * 128:(n + 1) * 128] = 1

    maps = []
    for r in range(NC):
        dsl = slice(r * DC, (r + 1) * DC)

        def dshape(v):
            return np.ascontiguousarray(v[dsl].reshape(DT, 128).T)

        def dshape2(v):
            return np.ascontiguousarray(
                v[dsl].reshape(DT, 128, -1).transpose(1, 0, 2))

        m = {
            "xsl": np.ascontiguousarray(xT[:, r * T:(r + 1) * T]),
            "wqT": WqT, "woT": WoT, "memi": mem, "mnl": mnl, "bankw": bankw,
            "pwT": np.ascontiguousarray(pwT[:, :, r * OSL:(r + 1) * OSL]),
            "nwT": np.ascontiguousarray(nwT[:, :, r * OSL:(r + 1) * OSL]),
            "pbr": pbr, "nbr": nbr, "fexps": fexps, "fbexps": fbexps,
            "fmask": fmask, "fscale": fscale, "fbsc": fbsc,
            "ipT": np.ascontiguousarray(
                np.concatenate([ipT[:, dsl], ipT[:, DIN + r * DC:DIN + (r + 1) * DC]],
                               axis=1)),
            "cwi": dshape2(convw), "cbi": dshape(convb),
            "xpT": dshape2(xpTf),
            "dtln": dtln, "blnv": blnv, "clnv": clnv,
            "dpT": np.ascontiguousarray(dpTf[:, dsl]),
            "dtbv": dshape(dtb),
            "alog": dshape2(alog),
            "dskip": dshape(dskip),
            "opT": np.ascontiguousarray(opTf[dsl, :]),
            "identi": ident,
            "seli": sel,
        }
        maps.append(m)
    return maps


def kernel(**inputs):
    cfg = make_cfg()
    nc = get_nc((cfg["S"], cfg["DM"], cfg["DIN"], cfg["NC"]))
    in_maps = host_shard(inputs, cfg)
    res = run_bass_kernel_spmd(nc, in_maps, core_ids=list(range(cfg["NC"])))
    parts = [np.asarray(res.results[r]["yout"]) for r in range(cfg["NC"])]
    full = np.concatenate(parts, axis=1)
    return np.ascontiguousarray(full.T)[None].astype(np.float32)


# version 35
# speedup vs baseline: 1.2128x; 1.0890x over previous
import os
import numpy as np
import ml_dtypes
from contextlib import ExitStack

import concourse.bass as bass
import concourse.mybir as mybir
from concourse.bass_utils import run_bass_kernel_spmd
from concourse.aot_env import allow_ici_all_to_all

os.environ.setdefault("TRN_ALLOW_ICI_ALL_TO_ALL", "1")

STRICT = False
RELAXED = ("tensor", "sync")

F32 = mybir.dt.float32
BF16 = mybir.dt.bfloat16
I8 = mybir.dt.int8
I32 = mybir.dt.int32
AF = mybir.ActivationFunctionType
ALU = mybir.AluOpType
AX = mybir.AxisListType
BF = ml_dtypes.bfloat16

S_IN, S_Z, S_U, S_DTLN = 0.05, 0.02, 0.01, 0.02
S_B, S_C, S_DT, S_Y = 0.02, 0.02, 0.01, 0.01
EPSB = 1e-6
EPSLN = 1e-5

ENGINES = ("tensor", "vector", "scalar", "gpsimd", "sync")


class Res:
    __slots__ = ("w", "readers", "name")

    def __init__(self, name=""):
        self.w = None
        self.readers = []
        self.name = name


class Op:
    __slots__ = ("eng", "fn", "deps", "queue")

    def __init__(self, eng, fn, deps, queue):
        self.eng = eng
        self.fn = fn
        self.deps = deps
        self.queue = queue


class Prog:
    def __init__(self, nc):
        self.nc = nc
        self.ops = []

    def res(self, name=""):
        return Res(name)

    def _expand(self, deps):
        out = set()
        stack = list(deps)
        while stack:
            d = stack.pop()
            if self.ops[d].fn is None:
                stack.extend(self.ops[d].deps)
            else:
                out.add(d)
        return out

    @staticmethod
    def _flat(rs):
        out = []
        stack = list(rs)
        while stack:
            r = stack.pop()
            if isinstance(r, Res):
                out.append(r)
            else:
                stack.extend(r)
        return out

    def emit(self, eng, fn, reads=(), writes=(), queue=None, tag=""):
        assert eng in ENGINES, eng
        reads = self._flat(reads)
        writes = self._flat(writes)
        deps = set()
        for r in reads:
            if r.w is not None:
                deps.add(r.w)
        for r in writes:
            if r.w is not None:
                deps.add(r.w)
            deps.update(r.readers)
        deps = self._expand(deps)
        oid = len(self.ops)
        self.ops.append(Op(eng, fn, deps, queue))
        for r in reads:
            r.readers.append(oid)
        for r in writes:
            r.w = oid
            r.readers = []
        return oid

    def build(self, ctx, block, dma_pool=10, cc_pool=3):
        nc = self.nc
        ops = self.ops
        n = len(ops)
        inc = [False] * n
        for op in ops:
            for d in op.deps:
                dop = ops[d]
                if (STRICT or dop.queue is not None or dop.eng != op.eng
                        or dop.eng not in RELAXED):
                    inc[d] = True
        sems = {}

        def get_sem(name):
            if name not in sems:
                sems[name] = ctx.enter_context(nc.semaphore(name))
            return sems[name]

        rr = {}
        total = {}
        assign = {}
        for i, op in enumerate(ops):
            if op.fn is None:
                continue
            if op.queue == "dma":
                inc[i] = True
                j = rr.get(op.eng, 0)
                rr[op.eng] = (j + 1) % dma_pool
                s = "d_%s%d" % (op.eng, j)
                prev = total.get(s, 0)
                total[s] = prev + 16
                assign[i] = (s, prev + 16, prev, 16)
            elif op.queue == "cc":
                inc[i] = True
                j = rr.get("cc", 0)
                rr["cc"] = (j + 1) % cc_pool
                s = "cc%d" % j
                prev = total.get(s, 0)
                total[s] = prev + 1
                assign[i] = (s, prev + 1, prev, 1)
            elif inc[i]:
                s = "c_%s" % op.eng
                prev = total.get(s, 0)
                total[s] = prev + 1
                assign[i] = (s, prev + 1, None, 1)

        lastw = {e: {} for e in ENGINES}
        plan = {e: [] for e in ENGINES}
        for i, op in enumerate(ops):
            waits = []

            def addw(s, v, eng=op.eng, waits=waits):
                if lastw[eng].get(s, 0) < v:
                    waits.append((s, v))
                    lastw[eng][s] = v

            for d in sorted(op.deps):
                dop = ops[d]
                if dop.fn is None:
                    continue
                if not (STRICT or dop.queue is not None or dop.eng != op.eng
                        or dop.eng not in RELAXED):
                    continue
                s, v, _, _ = assign[d]
                addw(s, v)
            if op.fn is not None and i in assign and assign[i][2] is not None:
                s, v, prev, _ = assign[i]
                if prev > 0:
                    addw(s, prev)
            myinc = assign.get(i) if (op.fn is not None and inc[i]) else None
            plan[op.eng].append((waits, op.fn, myinc))

        for i in sorted(assign):
            get_sem(assign[i][0])

        def make_body(eng):
            def body(e):
                for waits, fn, myinc in plan[eng]:
                    for s, v in waits:
                        e.wait_ge(sems[s], v)
                    if fn is None:
                        continue
                    inst = fn(e)
                    if myinc is not None:
                        s, v, prev, step = myinc
                        inst.then_inc(sems[s], step)
            return body

        for eng in ENGINES:
            if plan[eng]:
                getattr(block, eng)(make_body(eng))


class Arena:

    def __init__(self, prog, tensor, slot=512):
        self.t = tensor
        self.slot = slot
        self.W = tensor.shape[-1]
        nslots = (self.W + slot - 1) // slot
        self.res = [prog.res("%s[%d]" % (tensor.name, i)) for i in range(nslots)]

    def view(self, lo, hi):
        ap = self.t[:, lo:hi]
        rs = self.res[lo // self.slot:(hi + self.slot - 1) // self.slot]
        return ap, rs


def make_cfg(S=8192, DM=2048, DIN=4096, NC=8):
    c = dict(S=S, DM=DM, DIN=DIN, NC=NC)
    c["T"] = S // NC
    c["DC"] = DIN // NC
    c["KT"] = DM // 128
    c["DT"] = c["DC"] // 128
    c["NKT"] = DIN // 128
    c["NB"], c["N"], c["R"], c["K"] = 4, 16, 128, 4
    c["OSL"] = DM // NC
    c["CH"] = min(512, c["T"])
    c["LH"] = c["CH"]
    c["TBW"] = c["CH"]
    assert c["T"] % c["CH"] == 0 and S % c["CH"] == 0
    assert c["OSL"] % 128 == 0 and c["DT"] >= 1
    return c


def build(cfg):
    S, DM, DIN, NC = cfg["S"], cfg["DM"], cfg["DIN"], cfg["NC"]
    T, DC, KT, DT, NKT = cfg["T"], cfg["DC"], cfg["KT"], cfg["DT"], cfg["NKT"]
    NB, N, R, K = cfg["NB"], cfg["N"], cfg["R"], cfg["K"]
    OSL, CH, LH, TBW = cfg["OSL"], cfg["CH"], cfg["LH"], cfg["TBW"]
    RN2 = R + 2 * N
    NTB = S // TBW
    NCH = S // CH

    nc = bass.Bass("TRN2", target_bir_lowering=False, debug=False,
               detect_race_conditions=STRICT)

    def inp(name, shape, dt=F32):
        return nc.dram_tensor(name, shape, dt, kind="ExternalInput").ap()

    xsl = inp("xsl", [DM, T])
    wqT = inp("wqT", [DM, DM])
    woT = inp("woT", [DM, DM])
    memi = inp("memi", [NB, KT, 128, 128])
    mnl = inp("mnl", [128, KT, NB])
    bankw = inp("bankw", [KT, NB])
    pwT = inp("pwT", [DM, NB, OSL])
    nwT = inp("nwT", [DM, NB, OSL])
    pbr = inp("pbr", [128, KT, NB])
    nbr = inp("nbr", [128, KT, NB])
    fexps = inp("fexps", [1, NB])
    fbexps = inp("fbexps", [1, NB])
    fmask = inp("fmask", [1, NB])
    fscale = inp("fscale", [1, 1])
    fbsc = inp("fbsc", [1, 1])
    ipT = inp("ipT", [DM, 2 * DC])
    cwi = inp("cwi", [128, DT, K])
    cbi = inp("cbi", [128, DT])
    xpT = inp("xpT", [128, DT, RN2])
    dtln = inp("dtln", [128, 1])
    blnv = inp("blnv", [N, 1])
    clnv = inp("clnv", [N, 1])
    dpT = inp("dpT", [128, DC])
    dtbv = inp("dtbv", [128, DT])
    alog = inp("alog", [128, DT, N])
    dskip = inp("dskip", [128, DT])
    opT = inp("opT", [DC, DM])
    identi = inp("identi", [128, 128], BF16)
    seli = inp("seli", [4, 4 * 128], BF16)

    yout = nc.dram_tensor("yout", [DM, T], F32, kind="ExternalOutput").ap()

    d_xqin = nc.dram_tensor("d_xqin", [DM, T], I8)
    d_xqout = nc.dram_tensor("d_xqout", [NC * DM, T], I8)
    d_ffin = nc.dram_tensor("d_ffin", [DM, OSL], BF16)
    d_ffout = nc.dram_tensor("d_ffout", [NC * DM, OSL], BF16)
    d_ssin = [nc.dram_tensor("d_ssin%d" % i, [RN2, S // 2], F32) for i in range(2)]
    d_ssout = [nc.dram_tensor("d_ssout%d" % i, [RN2, S // 2], F32) for i in range(2)]
    NHF = T // cfg["CH"]
    d_yqin = [nc.dram_tensor("d_yqin%d" % i, [DIN, cfg["CH"]], I8)
              for i in range(NHF)]
    d_yqout = [nc.dram_tensor("d_yqout%d" % i, [DIN, cfg["CH"]], I8)
               for i in range(NHF)]
    d_opin = nc.dram_tensor("d_opin", [DC, DM], BF16)
    d_opout = nc.dram_tensor("d_opout", [DIN, DM], BF16)
    d_mxin = nc.dram_tensor("d_mxin", [8, 1], F32)
    d_mxout = nc.dram_tensor("d_mxout", [8, 1], F32)
    d_mxv = nc.dram_tensor("d_mxv", [128 * 8], F32)
    d_row8 = nc.dram_tensor("d_row8", [8], F32)
    d_gate = nc.dram_tensor("d_gate", [DC, S], I8)
    d_bc = nc.dram_tensor("d_bc", [2 * N, S], BF16)

    p = Prog(nc)
    E = p.emit
    RZ = p.res

    with ExitStack() as ctx:
        en = ctx.enter_context

        def sbt(name, shape, dt=F32):
            return en(nc.sbuf_tensor(name, shape, dt))

        x_sb = sbt("x_sb", [128, KT, T])
        fa_t = sbt("fa", [128, 13312])
        ba_t = sbt("ba", [128, 24576], BF16)
        ia_t = sbt("ia", [128, DT * S], I8)
        ti_b = [sbt("ti%d" % i, [128, CH], I32) for i in range(2)]
        yq_st = [sbt("yqst%d" % i, [128, CH], I8) for i in range(2)]
        yq_st4 = None
        xq_st = sbt("xq_st", [128, LH], I8)
        halo = sbt("halo", [128, DT, K])
        st_sb = sbt("st_sb", [128, DT * N])
        onerow_b = sbt("onerow_b", [1, 128], BF16)
        onerow_f = sbt("onerow_f", [1, 128])
        onecol_f = sbt("onecol_f", [128, 1])
        onecol_b = sbt("onecol_b", [128, 1], BF16)
        ident_b = sbt("ident_b", [128, 128], BF16)
        mxcol = sbt("mxcol", [128, 8])
        mxrow = sbt("mxrow", [8, 128])
        grow = sbt("grow", [1, 16])
        sc_sb = sbt("sc_sb", [128, 16])
        mn_b = sbt("mn_b", [128, KT, NB], BF16)
        bw_sb = sbt("bw_sb", [KT, NB + 4])
        weT_sb = sbt("weT_sb", [NB, KT])
        act_sb = sbt("act_sb", [NB, 1])
        cw_eff = sbt("cw_eff", [128, DT, K])
        cb_sb = sbt("cb_sb", [128, DT])
        xpq_b = sbt("xpq_b", [128, DT, RN2], BF16)
        dpq_b = sbt("dpq_b", [128, DC], BF16)
        dtb_sb = sbt("dtb_sb", [128, DT])
        a_sb = sbt("a_sb", [128, DT, N])
        dsk_sb = sbt("dsk_sb", [128, DT])
        wdt_sb = sbt("wdt_sb", [128, 1])
        wbc_sb = sbt("wbc_sb", [N, 2])
        cbn_sb = sbt("cbn_sb", [128, NB])
        bxc_sb = sbt("bxc_sb", [128, NB])
        bsc_sb = sbt("bsc_sb", [128, 1])
        bias_sb = sbt("bias_sb", [128, KT])
        fbq_sb = sbt("fbq_sb", [128, KT], I32)
        fb2_sb = sbt("fb2_sb", [128, KT + 8])

        ps5 = [en(nc.psum_tensor("ps5%d" % i, [128, CH])) for i in range(4)]
        psA = [en(nc.psum_tensor("psA%d" % i, [128, 2 * CH])) for i in range(2)]

        block = en(nc.Block())

        FA = Arena(p, fa_t)
        BA = Arena(p, ba_t)
        IA = Arena(p, ia_t)

        v_z2, r_z2 = FA.view(0, 1024)
        v_tq, r_tq = FA.view(1024, 2048)
        assert len(r_z2) == 2 and len(r_tq) == 2
        v_ev, r_ev = zip(*[FA.view(2048 + i * CH, 2048 + (i + 1) * CH)
                           for i in range(4)])
        v_sil, r_sil = zip(*[FA.view(4096 + i * CH, 4096 + (i + 1) * CH)
                             for i in range(2)])
        v_dtd, r_dtd = FA.view(5120, 5120 + DT * CH)
        v_wd, r_wd = FA.view(7168, 7168 + DT * CH)
        v_dA = [fa_t[:, 9216 + i * 512:9216 + i * 512 + (CH + 1) // 2]
                .bitcast(BF16) for i in range(4)]
        r_dA = [FA.view(9216 + i * 512, 9216 + i * 512 + (CH + 1) // 2)[1]
                for i in range(4)]
        v_rows2, r_rows2 = FA.view(10240, 12288)
        ti_u = fa_t[:, 12288:12288 + CH].bitcast(I32)
        _, r_tiu = FA.view(12288, 12288 + CH)
        rows_sb = fa_t[0:1, 10240:12288]
        v_pw, r_pw = FA.view(5120, 5120 + NB * OSL)
        v_nw, r_nw = FA.view(5120 + NB * OSL, 5120 + 2 * NB * OSL)
        v_soft, r_soft = FA.view(7168, 7168 + NB * OSL)
        v_facc, r_facc = FA.view(8192, 8192 + OSL)
        v_wld, r_wld = FA.view(8448, 8448 + 1792)
        CW = min(2048, S)
        cacc_pair = [FA.view(5120, 5120 + CW), FA.view(7168, 7168 + CW)]

        dtdv = v_dtd.rearrange("p (d c) -> p d c", d=DT)
        wdv = v_wd.rearrange("p (d c) -> p d c", d=DT)

        v_xbh, r_xbh = BA.view(0, KT * LH)
        xbhv = v_xbh.rearrange("p (k t) -> p k t", k=KT)
        v_combh, r_combh = BA.view(KT * LH, 2 * KT * LH)
        combhv = v_combh.rearrange("p (k t) -> p k t", k=KT)
        v_wgl, r_wgl = zip(*[BA.view(16384 + i * 2048, 16384 + (i + 1) * 2048)
                             for i in range(2)])
        wglv = [v[:, 0:KT * 128].rearrange("p (k o) -> p k o", k=KT) for v in v_wgl]
        v_memn, r_memn = zip(*[BA.view(20480 + i * NB * 128,
                                       20480 + (i + 1) * NB * 128)
                               for i in range(2)])
        memnv = [v.rearrange("p (n e) -> p n e", n=NB) for v in v_memn]
        v_sig, r_sig = BA.view(21504, 21504 + LH)
        v_rr, r_rr = BA.view(22016, 22016 + LH)
        v_qst, r_qst = BA.view(16384, 16384 + 2048)
        car_pair = [BA.view(22528, 22528 + CH),
                    (ia_t[:, 0:2 * CH].bitcast(BF16), IA.view(0, 2 * CH)[1])]
        v_lnx, r_lnx = BA.view(23040, 23040 + CH)
        v_lnsq, r_lnsq = BA.view(23552, 23552 + CH)
        rbS_pair = [BA.view(24064, 24064 + CH),
                    (ia_t[:, 2 * CH:4 * CH].bitcast(BF16),
                     IA.view(2 * CH, 4 * CH)[1])]
        sel_sb = sbt("sel_sb", [4, 4 * 128], BF16)
        v_fst, r_fst = BA.view(18432, 18432 + OSL)
        v_dtq, r_dtq = BA.view(0, S)
        v_wqip, r_wqip = BA.view(0, KT * 1024)
        wqipv = v_wqip.rearrange("p (k o) -> p k o", k=KT)
        TB2 = TBW // 2
        rhs_pair = [BA.view(16384, 16384 + KT * TB2),
                    BA.view(16384 + KT * TB2, 16384 + 2 * KT * TB2)]
        rhsv2 = [v.rearrange("p (k t) -> p k t", k=KT) for v, _ in rhs_pair]
        r_rhs2 = [r for _, r in rhs_pair]
        bcB_pair = [BA.view(8192 + i * 8 * CH, 8192 + i * 8 * CH + 4 * CH)
                    for i in range(2)]
        bcC_pair = [BA.view(8192 + i * 8 * CH + 4 * CH, 8192 + (i + 1) * 8 * CH)
                    for i in range(2)]
        bcBf = [v for v, _ in bcB_pair]
        bcCf = [v for v, _ in bcC_pair]
        r_bcB = [r for _, r in bcB_pair]
        r_bcC = [r for _, r in bcC_pair]
        v_wBset, r_wBset = BA.view(16384, 16384 + 4 * CH)
        v_wB = [v_wBset[:, i * CH:(i + 1) * CH] for i in range(4)]
        r_wB = [r_wBset[min(i, len(r_wBset) - 1):min(i, len(r_wBset) - 1) + 1]
               for i in range(4)]
        v_hset, r_hset = BA.view(20480, 20480 + 4 * CH)
        v_h = [v_hset[:, i * CH:(i + 1) * CH] for i in range(4)]
        r_h = [r_hset[min(i, len(r_hset) - 1):min(i, len(r_hset) - 1) + 1]
               for i in range(4)]
        v_chset, r_chset = BA.view(22528, 22528 + 4 * CH)
        v_ch = [v_chset[:, i * CH:(i + 1) * CH] for i in range(4)]
        r_ch = [r_chset[min(i, len(r_chset) - 1):min(i, len(r_chset) - 1) + 1]
               for i in range(4)]
        v_gl, r_gl = zip(*[BA.view(16384 + (6 + i) * CH, 16384 + (7 + i) * CH)
                           for i in range(2)])
        v_yr, r_yr = BA.view(0, NKT * CH)
        yrv = v_yr.rearrange("p (k t) -> p k t", k=NKT)
        v_opl, r_opl = zip(*[BA.view(16384 + i * NKT * 128,
                                     16384 + (i + 1) * NKT * 128)
                             for i in range(2)])
        oplv = [v.rearrange("p (k o) -> p k o", k=NKT) for v in v_opl]
        v_ubx, r_ubx = BA.view(8192, 8192 + DT * TBW)
        ubxv = v_ubx.rearrange("p (d t) -> p d t", d=DT)
        _yqb = [BA.view(18432 + i * 512, 18432 + i * 512 + (CH + 1) // 2)
                for i in range(2)]
        yq_st4 = [yq_st[0], yq_st[1],
                  _yqb[0][0].bitcast(I8), _yqb[1][0].bitcast(I8)]

        v_hs, r_hs = IA.view(0, DT * S)
        hsv = v_hs.rearrange("p (d t) -> p d t", d=DT)
        uv = hsv
        r_u = r_hs

        r_x = [RZ("x%d" % j) for j in range(KT)]
        r_ti = [RZ("ti%d" % i) for i in range(2)]
        r_yqst = [RZ("yqst%d" % i) for i in range(2)]
        r_yqst4 = [r_yqst[0], r_yqst[1], _yqb[0][1], _yqb[1][1]]
        r_xqst = RZ("xqst")
        r_halo = RZ("halo")
        r_st = [RZ("st%d" % i) for i in range(8)]
        r_const = RZ("const")
        r_mxcol, r_mxrow, r_grow, r_sc = RZ("mxc"), RZ("mxr"), RZ("grow"), RZ("sc")
        r_mn, r_bw, r_weT, r_act = RZ("mn"), RZ("bw"), RZ("weT"), RZ("act")
        r_cw, r_cb, r_xpq, r_dpq = RZ("cw"), RZ("cb"), RZ("xpq"), RZ("dpq")
        r_dtb, r_a, r_dsk, r_wdtc, r_wbcc = RZ("dtb"), RZ("a"), RZ("dsk"), RZ("wdt"), RZ("wbc")
        r_cbn, r_bxc, r_bsc, r_bias, r_fbq, r_fb2 = (RZ("cbn"), RZ("bxc"),
                                                     RZ("bsc"), RZ("bias"),
                                                     RZ("fbq"), RZ("fb2"))
        r_rows = r_rows2
        r_ps5 = [RZ("ps5%d" % i) for i in range(4)]
        r_psA = [RZ("psA%d" % i) for i in range(2)]
        r_dxqin, r_dxqout = RZ("dxqin"), RZ("dxqout")
        r_dffin, r_dffout = RZ("dffin"), RZ("dffout")
        r_dssin = [RZ("dssin%d" % i) for i in range(2)]
        r_dssout = [RZ("dssout%d" % i) for i in range(2)]
        r_dyqin = [RZ("dyqin%d" % i) for i in range(NHF)]
        r_dyqout = [RZ("dyqout%d" % i) for i in range(NHF)]
        r_dopin, r_dopout = RZ("dopin"), RZ("dopout")
        r_dmxin, r_dmxout, r_dmxv, r_drow8 = RZ("dmxi"), RZ("dmxo"), RZ("dmxv"), RZ("dr8")
        r_dgate, r_dbc = RZ("dgate"), RZ("dbc")
        r_sel = RZ("sel")
        r_out = RZ("out")

        LRS = list(r_ev) + list(r_sil) + [r_z2, r_tq]

        E("vector", lambda e: e.memset(onerow_b[:, :], 1.0), writes=[r_const])
        E("vector", lambda e: e.memset(onerow_f[:, :], 1.0), writes=[r_const])
        E("vector", lambda e: e.memset(onecol_f[:, :], 1.0), writes=[r_const])
        E("vector", lambda e: e.memset(onecol_b[:, :], 1.0), writes=[r_const])
        E("sync", lambda e: e.dma_start(out=ident_b[:, :], in_=identi),
          writes=[r_const], queue="dma")
        E("sync", lambda e: e.dma_start(out=sel_sb[:, :], in_=seli),
          writes=[r_sel], queue="dma")
        E("sync", lambda e: e.dma_start(out=bw_sb[:, 0:NB], in_=bankw),
          writes=[r_bw], queue="dma")
        E("gpsimd", lambda e: e.dma_start(out=mn_b[:, :, :], in_=mnl),
          writes=[r_mn], queue="dma")
        E("sync", lambda e: e.dma_start(out=cw_eff[:, :, :], in_=cwi),
          writes=[r_cw], queue="dma")
        E("sync", lambda e: e.dma_start(out=cb_sb[:, :], in_=cbi),
          writes=[r_cb], queue="dma")
        E("sync", lambda e: e.dma_start(out=wdt_sb[:, :], in_=dtln),
          writes=[r_wdtc], queue="dma")
        E("sync", lambda e: e.dma_start(out=wbc_sb[:, 0:1], in_=blnv),
          writes=[r_wbcc], queue="dma")
        E("sync", lambda e: e.dma_start(out=wbc_sb[:, 1:2], in_=clnv),
          writes=[r_wbcc], queue="dma")
        E("sync", lambda e: e.dma_start(out=dtb_sb[:, :], in_=dtbv),
          writes=[r_dtb], queue="dma")
        E("sync", lambda e: e.dma_start(out=a_sb[:, :, :], in_=alog),
          writes=[r_a], queue="dma")
        E("sync", lambda e: e.dma_start(out=dsk_sb[:, :], in_=dskip),
          writes=[r_dsk], queue="dma")
        E("scalar", lambda e: e.activation(a_sb[:, :, :], a_sb[:, :, :], AF.Exp),
          reads=[r_a], writes=[r_a])
        E("vector", lambda e: e.tensor_scalar_mul(a_sb[:, :, :], a_sb[:, :, :], -1.0),
          reads=[r_a], writes=[r_a])
        E("vector", lambda e: e.tensor_scalar_mul(wdt_sb[:, :], wdt_sb[:, :],
                                                  1.0 / S_DTLN),
          reads=[r_wdtc], writes=[r_wdtc])
        E("vector", lambda e: e.tensor_scalar_mul(wbc_sb[:, 0:1], wbc_sb[:, 0:1],
                                                  1.0 / S_B),
          reads=[r_wbcc], writes=[r_wbcc])
        E("vector", lambda e: e.tensor_scalar_mul(wbc_sb[:, 1:2], wbc_sb[:, 1:2],
                                                  1.0 / S_C),
          reads=[r_wbcc], writes=[r_wbcc])
        for j in range(KT):
            E("sync", lambda e, j=j: e.dma_start(out=x_sb[:, j, :],
                                                 in_=xsl[j * 128:(j + 1) * 128, :]),
              writes=[r_x[j]], queue="dma")

        WLDW = 1792
        E("vector", lambda e: e.memset(mxcol[:, :], 0.0), writes=[r_mxcol])

        def absmax_into(col, src_res, src_ap, first):
            if first:
                E("vector", lambda e: e.reduce_max(mxcol[:, col:col + 1], src_ap,
                                                   axis=AX.X,
                                                   apply_absolute_value=True),
                  reads=src_res, writes=[r_mxcol])
            else:
                E("vector", lambda e: e.reduce_max(mxcol[:, 7:8], src_ap, axis=AX.X,
                                                   apply_absolute_value=True),
                  reads=src_res, writes=[r_mxcol])
                E("vector", lambda e: e.tensor_max(mxcol[:, col:col + 1],
                                                   mxcol[:, col:col + 1],
                                                   mxcol[:, 7:8]),
                  reads=[r_mxcol], writes=[r_mxcol])

        first = True
        for i in range(KT):
            for c0 in range(0, 2 * DC, WLDW):
                c1 = min(c0 + WLDW, 2 * DC)
                E("sync", lambda e, i=i, c0=c0, c1=c1: e.dma_start(
                    out=v_wld[:, 0:c1 - c0], in_=ipT[i * 128:(i + 1) * 128, c0:c1]),
                  writes=r_wld, queue="dma")
                absmax_into(0, r_wld, v_wld[:, 0:c1 - c0], first)
                first = False
        absmax_into(1, [r_cw], cw_eff.ap().rearrange("p a b -> p (a b)"), True)
        E("sync", lambda e: e.dma_start(out=v_wld[:, 0:DT * RN2], in_=xpT),
          writes=r_wld, queue="dma")
        absmax_into(2, r_wld, v_wld[:, 0:DT * RN2], True)
        E("vector", lambda e: e.tensor_copy(v_rows2[:, 0:DT * RN2],
                                            v_wld[:, 0:DT * RN2]),
          reads=r_wld, writes=r_rows2)
        E("sync", lambda e: e.dma_start(out=v_wld[:, 0:DC], in_=dpT),
          writes=r_wld, queue="dma")
        absmax_into(3, r_wld, v_wld[:, 0:DC], True)
        E("vector", lambda e: e.tensor_copy(v_z2[:, 0:DC], v_wld[:, 0:DC]),
          reads=r_wld, writes=r_z2)
        first = True
        for i in range(DT):
            for c0 in range(0, DM, WLDW):
                c1 = min(c0 + WLDW, DM)
                E("sync", lambda e, i=i, c0=c0, c1=c1: e.dma_start(
                    out=v_wld[:, 0:c1 - c0], in_=opT[i * 128:(i + 1) * 128, c0:c1]),
                  writes=r_wld, queue="dma")
                absmax_into(4, r_wld, v_wld[:, 0:c1 - c0], first)
                first = False
        E("sync", lambda e: e.dma_start(out=d_mxv[:], in_=mxcol[:, 0:8]),
          reads=[r_mxcol], writes=[r_dmxv], queue="dma")
        def mxv_rd(e):
            with nc.allow_non_contiguous_dma(reason="tiny 4KB transpose read"):
                return e.dma_start(out=mxrow[:, :],
                                   in_=d_mxv.ap().rearrange("(p c) -> c p", c=8))
        E("sync", mxv_rd, reads=[r_dmxv], writes=[r_mxrow], queue="dma")
        E("vector", lambda e: e.reduce_max(fb2_sb[0:8, KT:KT + 1], mxrow[:, :],
                                           axis=AX.X),
          reads=[r_mxrow], writes=[r_fb2])
        E("sync", lambda e: e.dma_start(out=d_mxin[:, :], in_=fb2_sb[0:8, KT:KT + 1]),
          reads=[r_fb2], writes=[r_dmxin], queue="dma")
        E("gpsimd", lambda e: e.collective_compute(
            "AllReduce", ALU.max, replica_groups=[list(range(NC))],
            ins=[d_mxin.ap().opt()], outs=[d_mxout.ap().opt()]),
          reads=[r_dmxin], writes=[r_dmxout], queue="cc")
        E("sync", lambda e: e.dma_start(out=d_row8[:], in_=d_mxout[:, :]),
          reads=[r_dmxout], writes=[r_drow8], queue="dma")
        E("sync", lambda e: e.dma_start(
            out=grow[:, 0:8], in_=d_row8.ap().rearrange("(a b) -> a b", a=1)),
          reads=[r_drow8], writes=[r_grow], queue="dma")
        E("vector", lambda e: e.reciprocal(grow[:, 8:13], grow[:, 0:5]),
          reads=[r_grow], writes=[r_grow])
        E("vector", lambda e: e.tensor_scalar_mul(grow[:, 8:13], grow[:, 8:13],
                                                  127.0),
          reads=[r_grow], writes=[r_grow])
        E("vector", lambda e: e.tensor_scalar_mul(grow[:, 13:14], grow[:, 0:1],
                                                  S_IN / (127.0 * S_Z)),
          reads=[r_grow], writes=[r_grow])
        E("vector", lambda e: e.tensor_scalar_mul(grow[:, 14:15], grow[:, 2:3],
                                                  S_U / 127.0),
          reads=[r_grow], writes=[r_grow])
        E("vector", lambda e: e.tensor_scalar_mul(grow[:, 15:16], grow[:, 3:4],
                                                  S_DTLN / (127.0 * S_DT)),
          reads=[r_grow], writes=[r_grow])
        E("vector", lambda e: e.tensor_scalar_mul(grow[:, 7:8], grow[:, 4:5],
                                                  S_Y / 127.0),
          reads=[r_grow], writes=[r_grow])
        E("vector", lambda e: e.tensor_scalar_mul(grow[:, 6:7], grow[:, 1:2],
                                                  S_Z / 127.0),
          reads=[r_grow], writes=[r_grow])
        E("tensor", lambda e: e.matmul(ps5[0][:, 0:16], onerow_f[:1, :],
                                       grow[:1, :], start=True, stop=True),
          reads=[r_const, r_grow], writes=[r_ps5[0]])
        E("vector", lambda e: e.tensor_copy(sc_sb[:, :], ps5[0][:, 0:16]),
          reads=[r_ps5[0]], writes=[r_sc])

        def qround(src_res, src_ap, dst_res, dst_ap, qcol, Wd, tag=""):
            for c0 in range(0, Wd, CH):
                c1 = min(c0 + CH, Wd)
                E("vector", lambda e, c0=c0, c1=c1: e.scalar_tensor_tensor(
                    v_ev[3][:, 0:c1 - c0], src_ap[:, c0:c1],
                    sc_sb[:, qcol:qcol + 1], src_ap[:, c0:c1],
                    ALU.mult, ALU.bypass),
                  reads=list(src_res) + [r_sc], writes=[r_ev[3]], tag="q1" + tag)
                E("vector", lambda e, c0=c0, c1=c1: e.tensor_copy(
                    ti_b[1][:, 0:c1 - c0], v_ev[3][:, 0:c1 - c0]),
                  reads=[r_ev[3]], writes=[r_ti[1]], tag="q2" + tag)
                E("vector", lambda e, c0=c0, c1=c1: e.tensor_scalar(
                    dst_ap[:, c0:c1], ti_b[1][:, 0:c1 - c0], -128.0, 127.0,
                    ALU.max, ALU.min),
                  reads=[r_ti[1]], writes=list(dst_res), tag="q3" + tag)

        qround(r_rows2, v_rows2, [r_xpq],
               xpq_b.ap().rearrange("p a b -> p (a b)"), 10, DT * RN2, "xp")
        qround(r_z2, v_z2, [r_dpq], dpq_b, 11, DC, "dp")
        E("vector", lambda e: e.scalar_tensor_tensor(
            v_ev[3][:, 0:DT * K], cw_eff.ap().rearrange("p a b -> p (a b)"),
            sc_sb[:, 9:10], cw_eff.ap().rearrange("p a b -> p (a b)"),
            ALU.mult, ALU.bypass),
          reads=[r_cw, r_sc], writes=[r_ev[3]])
        E("vector", lambda e: e.tensor_copy(ti_b[1][:, 0:DT * K],
                                            v_ev[3][:, 0:DT * K]),
          reads=[r_ev[3]], writes=[r_ti[1]])
        E("vector", lambda e: e.tensor_scalar(v_ev[3][:, 0:DT * K],
                                              ti_b[1][:, 0:DT * K],
                                              -128.0, 127.0, ALU.max, ALU.min),
          reads=[r_ti[1]], writes=[r_ev[3]])
        E("vector", lambda e: e.scalar_tensor_tensor(
            cw_eff.ap().rearrange("p a b -> p (a b)"), v_ev[3][:, 0:DT * K],
            sc_sb[:, 6:7], v_ev[3][:, 0:DT * K], ALU.mult, ALU.bypass),
          reads=[r_ev[3], r_sc], writes=[r_cw])
        for i in range(DT):
            for c0 in range(0, DM, WLDW):
                c1 = min(c0 + WLDW, DM)
                E("sync", lambda e, i=i, c0=c0, c1=c1: e.dma_start(
                    out=v_wld[:, 0:c1 - c0], in_=opT[i * 128:(i + 1) * 128, c0:c1]),
                  writes=r_wld, queue="dma")
                qround(r_wld, v_wld, r_qst, v_qst, 12, c1 - c0, "op")
                E("sync", lambda e, i=i, c0=c0, c1=c1: e.dma_start(
                    out=d_opin[i * 128:(i + 1) * 128, c0:c1],
                    in_=v_qst[:, 0:c1 - c0]),
                  reads=r_qst, writes=[r_dopin], queue="dma")
        E("gpsimd", lambda e: e.collective_compute(
            "AllGather", ALU.bypass, replica_groups=[list(range(NC))],
            ins=[d_opin.ap().opt()], outs=[d_opout.ap().opt()]),
          reads=[r_dopin], writes=[r_dopout], queue="cc")

        E("sync", lambda e: e.dma_start(out=grow[:, 8:12], in_=fmask),
          reads=[r_grow], writes=[r_grow], queue="dma")
        E("sync", lambda e: e.dma_start(out=grow[:, 12:16], in_=fexps),
          reads=[r_grow], writes=[r_grow], queue="dma")
        E("sync", lambda e: e.dma_start(out=grow[:, 4:5], in_=fscale),
          reads=[r_grow], writes=[r_grow], queue="dma")
        E("scalar", lambda e: e.activation(grow[:, 8:12], grow[:, 8:12], AF.Sigmoid),
          reads=[r_grow], writes=[r_grow])
        E("vector", lambda e: e.tensor_mul(grow[:, 8:12], grow[:, 8:12],
                                           grow[:, 12:16]),
          reads=[r_grow], writes=[r_grow])
        E("vector", lambda e: e.scalar_tensor_tensor(
            grow[:, 8:12], grow[:, 8:12], grow[:1, 4:5], grow[:, 8:12],
            ALU.mult, ALU.bypass),
          reads=[r_grow], writes=[r_grow])
        E("tensor", lambda e: e.matmul(ps5[1][:, 0:4], onerow_f[:1, :],
                                       grow[:1, 8:12], start=True, stop=True),
          reads=[r_const, r_grow], writes=[r_ps5[1]])
        E("vector", lambda e: e.tensor_copy(cbn_sb[:, :], ps5[1][:, 0:4]),
          reads=[r_ps5[1]], writes=[r_cbn])
        sview = v_soft[:, 0:NB * OSL].rearrange("p (n o) -> p n o", n=NB)
        for i in range(KT):
            E("sync", lambda e, i=i: e.dma_start(
                out=v_pw, in_=pwT[i * 128:(i + 1) * 128, :, :]),
              writes=r_pw, queue="dma")
            E("scalar", lambda e, i=i: e.dma_start(
                out=v_nw, in_=nwT[i * 128:(i + 1) * 128, :, :]),
              writes=r_nw, queue="dma")
            E("scalar", lambda e: e.activation(v_pw, v_pw, AF.Sigmoid),
              reads=r_pw, writes=r_pw)
            E("scalar", lambda e: e.activation(v_nw, v_nw, AF.Sigmoid),
              reads=r_nw, writes=r_nw)
            E("vector", lambda e: e.tensor_sub(v_soft, v_pw, v_nw),
              reads=list(r_pw) + list(r_nw), writes=r_soft)
            E("vector", lambda e: e.scalar_tensor_tensor(
                v_facc, sview[:, 0, :], cbn_sb[:, 0:1], sview[:, 0, :],
                ALU.mult, ALU.bypass),
              reads=list(r_soft) + [r_cbn], writes=r_facc)
            for nn in range(1, NB):
                E("vector", lambda e, nn=nn: e.scalar_tensor_tensor(
                    v_facc, sview[:, nn, :], cbn_sb[:, nn:nn + 1], v_facc,
                    ALU.mult, ALU.add),
                  reads=list(r_soft) + [r_cbn] + list(r_facc), writes=r_facc)
            E("vector", lambda e: e.tensor_copy(v_fst, v_facc),
              reads=r_facc, writes=r_fst)
            E("sync", lambda e, i=i: e.dma_start(
                out=d_ffin[i * 128:(i + 1) * 128, :], in_=v_fst),
              reads=r_fst, writes=[r_dffin], queue="dma")
        E("gpsimd", lambda e: e.collective_compute(
            "AllGather", ALU.bypass, replica_groups=[list(range(NC))],
            ins=[d_ffin.ap().opt()], outs=[d_ffout.ap().opt()]),
          reads=[r_dffin], writes=[r_dffout], queue="cc")
        E("sync", lambda e: e.dma_start(out=v_z2[:, 0:KT * NB], in_=pbr),
          writes=r_z2, queue="dma")
        E("sync", lambda e: e.dma_start(out=v_tq[:, 0:KT * NB], in_=nbr),
          writes=r_tq, queue="dma")
        E("vector", lambda e: e.tensor_sub(v_z2[:, 0:KT * NB], v_z2[:, 0:KT * NB],
                                           v_tq[:, 0:KT * NB]),
          reads=list(r_z2) + list(r_tq), writes=r_z2)
        E("sync", lambda e: e.dma_start(out=grow[:, 0:4], in_=fbexps),
          reads=[r_grow], writes=[r_grow], queue="dma")
        E("tensor", lambda e: e.matmul(ps5[2][:, 0:4], onerow_f[:1, :],
                                       grow[:1, 0:4], start=True, stop=True),
          reads=[r_const, r_grow], writes=[r_ps5[2]])
        E("vector", lambda e: e.tensor_copy(bxc_sb[:, :], ps5[2][:, 0:4]),
          reads=[r_ps5[2]], writes=[r_bxc])
        zview = v_z2[:, 0:KT * NB].rearrange("p (k n) -> p k n", n=NB)
        E("vector", lambda e: e.scalar_tensor_tensor(
            fb2_sb[:, 0:KT], zview[:, :, 0], bxc_sb[:, 0:1], zview[:, :, 0],
            ALU.mult, ALU.bypass),
          reads=list(r_z2) + [r_bxc], writes=[r_fb2])
        for nn in range(1, NB):
            E("vector", lambda e, nn=nn: e.scalar_tensor_tensor(
                fb2_sb[:, 0:KT], zview[:, :, nn], bxc_sb[:, nn:nn + 1],
                fb2_sb[:, 0:KT], ALU.mult, ALU.add),
              reads=list(r_z2) + [r_bxc, r_fb2], writes=[r_fb2])
        E("vector", lambda e: e.tensor_scalar(fb2_sb[:, 0:KT], fb2_sb[:, 0:KT],
                                              -1.0, 1.0, ALU.max, ALU.min),
          reads=[r_fb2], writes=[r_fb2])
        E("scalar", lambda e: e.activation(bias_sb[:, :], fb2_sb[:, 0:KT],
                                           AF.Abs, scale=15.0),
          reads=[r_fb2], writes=[r_bias])
        E("vector", lambda e: e.tensor_copy(fbq_sb[:, :], bias_sb[:, :]),
          reads=[r_bias], writes=[r_fbq])
        E("scalar", lambda e: e.sign(bias_sb[:, :], fb2_sb[:, 0:KT]),
          reads=[r_fb2], writes=[r_bias])
        E("vector", lambda e: e.tensor_mul(bias_sb[:, :], fbq_sb[:, :],
                                           bias_sb[:, :]),
          reads=[r_fbq, r_bias], writes=[r_bias])
        E("sync", lambda e: e.dma_start(out=grow[:, 5:6], in_=fbsc),
          reads=[r_grow], writes=[r_grow], queue="dma")
        E("tensor", lambda e: e.matmul(ps5[3][:, 0:1], onerow_f[:1, :],
                                       grow[:1, 5:6], start=True, stop=True),
          reads=[r_const, r_grow], writes=[r_ps5[3]])
        E("vector", lambda e: e.tensor_copy(bsc_sb[:, :], ps5[3][:, 0:1]),
          reads=[r_ps5[3]], writes=[r_bsc])
        E("vector", lambda e: e.tensor_scalar_mul(bias_sb[:, :], bias_sb[:, :],
                                                  1.0 / 15.0),
          reads=[r_bias], writes=[r_bias])
        E("vector", lambda e: e.scalar_tensor_tensor(
            bias_sb[:, :], bias_sb[:, :], bsc_sb[:, 0:1], bias_sb[:, :],
            ALU.mult, ALU.bypass),
          reads=[r_bias, r_bsc], writes=[r_bias])

        def layer_norm_half(hh, mode, tag):
            sl = slice(hh * LH, (hh + 1) * LH)
            for j in range(KT):
                E("vector", lambda e, j=j, sl=sl: e.tensor_copy(
                    v_lnx, x_sb[:, j, sl]),
                  reads=[r_x[j]], writes=r_lnx, tag="lncp" + tag)
                E("scalar", lambda e, j=j, sl=sl: e.activation(
                    v_lnsq, x_sb[:, j, sl], AF.Square),
                  reads=[r_x[j]], writes=r_lnsq, tag="lnsq" + tag)
                E("tensor", lambda e, j=j: e.matmul(
                    psA[0][:1, 0:LH], onecol_b[:, 0:1], v_lnx,
                    start=(j == 0), stop=(j == KT - 1)),
                  reads=list(r_lnx) + [r_const], writes=[r_psA[0]], tag="lnm1" + tag)
                E("tensor", lambda e, j=j: e.matmul(
                    psA[1][:1, 0:LH], onecol_b[:, 0:1], v_lnsq,
                    start=(j == 0), stop=(j == KT - 1)),
                  reads=list(r_lnsq) + [r_const], writes=[r_psA[1]], tag="lnm2" + tag)
            E("vector", lambda e: e.tensor_scalar_mul(
                rows_sb[:, 0:LH], psA[0][:1, 0:LH], 1.0 / DM),
              reads=[r_psA[0]], writes=[r_rows], tag="lnme" + tag)
            E("vector", lambda e: e.tensor_scalar_mul(
                rows_sb[:, LH:2 * LH], psA[1][:1, 0:LH], 1.0 / DM),
              reads=[r_psA[1]], writes=[r_rows], tag="lnms" + tag)
            E("vector", lambda e: e.scalar_tensor_tensor(
                rows_sb[:, 2 * LH:3 * LH], rows_sb[:, 0:LH], -1.0,
                rows_sb[:, 0:LH], ALU.mult, ALU.mult),
              reads=[r_rows], writes=[r_rows], tag="lnm2b" + tag)
            E("vector", lambda e: e.tensor_add(
                rows_sb[:, LH:2 * LH], rows_sb[:, LH:2 * LH],
                rows_sb[:, 2 * LH:3 * LH]),
              reads=[r_rows], writes=[r_rows], tag="lnvar" + tag)
            E("vector", lambda e: e.tensor_scalar_add(
                rows_sb[:, LH:2 * LH], rows_sb[:, LH:2 * LH], EPSLN),
              reads=[r_rows], writes=[r_rows], tag="lneps" + tag)
            E("scalar", lambda e: e.activation(rows_sb[:, 2 * LH:3 * LH],
                                               rows_sb[:, LH:2 * LH], AF.Sqrt),
              reads=[r_rows], writes=[r_rows], tag="lnsqr" + tag)
            E("vector", lambda e: e.reciprocal(rows_sb[:, LH:2 * LH],
                                               rows_sb[:, 2 * LH:3 * LH]),
              reads=[r_rows], writes=[r_rows], tag="lnrc" + tag)
            if mode == "xq":
                E("vector", lambda e: e.tensor_scalar_mul(
                    rows_sb[:, LH:2 * LH], rows_sb[:, LH:2 * LH], 1.0 / S_IN),
                  reads=[r_rows], writes=[r_rows], tag="lnsc" + tag)
            E("vector", lambda e: e.tensor_mul(
                rows_sb[:, 3 * LH:4 * LH], rows_sb[:, 0:LH],
                rows_sb[:, LH:2 * LH]),
              reads=[r_rows], writes=[r_rows], tag="lnmrs" + tag)
            E("tensor", lambda e: e.matmul(
                psA[0][:, 0:LH], onerow_f[:1, :], rows_sb[:1, LH:2 * LH],
                start=True, stop=True),
              reads=[r_rows, r_const], writes=[r_psA[0]], tag="lnb1" + tag)
            E("tensor", lambda e: e.matmul(
                psA[1][:, 0:LH], onerow_f[:1, :], rows_sb[:1, 3 * LH:4 * LH],
                start=True, stop=True),
              reads=[r_rows, r_const], writes=[r_psA[1]], tag="lnb2" + tag)
            for j in range(KT):
                E("vector", lambda e, j=j, sl=sl: e.scalar_tensor_tensor(
                    v_ev[0], x_sb[:, j, sl], 1.0, psA[0][:, 0:LH],
                    ALU.bypass, ALU.mult),
                  reads=[r_x[j], r_psA[0]], writes=[r_ev[0]], tag="lna1" + tag)
                if mode == "bf16":
                    E("vector", lambda e, j=j: e.tensor_sub(
                        xbhv[:, j, :], v_ev[0], psA[1][:, 0:LH]),
                      reads=[r_ev[0], r_psA[1]], writes=r_xbh, tag="lna2" + tag)
                else:
                    E("vector", lambda e, j=j: e.tensor_sub(
                        v_ev[1], v_ev[0], psA[1][:, 0:LH]),
                      reads=[r_ev[0], r_psA[1]], writes=[r_ev[1]], tag="lqa2" + tag)
                    E("vector", lambda e: e.tensor_copy(ti_b[0][:, 0:LH], v_ev[1]),
                      reads=[r_ev[1]], writes=[r_ti[0]], tag="lqc" + tag)
                    E("vector", lambda e: e.tensor_scalar(
                        xq_st[:, :], ti_b[0][:, 0:LH], -128.0, 127.0,
                        ALU.max, ALU.min),
                      reads=[r_ti[0]], writes=[r_xqst], tag="lqk" + tag)
                    E("sync", lambda e, j=j, hh=hh: e.dma_start(
                        out=d_xqin[j * 128:(j + 1) * 128, hh * LH:(hh + 1) * LH],
                        in_=xq_st[:, :]),
                      reads=[r_xqst], writes=[r_dxqin], queue="dma", tag="lqs" + tag)

        E("vector", lambda e: e.reduce_max(bw_sb[:, NB:NB + 1], bw_sb[:, 0:NB],
                                           axis=AX.X),
          reads=[r_bw], writes=[r_bw])
        E("vector", lambda e: e.scalar_tensor_tensor(
            bw_sb[:, 0:NB], bw_sb[:, 0:NB], bw_sb[:, NB:NB + 1], bw_sb[:, 0:NB],
            ALU.subtract, ALU.bypass),
          reads=[r_bw], writes=[r_bw])
        E("scalar", lambda e: e.activation(bw_sb[:, 0:NB], bw_sb[:, 0:NB], AF.Exp),
          reads=[r_bw], writes=[r_bw])
        E("vector", lambda e: e.reduce_sum(bw_sb[:, NB:NB + 1], bw_sb[:, 0:NB],
                                           axis=AX.X),
          reads=[r_bw], writes=[r_bw])
        E("vector", lambda e: e.reciprocal(bw_sb[:, NB + 1:NB + 2],
                                           bw_sb[:, NB:NB + 1]),
          reads=[r_bw], writes=[r_bw])
        E("vector", lambda e: e.scalar_tensor_tensor(
            bw_sb[:, 0:NB], bw_sb[:, 0:NB], bw_sb[:, NB + 1:NB + 2],
            bw_sb[:, 0:NB], ALU.mult, ALU.bypass),
          reads=[r_bw], writes=[r_bw])
        def actmm(e):
            last = None
            for k in range(KT):
                last = e.matmul(ps5[0][:1, 0:NB], onecol_b[:, 0:1], mn_b[:, k, :],
                                start=(k == 0), stop=(k == KT - 1))
            return last
        E("tensor", actmm, reads=[r_mn, r_const], writes=[r_ps5[0]])
        E("vector", lambda e: e.tensor_single_scalar(
            grow[:1, 0:NB], ps5[0][:1, 0:NB], EPSB, ALU.is_ge),
          reads=[r_ps5[0]], writes=[r_grow])
        E("vector", lambda e: e.tensor_copy(v_qst[0:KT, 0:NB], bw_sb[:, 0:NB]),
          reads=[r_bw], writes=r_qst)
        E("tensor", lambda e: e.matmul(ps5[1][0:NB, 0:KT], v_qst[0:KT, 0:NB],
                                       ident_b[0:KT, 0:KT], start=True, stop=True),
          reads=list(r_qst) + [r_const], writes=[r_ps5[1]])
        E("tensor", lambda e: e.matmul(ps5[2][0:NB, 0:1], grow[:1, 0:NB],
                                       onecol_f[0:1, 0:1], start=True, stop=True),
          reads=[r_grow, r_const], writes=[r_ps5[2]])
        E("vector", lambda e: e.tensor_copy(act_sb[:, :], ps5[2][0:NB, 0:1]),
          reads=[r_ps5[2]], writes=[r_act])
        E("vector", lambda e: e.tensor_copy(weT_sb[:, :], ps5[1][0:NB, 0:KT]),
          reads=[r_ps5[1]], writes=[r_weT])
        E("vector", lambda e: e.tensor_scalar_mul(weT_sb[:, :], weT_sb[:, :],
                                                  act_sb[:, 0:1]),
          reads=[r_weT, r_act], writes=[r_weT])

        for hh in range(T // LH):
            sl = slice(hh * LH, (hh + 1) * LH)
            layer_norm_half(hh, "bf16", "L1h%d" % hh)

            def emit_qhead(h):
                b = h % 2
                E("gpsimd", lambda e, h=h, b=b: e.dma_start(
                    out=wglv[b],
                    in_=wqT[:, h * 128:(h + 1) * 128]
                        .rearrange("(k p) o -> p k o", p=128)),
                  writes=r_wgl[b], queue="dma", tag="wqld")
                E("gpsimd", lambda e, h=h, b=b: e.dma_start(
                    out=memnv[b], in_=memi[:, h, :, :].rearrange("n p e -> p n e")),
                  writes=r_memn[b], queue="dma", tag="memld")

                def qmm(e, b=b):
                    last = None
                    for kk in range(KT):
                        last = e.matmul(psA[0][:, 0:LH], wglv[b][:, kk, :],
                                        xbhv[:, kk, :],
                                        start=(kk == 0), stop=(kk == KT - 1))
                    return last
                E("tensor", qmm, reads=list(r_wgl[b]) + list(r_xbh),
                  writes=[r_psA[0]], tag="qmm")

            emit_qhead(0)
            for h in range(KT):
                b = h % 2
                E("vector", lambda e: e.tensor_scalar_min(v_ev[0], psA[0][:, 0:LH],
                                                          0.0),
                  reads=[r_psA[0]], writes=[r_ev[0]], tag="sg1")
                E("scalar", lambda e: e.activation(v_ev[1], v_ev[0], AF.Exp),
                  reads=[r_ev[0]], writes=[r_ev[1]], tag="sg2")
                E("vector", lambda e: e.scalar_tensor_tensor(
                    v_sig, psA[0][:, 0:LH], 0.0, v_ev[1], ALU.max, ALU.add),
                  reads=[r_psA[0], r_ev[1]], writes=r_sig, tag="sg3")

                def nmm(e, h=h):
                    return e.matmul(ps5[2][0:NB, 0:LH], mn_b[:, h, :], v_sig,
                                    start=True, stop=True)
                E("tensor", nmm, reads=[r_mn] + list(r_sig), writes=[r_ps5[2]],
                  tag="nmm")
                if h + 1 < KT:
                    emit_qhead(h + 1)
                E("vector", lambda e: e.tensor_scalar_max(
                    v_ev[0][0:NB, :], ps5[2][0:NB, 0:LH], EPSB),
                  reads=[r_ps5[2]], writes=[r_ev[0]], tag="rc1")
                E("vector", lambda e: e.reciprocal(v_ev[1][0:NB, :],
                                                   v_ev[0][0:NB, :]),
                  reads=[r_ev[0]], writes=[r_ev[1]], tag="rc2")
                E("vector", lambda e, h=h: e.scalar_tensor_tensor(
                    v_rr[0:NB, :], v_ev[1][0:NB, :], weT_sb[:, h:h + 1],
                    v_ev[1][0:NB, :], ALU.mult, ALU.bypass),
                  reads=[r_ev[1], r_weT], writes=r_rr, tag="rc3")
                for nn in range(NB):
                    nb2 = nn % 2
                    psa_ = ps5[0] if nb2 == 0 else ps5[2]
                    r_psa_ = r_ps5[0] if nb2 == 0 else r_ps5[2]

                    def amm(e, nn=nn, b=b, psa_=psa_):
                        return e.matmul(psa_[:, 0:LH], memnv[b][:, nn, :], v_sig,
                                        start=True, stop=True)
                    E("tensor", amm, reads=list(r_memn[b]) + list(r_sig),
                      writes=[r_psa_], tag="amm")

                    v_car, r_car = car_pair[nb2]
                    v_rbS, r_rbS = rbS_pair[nb2]
                    psb = ps5[1] if nb2 == 0 else ps5[3]
                    r_psb = r_ps5[1] if nb2 == 0 else r_ps5[3]

                    def rbc(e, nn=nn, psb=psb):
                        return e.matmul(psb[:, 0:LH],
                                        sel_sb[:, nn * 128:(nn + 1) * 128],
                                        v_rr[0:NB, :], start=True, stop=True)
                    E("tensor", rbc, reads=[r_sel] + list(r_rr),
                      writes=[r_psb], tag="rbc")
                    E("scalar", lambda e, v_rbS=v_rbS, psb=psb: e.copy(
                        v_rbS, psb[:, 0:LH]),
                      reads=[r_psb], writes=r_rbS, tag="rbev")
                    E("vector", lambda e, v_car=v_car, v_rbS=v_rbS, psa_=psa_: (
                        e.tensor_mul(v_car, psa_[:, 0:LH], v_rbS)),
                      reads=[r_psa_] + list(r_rbS), writes=r_car, tag="car")

                    def cacc_mm(e, nn=nn, v_car=v_car):
                        return e.matmul(psA[1][:, 0:LH], ident_b[:, :], v_car,
                                        start=(nn == 0), stop=(nn == NB - 1))
                    E("tensor", cacc_mm, reads=[r_const] + list(r_car),
                      writes=[r_psA[1]], tag="cacm")
                E("scalar", lambda e, h=h: e.copy(combhv[:, h, :], psA[1][:, 0:LH]),
                  reads=[r_psA[1]], writes=r_combh, tag="cev")
            for o in range(KT):
                b = o % 2
                E("gpsimd", lambda e, o=o, b=b: e.dma_start(
                    out=wglv[b],
                    in_=woT[:, o * 128:(o + 1) * 128]
                        .rearrange("(k p) o -> p k o", p=128)),
                  writes=r_wgl[b], queue="dma", tag="wold")

                def womm(e, b=b):
                    last = None
                    for kk in range(KT):
                        last = e.matmul(psA[0][:, 0:LH], wglv[b][:, kk, :],
                                        combhv[:, kk, :],
                                        start=(kk == 0), stop=(kk == KT - 1))
                    return last
                E("tensor", womm, reads=list(r_wgl[b]) + list(r_combh),
                  writes=[r_psA[0]], tag="womm")
                E("scalar", lambda e: e.copy(v_ev[0], psA[0][:, 0:LH]),
                  reads=[r_psA[0]], writes=[r_ev[0]], tag="woev")
                E("vector", lambda e, o=o, sl=sl: e.tensor_add(
                    x_sb[:, o, sl], x_sb[:, o, sl], v_ev[0]),
                  reads=[r_ev[0], r_x[o]], writes=[r_x[o]], tag="woadd")

        def ffn(tag):
            for hh in range(T // LH):
                sl = slice(hh * LH, (hh + 1) * LH)
                layer_norm_half(hh, "bf16", "Lf%s%d" % (tag, hh))
                for o in range(KT):
                    b = o % 2
                    rk = (o * 128) // OSL
                    c0 = (o * 128) % OSL
                    E("sync", lambda e, rk=rk, c0=c0, b=b: e.dma_start(
                        out=wglv[b],
                        in_=d_ffout.ap()[rk * DM:(rk + 1) * DM, c0:c0 + 128]
                            .rearrange("(k p) o -> p k o", p=128)),
                      reads=[r_dffout], writes=r_wgl[b], queue="dma",
                      tag="ffld" + tag)

                    def fmm(e, b=b):
                        last = None
                        for kk in range(KT):
                            last = e.matmul(psA[0][:, 0:LH], wglv[b][:, kk, :],
                                            xbhv[:, kk, :],
                                            start=(kk == 0), stop=(kk == KT - 1))
                        return last
                    E("tensor", fmm, reads=list(r_wgl[b]) + list(r_xbh),
                      writes=[r_psA[0]], tag="fmm" + tag)
                    E("scalar", lambda e, o=o: e.activation(
                        v_ev[0], psA[0][:, 0:LH], AF.Identity,
                        bias=bias_sb[:, o:o + 1]),
                      reads=[r_psA[0], r_bias], writes=[r_ev[0]], tag="fev" + tag)
                    E("vector", lambda e, o=o, sl=sl: e.tensor_add(
                        x_sb[:, o, sl], x_sb[:, o, sl], v_ev[0]),
                      reads=[r_ev[0], r_x[o]], writes=[r_x[o]], tag="fadd" + tag)

        ffn("1")

        v_wld2, r_wld2 = FA.view(5120, 5120 + 1792)
        _wpair = [(v_wld, r_wld), (v_wld2, r_wld2)]
        _wi = 0
        for i in range(KT):
            for c0 in range(0, 2 * DC, WLDW):
                c1 = min(c0 + WLDW, 2 * DC)
                vw, rw = _wpair[_wi % 2]
                _wi += 1
                E("sync", lambda e, i=i, c0=c0, c1=c1, vw=vw: e.dma_start(
                    out=vw[:, 0:c1 - c0], in_=ipT[i * 128:(i + 1) * 128, c0:c1]),
                  writes=rw, queue="dma")
                qround(rw, vw, r_wqip, wqipv[:, i, c0:c1],
                       8, c1 - c0, "ip")

        for hh in range(T // LH):
            layer_norm_half(hh, "xq", "L3h%d" % hh)
        E("gpsimd", lambda e: e.collective_compute(
            "AllGather", ALU.bypass, replica_groups=[list(range(NC))],
            ins=[d_xqin.ap().opt()], outs=[d_xqout.ap().opt()]),
          reads=[r_dxqin], writes=[r_dxqout], queue="cc")

        for half_o, o_range in ((0, range(2 * DT)),):
          for tb2 in range(2 * NTB):
            rk = (tb2 * TB2) // T
            t0 = (tb2 * TB2) % T
            rb = tb2 % 2
            E("gpsimd", lambda e, rk=rk, t0=t0, rb=rb: e.dma_start(
                out=rhsv2[rb],
                in_=d_xqout.ap()[rk * DM:(rk + 1) * DM, t0:t0 + TB2]
                    .rearrange("(k p) t -> p k t", p=128)),
              reads=[r_dxqout], writes=r_rhs2[rb], queue="dma", tag="xqld")
            for o in o_range:
                pst = ps5[o % 4]
                r_pst = r_ps5[o % 4]

                def imm(e, o=o, pst=pst, rb=rb):
                    last = None
                    for kk in range(KT):
                        last = e.matmul(pst[:, 0:TB2],
                                        wqipv[:, kk, o * 128:(o + 1) * 128],
                                        rhsv2[rb][:, kk, :],
                                        start=(kk == 0), stop=(kk == KT - 1))
                    return last
                E("tensor", imm, reads=list(r_wqip) + list(r_rhs2[rb]),
                  writes=[r_pst], tag="imm")
                eb = o % 2
                E("scalar", lambda e, pst=pst, eb=eb: e.activation(
                    v_ev[eb][:, 0:TB2], pst[:, 0:TB2], AF.Identity,
                    scale=sc_sb[:, 13:14]),
                  reads=[r_pst, r_sc], writes=[r_ev[eb]], tag="iev")
                E("vector", lambda e, eb=eb: e.tensor_copy(ti_b[eb][:, 0:TB2],
                                                           v_ev[eb][:, 0:TB2]),
                  reads=[r_ev[eb]], writes=[r_ti[eb]], tag="icast")
                if o < DT:
                    E("vector", lambda e, o=o, tb2=tb2, eb=eb: e.tensor_scalar(
                        hsv[:, o, tb2 * TB2:(tb2 + 1) * TB2], ti_b[eb][:, 0:TB2],
                        -128.0, 127.0, ALU.max, ALU.min),
                      reads=[r_ti[eb]], writes=r_hs, tag="ich")
                else:
                    E("vector", lambda e, eb=eb: e.tensor_scalar(
                        yq_st[eb][:, 0:TB2], ti_b[eb][:, 0:TB2],
                        -128.0, 127.0, ALU.max, ALU.min),
                      reads=[r_ti[eb]], writes=[r_yqst[eb]], tag="icg")
                    E("sync", lambda e, o=o, tb2=tb2, eb=eb: e.dma_start(
                        out=d_gate[(o - DT) * 128:(o - DT + 1) * 128,
                                   tb2 * TB2:(tb2 + 1) * TB2],
                        in_=yq_st[eb][:, 0:TB2]),
                      reads=[r_yqst[eb]], writes=[r_dgate], queue="dma", tag="gsp")

        def u_res(d, c0, c1):
            lo = (d * S + c0) // 512
            hi = (d * S + c1 + 511) // 512
            return r_u[lo:hi]

        def xproj_half(hf):
            for tb in range(hf * (NTB // 2), (hf + 1) * (NTB // 2)):
                xproj_tb(tb, hf)
            E("gpsimd", lambda e, hf=hf: e.collective_compute(
                "AllReduce", ALU.add, replica_groups=[list(range(NC))],
                ins=[d_ssin[hf].ap().opt()], outs=[d_ssout[hf].ap().opt()]),
              reads=[r_dssin[hf]], writes=[r_dssout[hf]], queue="cc")

        def xproj_tb(tb, hf):
            r_u_tb = sum((u_res(d, tb * TBW, (tb + 1) * TBW)
                          for d in range(DT)), [])
            E("gpsimd", lambda e, tb=tb: e.dma_start(
                out=ubxv, in_=uv[:, :, tb * TBW:(tb + 1) * TBW]),
              reads=r_u_tb, writes=r_ubx, queue="dma", tag="uxld")

            def xpm(e):
                last = None
                for kk in range(DT):
                    last = e.matmul(ps5[0][:, 0:TBW], xpq_b[:, kk, 0:128],
                                    ubxv[:, kk, :],
                                    start=(kk == 0), stop=(kk == DT - 1))
                for kk in range(DT):
                    last = e.matmul(ps5[1][0:2 * N, 0:TBW], xpq_b[:, kk, 128:RN2],
                                    ubxv[:, kk, :],
                                    start=(kk == 0), stop=(kk == DT - 1))
                return last
            E("tensor", xpm, reads=[r_xpq] + list(r_ubx),
              writes=[r_ps5[0], r_ps5[1]], tag="xpm")
            E("scalar", lambda e: e.activation(v_ev[0], ps5[0][:, 0:TBW],
                                               AF.Identity, scale=sc_sb[:, 14:15]),
              reads=[r_ps5[0], r_sc], writes=[r_ev[0]], tag="xpe0")
            E("scalar", lambda e: e.activation(v_ev[1][0:2 * N, :],
                                               ps5[1][0:2 * N, 0:TBW],
                                               AF.Identity,
                                               scale=sc_sb[0:2 * N, 14:15]),
              reads=[r_ps5[1], r_sc], writes=[r_ev[1]], tag="xpe1")
            c0 = tb * TBW - hf * (S // 2)
            E("sync", lambda e, c0=c0, hf=hf: e.dma_start(
                out=d_ssin[hf][0:128, c0:c0 + TBW], in_=v_ev[0]),
              reads=[r_ev[0]], writes=[r_dssin[hf]], queue="dma", tag="xps0")
            E("sync", lambda e, c0=c0, hf=hf: e.dma_start(
                out=d_ssin[hf][128:RN2, c0:c0 + TBW],
                in_=v_ev[1][0:2 * N, :]),
              reads=[r_ev[1]], writes=[r_dssin[hf]], queue="dma", tag="xps1")

        NTH = S // CW
        for th in range(NTH):
            for d in range(DT):
                sl0 = th * CW
                v_cacc, r_cacc = cacc_pair[(th * DT + d) % 2]
                E("vector", lambda e, v_cacc=v_cacc: e.memset(v_cacc, 0.0),
                  writes=r_cacc, tag="cv0")
                for kk in range(K):
                    sh = K - 1 - kk
                    lo = sh
                    if sl0 > 0 and sh > 0:
                        E("vector", lambda e, d=d, kk=kk, sh=sh, v_cacc=v_cacc: (
                            e.scalar_tensor_tensor(
                                v_cacc[:, 0:sh], halo[:, d, K - sh:K],
                                cw_eff[:, d, kk:kk + 1], v_cacc[:, 0:sh],
                                ALU.mult, ALU.add)),
                          reads=[r_halo, r_cw] + list(r_cacc), writes=r_cacc,
                          tag="cvh")
                    E("vector", lambda e, d=d, kk=kk, sh=sh, lo=lo, sl0=sl0,
                        v_cacc=v_cacc: (
                        e.scalar_tensor_tensor(
                            v_cacc[:, lo:CW],
                            hsv[:, d, sl0 + lo - sh:sl0 + CW - sh],
                            cw_eff[:, d, kk:kk + 1], v_cacc[:, lo:CW],
                            ALU.mult, ALU.add)),
                      reads=list(r_hs) + [r_cw] + list(r_cacc), writes=r_cacc,
                      tag="cv")
                E("vector", lambda e, d=d, sl0=sl0: e.tensor_copy(
                    halo[:, d, 1:K], hsv[:, d, sl0 + CW - (K - 1):sl0 + CW]),
                  reads=r_hs, writes=[r_halo], tag="cvsv")
                E("scalar", lambda e, d=d, v_cacc=v_cacc: e.activation(
                    v_cacc, v_cacc, AF.Silu, bias=cb_sb[:, d:d + 1]),
                  reads=list(r_cacc) + [r_cb], writes=r_cacc, tag="cvsl")
                for ci in range(CW // CH):
                    E("scalar", lambda e, ci=ci, v_cacc=v_cacc: e.activation(
                        ti_u[:, 0:CH], v_cacc[:, ci * CH:(ci + 1) * CH],
                        AF.Identity, scale=1.0 / S_U),
                      reads=r_cacc, writes=r_tiu, tag="us")
                    E("vector", lambda e, d=d, sl0=sl0, ci=ci: e.tensor_scalar(
                        uv[:, d, sl0 + ci * CH:sl0 + (ci + 1) * CH],
                        ti_u[:, 0:CH], -128.0, 127.0, ALU.max, ALU.min),
                      reads=r_tiu,
                      writes=u_res(d, sl0 + ci * CH, sl0 + (ci + 1) * CH),
                      tag="uk")
            if NTH >= 2 and th == NTH // 2 - 1:
                xproj_half(0)
        if NTH < 2:
            xproj_half(0)
        xproj_half(1)

        def rms_chunk(tb, E):
            sl = slice(tb * CH, (tb + 1) * CH)
            hf = tb // (NCH // 2)
            lsl = slice(tb * CH - hf * (S // 2), (tb + 1) * CH - hf * (S // 2))
            p2 = tb % 2
            zh = p2 * CH
            vz, rz = v_z2[:, zh:zh + CH], [r_z2[p2]]
            vt, rt = v_tq[:, zh:zh + CH], [r_tq[p2]]
            tib, rti = ti_b[p2], [r_ti[p2]]
            vsq, rsq = v_h[p2], r_h[p2]
            vkb, rkb = v_h[2 + p2], r_h[2 + p2]
            psd, rpsd = psA[p2], [r_psA[p2]]
            rb = p2 * 2 * CH
            rw0 = rows_sb[:, rb:rb + CH]
            rw1 = rows_sb[:, rb + CH:rb + 2 * CH]
            rr = r_rows2[p2 * 2:p2 * 2 + 2]
            E("sync", lambda e, lsl=lsl, hf=hf, vz=vz: e.dma_start(
                out=vz, in_=d_ssout[hf].ap()[0:128, lsl]),
              reads=[r_dssout[hf]], writes=rz, queue="dma", tag="dtld")
            E("scalar", lambda e, vz=vz, vsq=vsq: e.activation(vsq, vz, AF.Square),
              reads=rz, writes=rsq, tag="dtsq")
            E("tensor", lambda e, vsq=vsq, psd=psd: e.matmul(
                psd[:1, 0:CH], onecol_b[:, 0:1], vsq, start=True, stop=True),
              reads=rsq + [r_const], writes=rpsd, tag="dtmm")
            E("vector", lambda e, psd=psd, rw0=rw0: e.tensor_scalar(
                rw0, psd[:1, 0:CH], 1.0 / R, EPSB, ALU.mult, ALU.add),
              reads=rpsd, writes=rr, tag="dtm")
            E("scalar", lambda e, rw0=rw0, rw1=rw1: e.activation(rw1, rw0, AF.Sqrt),
              reads=rr, writes=rr, tag="dts")
            E("vector", lambda e, rw0=rw0, rw1=rw1: e.reciprocal(rw0, rw1),
              reads=rr, writes=rr, tag="dtr")
            E("tensor", lambda e, psd=psd, rw0=rw0: e.matmul(
                psd[:, 0:CH], onerow_f[:1, :], rw0[:1, :], start=True, stop=True),
              reads=rr + [r_const], writes=rpsd, tag="dtbc")
            E("vector", lambda e, vz=vz, vt=vt, psd=psd: e.scalar_tensor_tensor(
                vt, vz, wdt_sb[:, 0:1], psd[:, 0:CH], ALU.mult, ALU.mult),
              reads=rz + [r_wdtc] + rpsd, writes=rt, tag="dtap")
            E("scalar", lambda e, vt=vt, tib=tib: e.activation(
                tib[:, 0:CH], vt, AF.Identity),
              reads=rt, writes=rti, tag="dtc")
            E("vector", lambda e, sl=sl, tib=tib: e.tensor_scalar(
                v_dtq[:, sl], tib[:, 0:CH], -128.0, 127.0, ALU.max, ALU.min),
              reads=rti, writes=r_dtq, tag="dtk")
            for bci, wcol in ((0, 0), (1, 1)):
                ro = 128 + bci * N
                E("sync", lambda e, lsl=lsl, ro=ro, hf=hf, vz=vz: e.dma_start(
                    out=vz[0:N, :], in_=d_ssout[hf].ap()[ro:ro + N, lsl]),
                  reads=[r_dssout[hf]], writes=rz, queue="dma", tag="bcld")
                E("scalar", lambda e, vz=vz, vsq=vsq: e.activation(
                    vsq[0:N, :], vz[0:N, :], AF.Square),
                  reads=rz, writes=rsq, tag="bcsq")
                E("tensor", lambda e, vsq=vsq, psd=psd: e.matmul(
                    psd[:1, 0:CH], onecol_b[0:N, 0:1], vsq[0:N, :],
                    start=True, stop=True),
                  reads=rsq + [r_const], writes=rpsd, tag="bcmm")
                E("vector", lambda e, psd=psd, rw0=rw0: e.tensor_scalar(
                    rw0, psd[:1, 0:CH], 1.0 / N, EPSB, ALU.mult, ALU.add),
                  reads=rpsd, writes=rr, tag="bcm")
                E("scalar", lambda e, rw0=rw0, rw1=rw1: e.activation(
                    rw1, rw0, AF.Sqrt),
                  reads=rr, writes=rr, tag="bcs")
                E("vector", lambda e, rw0=rw0, rw1=rw1: e.reciprocal(rw0, rw1),
                  reads=rr, writes=rr, tag="bcr")
                E("tensor", lambda e, psd=psd, rw0=rw0: e.matmul(
                    psd[0:N, 0:CH], onerow_f[:1, 0:N], rw0[:1, :],
                    start=True, stop=True),
                  reads=rr + [r_const], writes=rpsd, tag="bcbc")
                E("vector", lambda e, wcol=wcol, vz=vz, vt=vt, psd=psd: (
                    e.scalar_tensor_tensor(
                        vt[0:N, :], vz[0:N, :], wbc_sb[:, wcol:wcol + 1],
                        psd[0:N, 0:CH], ALU.mult, ALU.mult)),
                  reads=rz + [r_wbcc] + rpsd, writes=rt, tag="bcap")
                E("scalar", lambda e, vt=vt, tib=tib: e.activation(
                    tib[0:N, 0:CH], vt[0:N, :], AF.Identity),
                  reads=rt, writes=rti, tag="bcc")
                E("vector", lambda e, tib=tib, vkb=vkb: e.tensor_scalar(
                    vkb[0:N, :], tib[0:N, 0:CH], -128.0, 127.0,
                    ALU.max, ALU.min),
                  reads=rti, writes=rkb, tag="bck")
                E("sync", lambda e, ro2=bci * N, sl=sl, vkb=vkb: e.dma_start(
                    out=d_bc[ro2:ro2 + N, sl], in_=vkb[0:N, :]),
                  reads=rkb, writes=[r_dbc], queue="dma", tag="bcw")

        def _rec(lst):
            def EE(*a, **k):
                lst.append((a, k))
            return EE

        for tb0 in range(0, NCH, 2):
            b0, b1 = [], []
            rms_chunk(tb0, _rec(b0))
            if tb0 + 1 < NCH:
                rms_chunk(tb0 + 1, _rec(b1))
            for t in range(max(len(b0), len(b1))):
                for bb in (b0, b1):
                    if t < len(bb):
                        a, k = bb[t]
                        E(*a, **k)

        E("vector", lambda e: e.memset(st_sb[:, :], 0.0), writes=list(r_st))
        for cidx in range(NCH):
            sl = slice(cidx * CH, (cidx + 1) * CH)
            for d in range(DT):
                zh = (d % 2) * CH
                vz, rz = v_z2[:, zh:zh + CH], [r_z2[d % 2]]
                vt, rt = v_tq[:, zh:zh + CH], [r_tq[d % 2]]
                tib, rti = ti_b[d % 2], [r_ti[d % 2]]
                psd = psA[d % 2]
                rpsd = [r_psA[d % 2]]

                def d8m(e, d=d, sl=sl, psd=psd):
                    return e.matmul(psd[:, CH:2 * CH],
                                    dpq_b[:, d * 128:(d + 1) * 128],
                                    v_dtq[:, sl], start=True, stop=True)
                E("tensor", d8m, reads=[r_dpq] + list(r_dtq), writes=rpsd,
                  tag="d8m")
                E("scalar", lambda e, tib=tib, psd=psd: e.activation(
                    tib[:, 0:CH], psd[:, CH:2 * CH], AF.Identity,
                    scale=sc_sb[:, 15:16]),
                  reads=rpsd + [r_sc], writes=rti, tag="d8e")
                E("vector", lambda e, vt=vt, tib=tib: e.tensor_scalar(
                    vt, tib[:, 0:CH], -128.0, 127.0, ALU.max, ALU.min),
                  reads=rti, writes=rt, tag="d8k")
                E("scalar", lambda e, d=d, vz=vz, vt=vt: e.activation(
                    vz, vt, AF.Identity, scale=S_DT, bias=dtb_sb[:, d:d + 1]),
                  reads=rt + [r_dtb], writes=rz, tag="sp1")
                E("scalar", lambda e, vz=vz, vt=vt: e.activation(vt, vz, AF.Abs),
                  reads=rz, writes=rt, tag="sp2")
                E("scalar", lambda e, vt=vt: e.activation(vt, vt, AF.Exp,
                                                          scale=-1.0),
                  reads=rt, writes=rt, tag="sp3")
                E("scalar", lambda e, vt=vt: e.activation(vt, vt, AF.Ln,
                                                          bias=onecol_f[:, 0:1]),
                  reads=rt + [r_const], writes=rt, tag="sp4")
                E("vector", lambda e, d=d, vz=vz, vt=vt: e.scalar_tensor_tensor(
                    dtdv[:, d, :], vz, 0.0, vt, ALU.max, ALU.add),
                  reads=rz + rt, writes=[r_dtd[min(d, len(r_dtd) - 1)]],
                  tag="sp5")
                E("vector", lambda e, d=d, sl=sl: e.tensor_mul(
                    wdv[:, d, :], dtdv[:, d, :], uv[:, d, sl]),
                  reads=[r_dtd[min(d, len(r_dtd) - 1)]] + list(r_u),
                  writes=[r_wd[min(d, len(r_wd) - 1)]], tag="wmul")
            for nq in range(N // 4):
                db2 = nq % 2
                E("gpsimd", lambda e, nq=nq, sl=sl, db2=db2: e.dma_start(
                    out=bcBf[db2].rearrange("p (n t) -> p n t", n=4),
                    in_=d_bc.ap()[nq * 4:(nq + 1) * 4, sl].partition_broadcast(128)),
                  reads=[r_dbc], writes=r_bcB[db2], queue="dma", tag="bcBl")
                E("gpsimd", lambda e, nq=nq, sl=sl, db2=db2: e.dma_start(
                    out=bcCf[db2].rearrange("p (n t) -> p n t", n=4),
                    in_=d_bc.ap()[N + nq * 4:N + (nq + 1) * 4, sl]
                        .partition_broadcast(128)),
                  reads=[r_dbc], writes=r_bcC[db2], queue="dma", tag="bcCl")
                for d in range(DT):
                    rst = r_st[d % 8]
                    for n4 in range(4):
                        nn = nq * 4 + n4
                        E("scalar", lambda e, d=d, nn=nn, n4=n4: e.activation(
                            v_dA[n4], dtdv[:, d, :], AF.Exp,
                            scale=a_sb[:, d, nn:nn + 1]),
                          reads=[r_dtd[min(d, len(r_dtd) - 1)], r_a],
                          writes=[r_dA[n4]], tag="dA")
                        E("gpsimd" if n4 % 2 == 0 else "vector",
                          lambda e, d=d, n4=n4, db2=db2: (
                            e.tensor_mul(v_wB[n4], wdv[:, d, :],
                                         bcBf[db2][:, n4 * CH:(n4 + 1) * CH])),
                          reads=[r_wd[min(d, len(r_wd) - 1)]] + list(r_bcB[db2]),
                          writes=r_wB[n4], tag="wB")
                        E("vector", lambda e, d=d, nn=nn, n4=n4: (
                            e.tensor_tensor_scan(
                                v_h[n4], v_dA[n4], v_wB[n4],
                                st_sb[:, d * N + nn:d * N + nn + 1],
                                ALU.mult, ALU.add)),
                          reads=[r_dA[n4]] + list(r_wB[n4]) + [rst],
                          writes=list(r_h[n4]), tag="scan")
                    E("scalar", lambda e, d=d, nq=nq: e.activation(
                        st_sb[:, d * N + nq * 4:d * N + nq * 4 + 4],
                        v_hset.rearrange("p (n t) -> p n t", n=4)[:, :, CH - 1],
                        AF.Identity),
                      reads=list(r_hset), writes=[rst], tag="stsv")
                    E("vector", lambda e, db2=db2: e.tensor_mul(
                        v_chset, v_hset, bcCf[db2]),
                      reads=list(r_hset) + list(r_bcC[db2]), writes=list(r_chset),
                      tag="chm")

                    def yac(e, d=d, nq=nq):
                        last = None
                        for n4 in range(4):
                            last = e.matmul(ps5[d][:, 0:CH], ident_b[:, :],
                                            v_ch[n4],
                                            start=(nq == 0 and n4 == 0),
                                            stop=(nq == N // 4 - 1 and n4 == 3))
                        return last
                    E("tensor", yac, reads=[r_const] + list(r_chset),
                      writes=[r_ps5[d]], tag="yac")
            for d in range(DT):
                eb = d % 2
                c0 = sl.start
                E("gpsimd", lambda e, d=d, c0=c0, eb=eb: e.dma_start(
                    out=v_gl[eb],
                    in_=d_gate.ap()[d * 128:(d + 1) * 128, c0:c0 + CH]),
                  reads=[r_dgate], writes=r_gl[eb], queue="dma", tag="glr")
                E("scalar", lambda e, eb=eb: e.activation(
                    v_sil[eb], v_gl[eb], AF.Silu, scale=S_Z),
                  reads=list(r_gl[eb]), writes=[r_sil[eb]], tag="sil")
                E("vector", lambda e, d=d, eb=eb: e.tensor_scalar_mul(
                    v_ev[eb], ps5[d][:, 0:CH], S_U * S_B * S_C / S_Y),
                  reads=[r_ps5[d]], writes=[r_ev[eb]], tag="ysc")
                E("vector", lambda e, d=d, c0=c0, eb=eb: e.scalar_tensor_tensor(
                    v_ev[2 + eb], uv[:, d, c0:c0 + CH], dsk_sb[:, d:d + 1],
                    v_ev[eb], ALU.mult, ALU.add),
                  reads=list(r_u) + [r_dsk, r_ev[eb]], writes=[r_ev[2 + eb]],
                  tag="yskp")
                E("vector", lambda e, eb=eb: e.tensor_mul(
                    v_ev[eb], v_ev[2 + eb], v_sil[eb]),
                  reads=[r_ev[2 + eb], r_sil[eb]], writes=[r_ev[eb]], tag="ygat")
                E("vector", lambda e, eb=eb: e.tensor_copy(ti_b[eb][:, 0:CH],
                                                           v_ev[eb]),
                  reads=[r_ev[eb]], writes=[r_ti[eb]], tag="ycst")
                E("vector", lambda e, eb=eb, d=d: e.tensor_scalar(
                    yq_st4[d][:, 0:CH], ti_b[eb][:, 0:CH], -128.0, 127.0,
                    ALU.max, ALU.min),
                  reads=[r_ti[eb]], writes=[r_yqst4[d]], tag="yclp")
                blk = c0 // T
                haf = (c0 % T) // CH
                E("sync", lambda e, d=d, blk=blk, haf=haf: e.dma_start(
                    out=d_yqin[haf][blk * DC + d * 128:blk * DC + (d + 1) * 128,
                                    0:CH],
                    in_=yq_st4[d][:, 0:CH]),
                  reads=[r_yqst4[d]], writes=[r_dyqin[haf]], queue="dma",
                  tag="yqw")
            for haf in range(NHF):
                if cidx == NCH - NHF + haf:
                    with allow_ici_all_to_all():
                        E("gpsimd", lambda e, haf=haf: e.collective_compute(
                            "AllToAll", ALU.bypass,
                            replica_groups=[list(range(NC))],
                            ins=[d_yqin[haf].ap().opt()],
                            outs=[d_yqout[haf].ap().opt()]),
                          reads=[r_dyqin[haf]], writes=[r_dyqout[haf]],
                          queue="cc")
        for tc in range(T // CH):
            E("gpsimd", lambda e, tc=tc: e.dma_start(
                out=yrv,
                in_=d_yqout[tc].ap()[:, 0:CH]
                    .rearrange("(k p) t -> p k t", p=128)),
              reads=[r_dyqout[tc]], writes=r_yr, queue="dma", tag="yrld")
            for o in range(KT):
                b = o % 2
                E("sync", lambda e, o=o, b=b: e.dma_start(
                    out=oplv[b],
                    in_=d_opout.ap()[:, o * 128:(o + 1) * 128]
                        .rearrange("(k p) t -> p k t", p=128)),
                  reads=[r_dopout], writes=r_opl[b], queue="dma", tag="opld")

                def omm(e, b=b):
                    last = None
                    for kk in range(NKT):
                        last = e.matmul(psA[0][:, 0:CH], oplv[b][:, kk, :],
                                        yrv[:, kk, :],
                                        start=(kk == 0), stop=(kk == NKT - 1))
                    return last
                E("tensor", omm, reads=list(r_opl[b]) + list(r_yr),
                  writes=[r_psA[0]], tag="omm")
                E("scalar", lambda e: e.activation(v_ev[0], psA[0][:, 0:CH],
                                                   AF.Identity,
                                                   scale=sc_sb[:, 7:8]),
                  reads=[r_psA[0], r_sc], writes=[r_ev[0]], tag="oev")
                E("vector", lambda e, o=o, tc=tc: e.tensor_add(
                    x_sb[:, o, tc * CH:(tc + 1) * CH],
                    x_sb[:, o, tc * CH:(tc + 1) * CH], v_ev[0]),
                  reads=[r_ev[0], r_x[o]], writes=[r_x[o]], tag="oadd")

        ffn("2")

        for j in range(KT):
            E("sync", lambda e, j=j: e.dma_start(
                out=yout[j * 128:(j + 1) * 128, :], in_=x_sb[:, j, :]),
              reads=[r_x[j]], writes=[r_out], queue="dma")
        E("sync", lambda e: e.nop(), reads=[r_out])

        p.build(ctx, block)
    return nc


_NC_CACHE = {}


def get_nc(cfg_key):
    if cfg_key not in _NC_CACHE:
        _NC_CACHE[cfg_key] = build(make_cfg(*cfg_key))
    return _NC_CACHE[cfg_key]


def host_shard(inputs, cfg):
    S, DM, DIN, NC = cfg["S"], cfg["DM"], cfg["DIN"], cfg["NC"]
    T, DC, KT, DT = cfg["T"], cfg["DC"], cfg["KT"], cfg["DT"]
    NB, N, K = cfg["NB"], cfg["N"], cfg["K"]
    OSL = cfg["OSL"]
    f32 = np.float32

    def A(x):
        return np.ascontiguousarray(np.asarray(x), dtype=f32)

    xT = A(inputs["L0_attn_a_hidden_states"][0]).T
    WqT = np.ascontiguousarray(A(inputs["L0_attn_a_Wq"]).T)
    WoT = np.ascontiguousarray(A(inputs["L0_attn_a_Wo"]).T)
    mem = A(inputs["L0_attn_a_memories"])
    mnl = np.ascontiguousarray(
        A(inputs["L0_attn_a_memory_norms"]).transpose(2, 1, 0))
    bankw = A(inputs["L0_attn_a_bank_weights"])
    pwT = np.ascontiguousarray(A(inputs["L0_ffn_pweight"]).transpose(1, 2, 0))
    nwT = np.ascontiguousarray(A(inputs["L0_ffn_nweight"]).transpose(1, 2, 0))
    pbr = np.ascontiguousarray(
        A(inputs["L0_ffn_pbias"]).reshape(KT, 128, NB).transpose(1, 0, 2))
    nbr = np.ascontiguousarray(
        A(inputs["L0_ffn_nbias"]).reshape(KT, 128, NB).transpose(1, 0, 2))
    fexps = A(inputs["L0_ffn_exps"]).reshape(1, NB)
    fbexps = A(inputs["L0_ffn_bexps"]).reshape(1, NB)
    fmask = A(inputs["L0_ffn_mask_weight"]).reshape(1, NB)
    fscale = A(inputs["L0_ffn_scale"]).reshape(1, 1)
    fbsc = A(inputs["L0_ffn_biasscale"]).reshape(1, 1)
    ipT = A(inputs["L0_attn_b_in_proj_w"]).T
    convw = A(inputs["L0_attn_b_conv_w"])
    convb = A(inputs["L0_attn_b_conv_b"])
    xpTf = A(inputs["L0_attn_b_x_proj_w"]).T
    dtln = A(inputs["L0_attn_b_dt_ln_w"]).reshape(128, 1)
    blnv = A(inputs["L0_attn_b_B_ln_w"]).reshape(N, 1)
    clnv = A(inputs["L0_attn_b_C_ln_w"]).reshape(N, 1)
    dpTf = A(inputs["L0_attn_b_dt_proj_w"]).T
    dtb = A(inputs["L0_attn_b_dt_bias"])
    alog = A(inputs["L0_attn_b_A_log"])
    dskip = A(inputs["L0_attn_b_D_skip"])
    opTf = A(inputs["L0_attn_b_out_proj_w"]).T
    ident = np.eye(128).astype(BF)
    sel = np.zeros((NB, NB * 128), dtype=BF)
    for n in range(NB):
        sel[n, n * 128:(n + 1) * 128] = 1

    maps = []
    for r in range(NC):
        dsl = slice(r * DC, (r + 1) * DC)

        def dshape(v):
            return np.ascontiguousarray(v[dsl].reshape(DT, 128).T)

        def dshape2(v):
            return np.ascontiguousarray(
                v[dsl].reshape(DT, 128, -1).transpose(1, 0, 2))

        m = {
            "xsl": np.ascontiguousarray(xT[:, r * T:(r + 1) * T]),
            "wqT": WqT, "woT": WoT, "memi": mem, "mnl": mnl, "bankw": bankw,
            "pwT": np.ascontiguousarray(pwT[:, :, r * OSL:(r + 1) * OSL]),
            "nwT": np.ascontiguousarray(nwT[:, :, r * OSL:(r + 1) * OSL]),
            "pbr": pbr, "nbr": nbr, "fexps": fexps, "fbexps": fbexps,
            "fmask": fmask, "fscale": fscale, "fbsc": fbsc,
            "ipT": np.ascontiguousarray(
                np.concatenate([ipT[:, dsl], ipT[:, DIN + r * DC:DIN + (r + 1) * DC]],
                               axis=1)),
            "cwi": dshape2(convw), "cbi": dshape(convb),
            "xpT": dshape2(xpTf),
            "dtln": dtln, "blnv": blnv, "clnv": clnv,
            "dpT": np.ascontiguousarray(dpTf[:, dsl]),
            "dtbv": dshape(dtb),
            "alog": dshape2(alog),
            "dskip": dshape(dskip),
            "opT": np.ascontiguousarray(opTf[dsl, :]),
            "identi": ident,
            "seli": sel,
        }
        maps.append(m)
    return maps


def kernel(**inputs):
    cfg = make_cfg()
    nc = get_nc((cfg["S"], cfg["DM"], cfg["DIN"], cfg["NC"]))
    in_maps = host_shard(inputs, cfg)
    res = run_bass_kernel_spmd(nc, in_maps, core_ids=list(range(cfg["NC"])))
    parts = [np.asarray(res.results[r]["yout"]) for r in range(cfg["NC"])]
    full = np.concatenate(parts, axis=1)
    return np.ascontiguousarray(full.T)[None].astype(np.float32)
